# version 04
# baseline (speedup 1.0000x reference)
import math
import numpy as np
import concourse.bass as bass
import concourse.mybir as mybir
from concourse.bass_utils import run_bass_kernel_spmd
from contextlib import ExitStack

F32 = mybir.dt.float32
BF16 = mybir.dt.bfloat16
I32 = mybir.dt.int32
AF = mybir.ActivationFunctionType
ALU = mybir.AluOpType
AX = mybir.AxisListType

S = 8192
D = 2048
NH = 16
HD = 128
NT = 1024
KT = 4096
FH = 8192
NQ = 4
HQ = FH // NQ

SCALE = 1.0 / math.sqrt(HD)
EXP_BIAS = -12.0
LN_EPS = 1e-5
RMS_EPS = float(np.finfo(np.float32).eps)
MAGIC = 12582912.0
C1 = 6.28125
C2 = 0.0019353071795864769
HALF_PI = math.pi / 2.0

DMA_SEMS = ("qs", "qg", "pl", "st")
DEBUG_TAPS = ()
PHASE_LIMIT = 99
PARANOID = False
CDBG = 0


class Ctx:
    def __init__(self, nc):
        self.nc = nc
        self.sems = {}
        self.cnt = {}
        self.marks = {}
        self.last_wait = {}

    def add_sem(self, stack, name):
        self.sems[name] = stack.enter_context(self.nc.semaphore(name))
        self.cnt[name] = 0

    def inc(self, name, instr, n=1):
        step = 16 if name in DMA_SEMS else 1
        self.cnt[name] += n * step
        instr.then_inc(self.sems[name], n * step)
        return self.cnt[name]

    def mark(self, name, key):
        self.marks[(name, key)] = self.cnt[name]

    def mval(self, name, key):
        return self.marks[(name, key)]

    def wait(self, eng, name, key=None, val=None):
        if val is None:
            val = self.marks[(name, key)] if key is not None else self.cnt[name]
        if val <= 0:
            return
        k = (id(eng), name)
        if self.last_wait.get(k, -1) >= val:
            return
        self.last_wait[k] = val
        eng.wait_ge(self.sems[name], val)

    def gwait(self, gp, name, key=None, val=None):
        if val is None:
            val = self.marks[(name, key)] if key is not None else self.cnt[name]
        if val <= 0:
            return
        k = ("gp", name)
        if self.last_wait.get(k, -1) >= val:
            return
        self.last_wait[k] = val
        gp.wait_ge(self.sems[name], val)

    def dve(self, opname, *a, **kw):
        eng = self.nc.vector
        self.wait(eng, "dve", val=self.cnt["dve"])
        return self.inc("dve", getattr(eng, opname)(*a, **kw))

    def act(self, opname, *a, **kw):
        eng = self.nc.scalar
        self.wait(eng, "act", val=self.cnt["act"])
        return self.inc("act", getattr(eng, opname)(*a, **kw))

    def dma(self, eng, sem, out, in_, **kw):
        return self.inc(sem, eng.dma_start(out=out, in_=in_, **kw))


class StopEmission(Exception):
    pass


def build_program():
    nc = bass.Bass("TRN2", target_bir_lowering=False)
    ap = {}

    def din(name, shape, dt=F32):
        ap[name] = nc.dram_tensor(name, shape, dt, kind="ExternalInput").ap()

    din("x_own", [NT, D])
    din("x_kv", [KT, D])
    din("wqT", [D, D])
    din("wkT", [D, D])
    din("wvT", [D, D])
    din("woT", [D, D])
    din("wgu", [2 * FH, D])
    din("wp", [D, FH])
    din("gate", [1, NH])
    din("iotak", [1, 512])
    din("iotaq", [1, 512])
    din("invf", [128, 8])
    din("invf2", [128, 8])
    din("identm", [128, 128])
    din("role_t", [1, 1], I32)
    din("epoch_t", [1, 1], I32)
    ap["y"] = nc.dram_tensor("y", [NT, D], F32, kind="ExternalOutput").ap()

    ap["ksh"] = nc.dram_tensor("ksh", [D, S], BF16, kind="Internal",
                               addr_space="Shared").ap()
    ap["vsh"] = nc.dram_tensor("vsh", [NH, S, HD], BF16, kind="Internal",
                               addr_space="Shared").ap()
    ap["msh"] = nc.dram_tensor("msh", [2 * NH * 128, 129], F32, kind="Internal",
                               addr_space="Shared").ap()
    ap["fsh"] = nc.dram_tensor("fsh", [2, 8], I32, kind="Internal",
                               addr_space="Shared").ap()
    ap["x1d"] = nc.dram_tensor("x1d", [NT, D], F32, kind="Internal").ap()

    taps = {}

    def tout(name, shape, dt):
        taps[name] = nc.dram_tensor(name, shape, dt, kind="ExternalOutput").ap()

    if "ksh" in DEBUG_TAPS:
        tout("tap_ksh", [D, S], BF16)
        tout("tap_vsh", [NH * S, HD], BF16)
    if "q" in DEBUG_TAPS:
        tout("tap_q", [128, NH * NT], BF16)
    if "mem" in DEBUG_TAPS:
        tout("tap_mem", [2 * NH * 128, 129], F32)
    if "comb" in DEBUG_TAPS:
        tout("tap_comb", [128, NH * NT], BF16)
    if "x1" in DEBUG_TAPS:
        tout("tap_x1", [NT, D], F32)
    if "ln2" in DEBUG_TAPS:
        tout("tap_ln2", [128, 16 * NT], BF16)
    if "ffn" in DEBUG_TAPS:
        tout("tap_ffn", [128, 16 * NT], BF16)
    if "ob7" in DEBUG_TAPS:
        tout("tap_ob7", [128, 16 * 512], BF16)
    if "lnf" in DEBUG_TAPS:
        tout("tap_lnf", [128, 16 * 512], BF16)
        tout("tap_tblc", [128, 512], F32)
        tout("tap_tbls", [128, 512], F32)

    with ExitStack() as root:
        ctx = Ctx(nc)
        for s_ in ("qs", "qg", "pl", "st", "pe", "act", "dve"):
            ctx.add_sem(root, s_)
        blk = root.enter_context(nc.Block())

        @blk.gpsimd
        def _(gp):
            emit(nc, ctx, ap, taps, gp, root)

    return nc, taps


def emit(nc, ctx, ap, taps, gp, root):
    V, A, T, SY = nc.vector, nc.scalar, nc.tensor, nc.sync
    sb = {}

    def alloc(stack, name, shape, dt, side="left"):
        sb[name] = stack.enter_context(
            nc.sbuf_tensor("s_" + name, shape, dt, side=side))
        return sb[name]

    for nm, shp, dt in (
        ("invf", [128, 8], F32), ("invf2", [128, 8], F32),
        ("ones_col", [128, 1], BF16), ("ones_row", [1, 128], BF16),
        ("ones1f", [1, 128], F32),
        ("gate_raw", [1, NH], F32), ("gate_g", [1, NH], F32),
        ("gate_1mg", [1, NH], F32),
        ("role_sb", [1, 1], I32), ("epoch_sb", [1, 8], I32),
        ("flag_sb", [1, 8], I32),
        ("iotak", [1, 512], F32), ("iotaq", [1, 512], F32),
        ("st_sum", [128, 48], F32), ("st_ssq", [128, 48], F32),
        ("st_rstd", [128, 48], F32), ("st_bias", [128, 48], F32),
        ("sc_a", [128, 48], F32), ("sc_b", [128, 48], F32),
        ("identb", [128, 128], BF16), ("identf", [128, 128], F32),
        ("junk", [128, D], BF16), ("cexpb", [128, 1], F32),
    ):
        alloc(root, nm, shp, dt)

    V.memset(sb["ones_col"][:], 1.0)
    V.memset(sb["ones_row"][:], 1.0)
    V.memset(sb["ones1f"][:], 1.0)
    V.memset(sb["cexpb"][:], EXP_BIAS)
    nc.const_aps.register(F32, EXP_BIAS, sb["cexpb"][:])

    for nm in ("invf", "invf2", "iotak", "iotaq", "identm"):
        dst = "identf" if nm == "identm" else nm
        ctx.dma(SY, "qs", sb[dst][:], ap[nm])
    ctx.dma(SY, "qs", sb["gate_raw"][:], ap["gate"])
    ctx.dma(SY, "qs", sb["role_sb"][:], ap["role_t"])
    ctx.dma(SY, "qs", sb["epoch_sb"][:, 0:1], ap["epoch_t"])
    ctx.mark("qs", "consts")
    ctx.wait(V, "qs", "consts")
    ctx.dve("tensor_copy", sb["identb"][:], sb["identf"][:])
    ctx.mark("dve", "identb")

    rg = gp.register("r_role").__enter__()
    ctx.gwait(gp, "qs", "consts")
    gp.reg_load(rg, sb["role_sb"][0:1, 0:1])

    NKT, NOT_ = KT // 128, NT // 128
    NTILES = NKT + NOT_
    with ExitStack() as sA0:
        xb0 = [alloc(sA0, f"a0x{i}", [128, D], F32) for i in range(2)]
        ctx.wait(A, "qs", "consts")
        ctx.act("activation", sb["gate_g"][:], sb["gate_raw"][:], AF.Sigmoid)
        ctx.mark("act", "gate")
        ctx.wait(V, "act", "gate")
        ctx.dve("tensor_scalar", sb["gate_1mg"][:], sb["gate_g"][:],
                -1.0, 1.0, ALU.mult, ALU.add)
        for t in range(NTILES):
            src = (ap["x_kv"][128 * t:128 * (t + 1), :] if t < NKT else
                   ap["x_own"][128 * (t - NKT):128 * (t - NKT + 1), :])
            if t >= 2:
                ctx.wait(SY, "act", ("a0", t - 2))
                ctx.wait(SY, "dve", ("a0v", t - 2))
            ctx.dma(SY, "qs", xb0[t % 2][:], src)
            ctx.mark("qs", ("a0x", t))
            ctx.wait(V, "qs", ("a0x", t))
            ctx.dve("reduce_sum", sb["st_sum"][:, t:t + 1], xb0[t % 2][:], AX.X)
            ctx.mark("dve", ("a0v", t))
            ctx.wait(A, "qs", ("a0x", t))
            ctx.act("activation", sb["junk"][:], xb0[t % 2][:], AF.Square,
                    accum_out=sb["st_ssq"][:, t:t + 1])
            ctx.mark("act", ("a0", t))
        n = NTILES
        ctx.dve("tensor_scalar", sb["sc_a"][:, 0:n], sb["st_sum"][:, 0:n],
                1.0 / D, None, ALU.mult)
        ctx.wait(V, "act", ("a0", NTILES - 1))
        ctx.dve("tensor_scalar", sb["sc_b"][:, 0:n], sb["st_ssq"][:, 0:n],
                1.0 / D, None, ALU.mult)
        ctx.dve("tensor_tensor", sb["st_rstd"][:, 0:n], sb["sc_a"][:, 0:n],
                sb["sc_a"][:, 0:n], ALU.mult)
        ctx.dve("tensor_tensor", sb["st_bias"][:, 0:n], sb["sc_b"][:, 0:n],
                sb["st_rstd"][:, 0:n], ALU.subtract)
        ctx.dve("tensor_scalar", sb["sc_b"][:, 0:n], sb["st_bias"][:, 0:n],
                LN_EPS, None, ALU.add)
        ctx.dve("reciprocal", sb["st_rstd"][:, 0:n], sb["sc_b"][:, 0:n])
        ctx.mark("dve", "a0rv")
        ctx.wait(A, "dve", "a0rv")
        ctx.act("activation", sb["st_bias"][:, 0:n], sb["st_rstd"][:, 0:n],
                AF.Sqrt)
        ctx.mark("act", "a0rstd")
        ctx.wait(V, "act", "a0rstd")
        ctx.dve("tensor_copy", sb["st_rstd"][:, 0:n], sb["st_bias"][:, 0:n])
        ctx.dve("scalar_tensor_tensor", sb["st_bias"][:, 0:n],
                sb["sc_a"][:, 0:n], -1.0, sb["st_rstd"][:, 0:n],
                ALU.mult, ALU.mult)
        ctx.mark("dve", "stats")

    sAB = ExitStack()
    q_dm = alloc(sAB, "q_dm", [128, NH * NT], BF16)

    with ExitStack() as sA:
        ps_1 = sA.enter_context(nc.psum_tensor([128, 1024], F32))
        ps_2 = sA.enter_context(nc.psum_tensor([128, 1024], F32))
        ps_t = sA.enter_context(nc.psum_tensor([128, 1024], BF16))
        ps_3 = sA.enter_context(nc.psum_tensor([128, 512], F32))

        ln1f = alloc(sA, "ln1f", [128, 16 * 512], BF16)
        obuf = alloc(sA, "obuf", [128, 16 * 512], BF16)
        ln1f_3d = ln1f[:].rearrange("p (fc t) -> p fc t", t=512)
        pst_3d = ps_t[:].rearrange("p (k e) -> p k e", e=128)
        ksh3 = ap["ksh"].rearrange("(dc p) t -> p dc t", p=128)
        prev_pass = {"k": None, "v": "k", "q": "v"}

        sA1 = ExitStack()
        wslot = alloc(sA1, "wslot", [128, 16 * D], BF16)
        xb = [alloc(sA1, f"xab{i}", [128, D], F32) for i in range(2)]
        lt = [alloc(sA1, f"lt{i}", [128, D], BF16) for i in range(2)]
        tsb = alloc(sA1, "tsb", [128, 512], F32)
        tscr = [alloc(sA1, f"tscr{i}", [128, 512], F32) for i in range(4)]
        sarg = alloc(sA1, "sarg", [128, 512], F32)
        carg = alloc(sA1, "carg", [128, 512], F32)
        tblc = alloc(sA1, "tblc", [128, 512], BF16)
        tbls = alloc(sA1, "tbls", [128, 512], BF16)
        kei = alloc(sA1, "kei", [128, 512], BF16)
        kei8 = alloc(sA1, "kei8", [128, 512], BF16)
        rscrb = [alloc(sA1, f"rscrb{i}", [128, 512], BF16) for i in range(4)]

        ctx.mark("dve", "leprev")
        ctx.mark("qg", "kspill_prev")
        ctx.mark("act", "vspill_guard")

        def load_weight(wname, key):
            ctx.wait(V, "pe", val=ctx.cnt["pe"])
            ctx.wait(SY, "act", val=ctx.cnt["act"])
            ctx.wait(SY, "dve", val=ctx.cnt["dve"])
            for i in range(16):
                if i >= 2:
                    ctx.wait(SY, "dve", ("wc", key, i - 2))
                ctx.dma(SY, "qs", xb[i % 2][:],
                        ap[wname][128 * i:128 * (i + 1), :])
                ctx.mark("qs", ("wl", key, i))
                ctx.wait(V, "qs", ("wl", key, i))
                ctx.dve("tensor_copy", wslot[:, D * i:D * (i + 1)],
                        xb[i % 2][:])
                ctx.mark("dve", ("wc", key, i))
            ctx.mark("dve", ("wdone", key))

        def ln_block(key, blk, tile0, xsrc, prev_mm_mark=None,
                     prev_lt_mark=None):
            ctx.wait(SY, "dve", ("wdone", key))
            for tt in range(4):
                t = tile0 + 4 * blk + tt
                gi = 4 * blk + tt
                r0 = 512 * blk + 128 * tt
                if gi >= 2:
                    ctx.wait(SY, "act", ("lnc", key, gi - 2))
                ctx.dma(SY, "qs", xb[gi % 2][:], xsrc[r0:r0 + 128, :])
                ctx.mark("qs", ("xa", key, gi))
                ctx.wait(A, "qs", ("xa", key, gi))
                ctx.wait(A, "dve", "stats")
                if gi >= 2:
                    ctx.wait(A, "pe", ("lt", key, gi - 2, 1))
                elif prev_lt_mark is not None:
                    ctx.wait(A, "pe", val=prev_lt_mark)
                ctx.act("activation", lt[gi % 2][:], xb[gi % 2][:], AF.Identity,
                        bias=sb["st_bias"][:, t:t + 1],
                        scale=sb["st_rstd"][:, t:t + 1])
                ctx.mark("act", ("lnc", key, gi))
                for rnd in range(2):
                    ctx.wait(T, "act", ("lnc", key, gi))
                    ctx.wait(T, "dve", "leprev")
                    last = None
                    for k in range(8):
                        fc = 8 * rnd + k
                        last = T.transpose(ps_t[:, 128 * k:128 * (k + 1)],
                                           lt[gi % 2][:, 128 * fc:128 * (fc + 1)],
                                           sb["identb"][:])
                    ctx.inc("pe", last)
                    ctx.mark("pe", ("lt", key, gi, rnd))
                    ctx.wait(V, "pe", ("lt", key, gi, rnd))
                    if tt == 0 and rnd == 0 and prev_mm_mark is not None:
                        ctx.wait(V, "pe", val=prev_mm_mark)
                    ctx.dve("tensor_copy",
                            ln1f_3d[:, 8 * rnd:8 * rnd + 8,
                                    128 * tt:128 * (tt + 1)],
                            pst_3d[:, :, :])
                    ctx.mark("dve", "leprev")
            ctx.mark("dve", ("lnf", key, blk))

        def gen_block_bcast(iota_name):
            if ("act", "tsb") in ctx.marks:
                ctx.wait(T, "act", "tsb")
            mm = T.matmul(ps_3[:, 0:512], sb["ones1f"][:],
                          sb[iota_name][0:1, :], start=True, stop=True)
            ctx.inc("pe", mm)
            ctx.mark("pe", "tbc")
            ctx.wait(A, "pe", "tbc")
            ctx.act("copy", tsb[:], ps_3[:, 0:512])
            ctx.mark("act", "tsb")

        def gen_tables(pr, tlo):
            ctx.wait(V, "act", "tsb")
            th, u, kk, y1 = tscr
            ctx.dve("tensor_scalar", th[:], tsb[:], float(tlo),
                    sb["invf"][:, pr:pr + 1], ALU.add, ALU.mult)
            ctx.dve("tensor_scalar", u[:], tsb[:], float(tlo),
                    sb["invf2"][:, pr:pr + 1], ALU.add, ALU.mult)
            ctx.dve("tensor_scalar", kk[:], u[:], MAGIC, MAGIC,
                    ALU.add, ALU.subtract)
            ctx.dve("scalar_tensor_tensor", y1[:], kk[:], -C1, th[:],
                    ALU.mult, ALU.add)
            ctx.dve("scalar_tensor_tensor", sarg[:], kk[:], -C2, y1[:],
                    ALU.mult, ALU.add)
            ctx.dve("tensor_scalar", kk[:], u[:], 0.25, None, ALU.add)
            ctx.dve("tensor_scalar", u[:], kk[:], MAGIC, MAGIC,
                    ALU.add, ALU.subtract)
            ctx.dve("scalar_tensor_tensor", y1[:], u[:], -C1, th[:],
                    ALU.mult, ALU.add)
            ctx.dve("scalar_tensor_tensor", kk[:], u[:], -C2, y1[:],
                    ALU.mult, ALU.add)
            ctx.dve("tensor_scalar", carg[:], kk[:], HALF_PI, None, ALU.add)
            ctx.mark("dve", "tblarg")
            ctx.wait(A, "dve", "tblarg")
            ctx.act("activation", tbls[:], sarg[:], AF.Sin)
            ctx.act("activation", tblc[:], carg[:], AF.Sin)
            ctx.mark("act", "tbl")

        def kq_pass(key, wname, xsrc, nblk, tile0, iota_name, store_q):
            load_weight(wname, key)
            for blk in range(nblk):
                if not store_q:
                    ctx.wait(V, "qg", "kspill_prev")
                pm = ctx.mval("pe", ("mm", key, blk - 1, 7)) if blk >= 1 else \
                    None
                pl_ = None
                if blk == 0 and ("pe", ("lt", prev_pass[key], 31, 1)) in \
                        ctx.marks:
                    pl_ = ctx.mval("pe", ("lt", prev_pass[key], 31, 1))
                ln_block(key, blk, tile0, xsrc, pm, pl_)
                ctx.wait(T, "dve", ("lnf", key, blk))
                if key == "k" and blk == 7 and "tap_lnf" in taps:
                    ctx.wait(SY, "dve", ("lnf", key, blk))
                    ctx.dma(SY, "st", taps["tap_lnf"], ln1f[:])
                gen_block_bcast(iota_name)
                for pr in range(8):
                    if key == "q" and blk == 0 and pr < 2:
                        ctx.wait(T, "act", ("vev", 7, 15))
                    gen_tables(pr, 512 * blk)
                    slot = (pr % 2) * 512
                    if pr >= 2:
                        ctx.wait(T, "act", ("kev", key, blk, pr - 2))
                    elif blk >= 1:
                        ctx.wait(T, "act", ("kev", key, blk - 1, pr + 6))
                    for half, ps in ((0, ps_1), (1, ps_2)):
                        dc = pr + 8 * half
                        last = None
                        for fc in range(16):
                            last = T.matmul(
                                ps[:, slot:slot + 512],
                                wslot[:, D * fc + 128 * dc:
                                      D * fc + 128 * (dc + 1)],
                                ln1f[:, 512 * fc:512 * (fc + 1)],
                                start=(fc == 0), stop=(fc == 15))
                        ctx.inc("pe", last)
                    ctx.mark("pe", ("mm", key, blk, pr))
                    ctx.wait(A, "pe", ("mm", key, blk, pr))
                    i, i8 = pr, pr + 8
                    if store_q:
                        d0 = q_dm[:, NT * i + 512 * blk:
                                  NT * i + 512 * blk + 512]
                        d1 = q_dm[:, NT * i8 + 512 * blk:
                                  NT * i8 + 512 * blk + 512]
                    else:
                        d0 = obuf[:, 512 * i:512 * (i + 1)]
                        d1 = obuf[:, 512 * i8:512 * (i8 + 1)]
                    ctx.act("copy", kei[:], ps_1[:, slot:slot + 512])
                    ctx.act("copy", kei8[:], ps_2[:, slot:slot + 512])
                    ctx.mark("act", ("kev", key, blk, pr))
                    ctx.wait(V, "act", ("kev", key, blk, pr))
                    r0, r1, r2, r3 = rscrb
                    ctx.dve("tensor_tensor", r0[:], kei[:], tblc[:], ALU.mult)
                    ctx.dve("tensor_tensor", r1[:], kei8[:], tbls[:], ALU.mult)
                    ctx.dve("tensor_tensor", d0, r0[:], r1[:], ALU.subtract)
                    ctx.dve("tensor_tensor", r2[:], kei8[:], tblc[:], ALU.mult)
                    ctx.dve("tensor_tensor", r3[:], kei[:], tbls[:], ALU.mult)
                    ctx.dve("tensor_tensor", d1, r2[:], r3[:], ALU.add)
                    ctx.mark("dve", ("rope", key, blk, pr))
                    if key == "k" and blk == 7 and pr == 0 and "tap_lnf" in taps:
                        ctx.wait(SY, "dve", "tbl")
                        ctx.dma(SY, "st", taps["tap_tblc"], tblc[:])
                        ctx.dma(SY, "st", taps["tap_tbls"], tbls[:])
                if not store_q:
                    ctx.mark("dve", ("kblk", blk))
                    ctx.gwait(gp, "dve", ("kblk", blk))
                    if blk == 7 and "tap_ob7" in taps:
                        gp.dma_start(out=taps["tap_ob7"],
                                     in_=obuf[:]).then_inc(ctx.sems["st"], 16)
                        ctx.cnt["st"] += 16
                    src = obuf[:].rearrange("p (dc t) -> p dc t", t=512)
                    with gp.If_eq(rg, 0):
                        gp.dma_start(
                            out=ksh3[:, :, 512 * blk:512 * (blk + 1)],
                            in_=src).then_inc(ctx.sems["qg"], 16)
                    with gp.Else():
                        gp.dma_start(
                            out=ksh3[:, :, KT + 512 * blk:KT + 512 * (blk + 1)],
                            in_=src).then_inc(ctx.sems["qg"], 16)
                    ctx.cnt["qg"] += 16
                    ctx.mark("qg", "kspill_prev")

        if PHASE_LIMIT >= 2:
            kq_pass("k", "wkT", ap["x_kv"], 8, 0, "iotak", False)

        if "tap_ksh" in taps and PHASE_LIMIT < 6:
            ctx.gwait(gp, "qg", "kspill_prev")
            gp.dma_start(out=taps["tap_ksh"], in_=ap["ksh"]).then_inc(
                ctx.sems["st"], 16)
            ctx.cnt["st"] += 16
            if PHASE_LIMIT >= 3:
                pass

        if PHASE_LIMIT >= 3:
          load_weight("wvT", "v")
          slots = [(ps_1, 0), (ps_1, 512), (ps_2, 0), (ps_2, 512)]
          for blk in range(8):
              ctx.wait(A, "qg", "kspill_prev")
              ctx.wait(A, "act", "vspill_guard")
              ln_block("v", blk, 0, ap["x_kv"])
              ctx.wait(T, "dve", ("lnf", "v", blk))
              it = 0
              for ssub in range(4):
                  for eb in range(4):
                      ps, off = slots[it % 4]
                      if it >= 4:
                          ctx.wait(T, "act", ("vev", blk, it - 4))
                      elif blk >= 1:
                          ctx.wait(T, "act", ("vev", blk - 1, 12 + it))
                      else:
                          ctx.wait(T, "dve", ("rope", "k", 7, 7))
                      last = None
                      for fc in range(16):
                          last = T.matmul(
                              ps[:, off:off + 512],
                              ln1f[:, 512 * fc + 128 * ssub:
                                   512 * fc + 128 * (ssub + 1)],
                              wslot[:, D * fc + 512 * eb:D * fc + 512 * (eb + 1)],
                              start=(fc == 0), stop=(fc == 15))
                      ctx.inc("pe", last)
                      ctx.mark("pe", ("vmm", blk, it))
                      ctx.wait(A, "pe", ("vmm", blk, it))
                      ctx.act("copy",
                              obuf[:, 2048 * ssub + 512 * eb:
                                   2048 * ssub + 512 * (eb + 1)],
                              ps[:, off:off + 512])
                      ctx.mark("act", ("vev", blk, it))
                      it += 1
              ctx.mark("act", ("vblk", blk))
              ctx.gwait(gp, "act", ("vblk", blk))
              trow0, trow1 = 512 * blk, KT + 512 * blk
              with gp.If_eq(rg, 0):
                  for cc in range(4):
                      src3 = obuf[:, 2048 * cc:2048 * (cc + 1)].rearrange(
                          "p (h e) -> p h e", e=HD)
                      r0 = trow0 + 128 * cc
                      gp.dma_start(
                          out=ap["vsh"][:, r0:r0 + 128, :].rearrange(
                              "h p e -> p h e"),
                          in_=src3).then_inc(ctx.sems["qg"], 16)
              with gp.Else():
                  for cc in range(4):
                      src3 = obuf[:, 2048 * cc:2048 * (cc + 1)].rearrange(
                          "p (h e) -> p h e", e=HD)
                      r1 = trow1 + 128 * cc
                      gp.dma_start(
                          out=ap["vsh"][:, r1:r1 + 128, :].rearrange(
                              "h p e -> p h e"),
                          in_=src3).then_inc(ctx.sems["qg"], 16)
              ctx.cnt["qg"] += 64
              ctx.mark("qg", ("vspill", blk))
              ctx.mark("qg", "kspill_prev")
          ctx.mark("qg", "vspill_done")

        if PHASE_LIMIT >= 4:
            kq_pass("q", "wqT", ap["x_own"], 2, NKT, "iotaq", True)
            ctx.mark("dve", "qdone")
            if "tap_q" in taps:
                ctx.wait(SY, "dve", "qdone")
                ctx.dma(SY, "st", taps["tap_q"], q_dm[:])

        sA1.close()
        if PHASE_LIMIT >= 5:
          v129 = alloc(sA, "v129", [128, 32 * 129 + 64], BF16)
          kfb = alloc(sA, "kfb", [128, 512], BF16)
          kfs = [alloc(sA, f"kfs{i}", [128, 512], F32) for i in range(2)]
          mstg = alloc(sA, "mstg", [128, 129], F32)
          v129_3 = v129[:, 0:4128].rearrange("p (c e) -> p c e", e=129) \
              if False else None

          krh = obuf
          ctx.mark("qg", "mspill_prev")
          for h in range(NH):
              ctx.gwait(gp, "dve", "qdone")
              ctx.gwait(gp, "qg", "vspill_done")
              if h >= 1:
                  ctx.gwait(gp, "pe", ("a2mm", h - 1))
              with gp.If_eq(rg, 0):
                  gp.dma_start(out=krh[:, 0:KT],
                               in_=ap["ksh"][128 * h:128 * (h + 1), 0:KT]
                               ).then_inc(ctx.sems["qg"], 16)
                  gp.dma_start(
                      out=v129[:, 0:4128].rearrange("p (c e) -> p c e", e=129)[:, 0:32, 0:128],
                      in_=ap["vsh"][h, 0:KT, :].rearrange("(c p) e -> p c e",
                                                          p=128)
                      ).then_inc(ctx.sems["qg"], 16)
              with gp.Else():
                  gp.dma_start(out=krh[:, 0:KT],
                               in_=ap["ksh"][128 * h:128 * (h + 1), KT:S]
                               ).then_inc(ctx.sems["qg"], 16)
                  gp.dma_start(
                      out=v129[:, 0:4128].rearrange("p (c e) -> p c e", e=129)[:, 0:32, 0:128],
                      in_=ap["vsh"][h, KT:S, :].rearrange("(c p) e -> p c e",
                                                          p=128)
                      ).then_inc(ctx.sems["qg"], 16)
              ctx.cnt["qg"] += 32
              ctx.mark("qg", ("a2kv", h))
              if h == 0:
                  ctx.wait(V, "qg", ("a2kv", h))
                  ctx.dve("memset",
                          v129[:, 0:4128].rearrange(
                              "p (c e) -> p c e", e=129)[:, 0:32, 128:129], 1.0)
                  ctx.mark("dve", "v129ones")
              for grp in range(8):
                  ctx.wait(T, "qg", ("a2kv", h))
                  ctx.wait(T, "dve", "leprev")
                  last = None
                  for g in range(4):
                      j = 4 * grp + g
                      last = T.transpose(ps_t[:, 128 * g:128 * (g + 1)],
                                         krh[:, 128 * j:128 * (j + 1)],
                                         sb["identb"][:])
                  ctx.inc("pe", last)
                  ctx.mark("pe", ("a2t", h, grp))
                  ctx.wait(V, "pe", ("a2t", h, grp))
                  ctx.dve("tensor_scalar", kfs[0][:], ps_t[:, 0:512], 0.0, None,
                          ALU.min)
                  ctx.mark("dve", ("a2m", h, grp))
                  ctx.wait(A, "dve", ("a2m", h, grp))
                  ctx.act("activation", kfs[1][:], kfs[0][:], AF.Exp)
                  ctx.mark("act", ("a2e", h, grp))
                  ctx.wait(V, "act", ("a2e", h, grp))
                  ctx.dve("scalar_tensor_tensor", kfb[:], ps_t[:, 0:512], 0.0,
                          kfs[1][:], ALU.max, ALU.add)
                  ctx.mark("dve", "leprev")
                  ctx.wait(T, "dve", "leprev")
                  ctx.wait(T, "dve", "v129ones")
                  last = None
                  for g in range(4):
                      j = 4 * grp + g
                      last = T.matmul(ps_3[:, 0:129],
                                      kfb[:, 128 * g:128 * (g + 1)],
                                      v129[:, 129 * j:129 * j + 129],
                                      start=(grp == 0 and g == 0),
                                      stop=(grp == 7 and g == 3))
                  ctx.inc("pe", last)
              ctx.mark("pe", ("a2mm", h))
              ctx.wait(A, "pe", ("a2mm", h))
              ctx.wait(A, "qg", "mspill_prev")
              ctx.act("copy", mstg[:], ps_3[:, 0:129])
              ctx.mark("act", ("a2ev", h))
              ctx.gwait(gp, "act", ("a2ev", h))
              with gp.If_eq(rg, 0):
                  gp.dma_start(out=ap["msh"][128 * h:128 * (h + 1), :],
                               in_=mstg[:]).then_inc(ctx.sems["qg"], 16)
              with gp.Else():
                  gp.dma_start(
                      out=ap["msh"][NH * 128 + 128 * h:NH * 128 + 128 * (h + 1), :],
                      in_=mstg[:]).then_inc(ctx.sems["qg"], 16)
              ctx.cnt["qg"] += 16
              ctx.mark("qg", "mspill_prev")

        if PHASE_LIMIT >= 6:
          with (gp.register("r_epoch") as r_epoch,
                gp.register("r_f") as r_f,
                gp.register("r_cont") as r_cont,
                gp.register("r_tgt") as r_tgt):
              gp.reg_load(r_epoch, sb["epoch_sb"][0:1, 0:1])
              with gp.If_eq(rg, 0):
                  gp.dma_start(out=ap["fsh"][0:1, 0:1],
                               in_=sb["epoch_sb"][0:1, 0:1]
                               ).then_inc(ctx.sems["qg"], 16)
              with gp.Else():
                  gp.dma_start(out=ap["fsh"][1:2, 0:1],
                               in_=sb["epoch_sb"][0:1, 0:1]
                               ).then_inc(ctx.sems["qg"], 16)
              ctx.cnt["qg"] += 16
              gp.wait_ge(ctx.sems["qg"], ctx.cnt["qg"])
              gp.reg_mov(r_cont, 1)
              gp.reg_mov(r_tgt, 16)
              with gp.While(r_cont):
                  with gp.If_eq(rg, 0):
                      gp.dma_start(out=sb["flag_sb"][0:1, 0:1],
                                   in_=ap["fsh"][1:2, 0:1]
                                   ).then_inc(ctx.sems["pl"], 16)
                  with gp.Else():
                      gp.dma_start(out=sb["flag_sb"][0:1, 0:1],
                                   in_=ap["fsh"][0:1, 0:1]
                                   ).then_inc(ctx.sems["pl"], 16)
                  gp.wait_ge(ctx.sems["pl"], r_tgt)
                  gp.reg_add(r_tgt, r_tgt, 16)
                  gp.reg_load(r_f, sb["flag_sb"][0:1, 0:1])
                  gp.reg_alu(r_cont, r_f, r_epoch, ALU.not_equal)

        if "tap_ksh" in taps:
            gp.dma_start(out=taps["tap_ksh"], in_=ap["ksh"]).then_inc(
                ctx.sems["st"], 16)
            gp.dma_start(out=taps["tap_vsh"],
                         in_=ap["vsh"].rearrange("h s e -> (h s) e")
                         ).then_inc(ctx.sems["st"], 16)
            ctx.cnt["st"] += 32
        if "tap_mem" in taps:
            gp.dma_start(out=taps["tap_mem"], in_=ap["msh"]).then_inc(
                ctx.sems["st"], 16)
            ctx.cnt["st"] += 16

    sBC = ExitStack()
    if PHASE_LIMIT >= 7:
      comb = alloc(sBC, "comb", [128, NH * NT], BF16, side="right")
      with ExitStack() as sB:
          ps_s0 = sB.enter_context(nc.psum_tensor([128, 1024], F32))
          ps_s1 = sB.enter_context(nc.psum_tensor([128, 1024], F32))
          ps_pv = sB.enter_context(nc.psum_tensor([128, 1024], F32))
          ps_den = sB.enter_context(nc.psum_tensor([1, 1024], F32))
          ps_s = [ps_s0, ps_s1]

          kh = [alloc(sB, f"kh{i}", [128, S], BF16) for i in range(2)]
          vt = [alloc(sB, f"vt{i}", [128, S], BF16) for i in range(2)]
          pT = [alloc(sB, f"pT{i}", [128, 1024], BF16) for i in range(2)]
          qfb = alloc(sB, "qfb", [128, 1024], BF16)
          qfs = [alloc(sB, f"qfs{i}", [128, 1024], F32) for i in range(2)]
          mem_sb = alloc(sB, "mem_sb", [128, 1024], F32)
          mp = alloc(sB, "mp", [128, 258], F32)
          mn = alloc(sB, "mn", [128, 129], BF16)
          Bsb = alloc(sB, "Bsb", [128, 1024], BF16)
          Blb = alloc(sB, "Blb", [128, 1024], BF16)
          u1 = alloc(sB, "u1", [128, 1024], F32)
          u2 = alloc(sB, "u2", [128, 1024], F32)
          rlr = alloc(sB, "rlr", [1, 1024], F32)
          rsr = alloc(sB, "rsr", [1, 1024], F32)
          rsb = alloc(sB, "rsb", [1, 1024], BF16)
          rlb = alloc(sB, "rlb", [1, 1024], BF16)

          for h in range(NH):
              if h >= 2:
                  ctx.gwait(gp, "pe", ("S", h - 2, 63))
                  ctx.gwait(gp, "pe", ("PV", h - 2, 63))
              if h >= 1:
                  ctx.gwait(gp, "dve", ("mn", h - 1))
              gp.dma_start(out=kh[h % 2][:],
                           in_=ap["ksh"][128 * h:128 * (h + 1), :]
                           ).then_inc(ctx.sems["qg"], 16)
              gp.dma_start(out=vt[h % 2][:].rearrange("p (c e) -> p c e", e=HD),
                           in_=ap["vsh"][h, :, :].rearrange("(c p) e -> p c e",
                                                            p=128)
                           ).then_inc(ctx.sems["qg"], 16)
              gp.dma_start(out=mp[:, 0:129],
                           in_=ap["msh"][128 * h:128 * (h + 1), :]
                           ).then_inc(ctx.sems["qg"], 16)
              gp.dma_start(out=mp[:, 129:258],
                           in_=ap["msh"][NH * 128 + 128 * h:
                                         NH * 128 + 128 * (h + 1), :]
                           ).then_inc(ctx.sems["qg"], 16)
              ctx.cnt["qg"] += 64
              ctx.mark("qg", ("bkv", h))

              ctx.wait(V, "qg", ("bkv", h))
              ctx.dve("tensor_tensor", mn[:], mp[:, 0:129], mp[:, 129:258],
                      ALU.add)
              ctx.mark("dve", ("mn", h))
              if h >= 1:
                  ctx.wait(V, "pe", ("bdl", h - 1))
              ctx.dve("tensor_scalar", qfs[0][:], q_dm[:, NT * h:NT * (h + 1)],
                      0.0, None, ALU.min)
              ctx.mark("dve", ("bqf0", h))
              ctx.wait(A, "dve", ("bqf0", h))
              ctx.act("activation", qfs[1][:], qfs[0][:], AF.Exp)
              ctx.mark("act", ("bqf1", h))
              ctx.wait(V, "act", ("bqf1", h))
              ctx.dve("scalar_tensor_tensor", qfb[:],
                      q_dm[:, NT * h:NT * (h + 1)], 0.0, qfs[1][:],
                      ALU.max, ALU.add)
              ctx.mark("dve", ("bqf", h))

              if h >= 1:
                  ctx.wait(T, "dve", ("comb", h - 1))
                  ctx.wait(T, "dve", ("rs", h - 1))
              ctx.wait(T, "dve", ("bqf", h))
              T.matmul(ps_pv[:, 0:512], mn[:, 0:128], qfb[:, 0:512],
                       start=True, stop=True)
              last = T.matmul(ps_pv[:, 512:1024], mn[:, 0:128], qfb[:, 512:1024],
                              start=True, stop=True)
              ctx.inc("pe", last)
              ctx.mark("pe", ("bmo", h))
              T.matmul(ps_den[0:1, 0:512], mn[:, 128:129], qfb[:, 0:512],
                       start=True, stop=True)
              last = T.matmul(ps_den[0:1, 512:1024], mn[:, 128:129],
                              qfb[:, 512:1024], start=True, stop=True)
              ctx.inc("pe", last)
              ctx.mark("pe", ("bdl", h))
              ctx.wait(A, "pe", ("bmo", h))
              if h >= 1:
                  ctx.wait(A, "dve", ("comb", h - 1))
              ctx.act("copy", mem_sb[:], ps_pv[:])
              ctx.mark("act", ("bmev", h))
              ctx.wait(V, "pe", ("bdl", h))
              ctx.dve("reciprocal", rlr[:], ps_den[0:1, :])
              ctx.mark("dve", ("brl", h))

              def emit_pv(j):
                  ctx.wait(T, "act", ("exp", h, j))
                  if j == 0:
                      ctx.wait(T, "act", ("bmev", h))
                      ctx.wait(T, "dve", ("brl", h))
                  T.matmul(ps_pv[:, 0:512], vt[h % 2][:, 128 * j:128 * (j + 1)],
                           pT[j % 2][:, 0:512], start=(j == 0), stop=(j == 63))
                  T.matmul(ps_pv[:, 512:1024],
                           vt[h % 2][:, 128 * j:128 * (j + 1)],
                           pT[j % 2][:, 512:1024], start=(j == 0), stop=(j == 63))
                  T.matmul(ps_den[0:1, 0:512], sb["ones_col"][:],
                           pT[j % 2][:, 0:512], start=(j == 0), stop=(j == 63))
                  last = T.matmul(ps_den[0:1, 512:1024], sb["ones_col"][:],
                                  pT[j % 2][:, 512:1024], start=(j == 0),
                                  stop=(j == 63))
                  ctx.inc("pe", last)
                  ctx.mark("pe", ("PV", h, j))

              for j in range(64):
                  if j >= 2:
                      ctx.wait(T, "act", ("exp", h, j - 2))
                  elif h >= 1:
                      ctx.wait(T, "act", ("bce", h - 1))
                  T.matmul(ps_s[j % 2][:, 0:512],
                           kh[h % 2][:, 128 * j:128 * (j + 1)],
                           q_dm[:, NT * h:NT * h + 512], start=True, stop=True)
                  last = T.matmul(ps_s[j % 2][:, 512:1024],
                                  kh[h % 2][:, 128 * j:128 * (j + 1)],
                                  q_dm[:, NT * h + 512:NT * h + 1024],
                                  start=True, stop=True)
                  ctx.inc("pe", last)
                  ctx.mark("pe", ("S", h, j))
                  ctx.wait(A, "pe", ("S", h, j))
                  ctx.act("activation", pT[j % 2][:], ps_s[j % 2][:], AF.Exp,
                          scale=SCALE, bias=EXP_BIAS)
                  ctx.mark("act", ("exp", h, j))
                  if j >= 1:
                      emit_pv(j - 1)
              emit_pv(63)

              ctx.wait(V, "pe", ("PV", h, 63))
              ctx.dve("reciprocal", rsr[:], ps_den[0:1, :])
              ctx.mark("dve", ("rs", h))
              ctx.dve("tensor_scalar", rsb[:], rsr[:],
                      sb["gate_1mg"][0:1, h:h + 1], None, ALU.mult)
              ctx.dve("tensor_scalar", rlb[:], rlr[:],
                      sb["gate_g"][0:1, h:h + 1], None, ALU.mult)
              ctx.mark("dve", ("brows", h))
              ctx.wait(T, "act", ("exp", h, 63))
              ctx.wait(T, "dve", ("brows", h))
              T.matmul(ps_s0[:, 0:512], sb["ones_row"][:], rsb[0:1, 0:512],
                       start=True, stop=True)
              T.matmul(ps_s0[:, 512:1024], sb["ones_row"][:], rsb[0:1, 512:1024],
                       start=True, stop=True)
              T.matmul(ps_s1[:, 0:512], sb["ones_row"][:], rlb[0:1, 0:512],
                       start=True, stop=True)
              last = T.matmul(ps_s1[:, 512:1024], sb["ones_row"][:],
                              rlb[0:1, 512:1024], start=True, stop=True)
              ctx.inc("pe", last)
              ctx.mark("pe", ("bc", h))
              ctx.wait(A, "pe", ("bc", h))
              ctx.act("copy", Bsb[:], ps_s0[:])
              ctx.act("copy", Blb[:], ps_s1[:])
              ctx.mark("act", ("bce", h))
              ctx.wait(V, "act", ("bce", h))
              ctx.dve("tensor_tensor", u1[:], ps_pv[:], Bsb[:], ALU.mult)
              ctx.dve("tensor_tensor", u2[:], mem_sb[:], Blb[:], ALU.mult)
              ctx.dve("tensor_tensor", comb[:, NT * h:NT * (h + 1)], u1[:],
                      u2[:], ALU.add)
              ctx.mark("dve", ("comb", h))

          ctx.mark("dve", "combdone")
          if "tap_comb" in taps:
              ctx.wait(SY, "dve", "combdone")
              ctx.dma(SY, "st", taps["tap_comb"], comb[:])

    sAB.close()

    ln2f = alloc(root, "ln2f", [128, 16 * NT], BF16)
    if PHASE_LIMIT >= 8:
      with ExitStack() as sC:
          cpa = sC.enter_context(nc.psum_tensor([128, 1024], F32))
          cpb = sC.enter_context(nc.psum_tensor([128, 1024], F32))
          ps_t = sC.enter_context(nc.psum_tensor([128, 1024], BF16))

          wo_bf = alloc(sC, "wo_bf", [128, 16 * D], BF16)
          xot = [alloc(sC, f"xot{i}", [128, D], F32) for i in range(2)]
          tC = alloc(sC, "tC", [128, D], F32)
          x1t = alloc(sC, "x1t", [128, D], F32)
          l2t = alloc(sC, "l2t", [128, D], BF16)
          st4 = alloc(sC, "st4", [128, 16], F32)
          sq4 = alloc(sC, "sq4", [128, 16], F32)

          ctx.wait(SY, "dve", "combdone")
          for i in range(16):
              if i >= 2:
                  ctx.wait(SY, "dve", ("cwo", i - 2))
              ctx.dma(SY, "qs", xot[i % 2][:], ap["woT"][128 * i:128 * (i + 1), :])
              ctx.mark("qs", ("cwl", i))
              ctx.wait(V, "qs", ("cwl", i))
              ctx.dve("tensor_copy", wo_bf[:, D * i:D * (i + 1)], xot[i % 2][:])
              ctx.mark("dve", ("cwo", i))
          ctx.mark("dve", "cwodone")

          psl = [cpa[:, 0:512], cpa[:, 512:1024], cpb[:, 0:512], cpb[:, 512:1024]]
          ln2f_3d = ln2f[:].rearrange("p (fc t) -> p fc t", t=NT)
          pst_3d = ps_t[:].rearrange("p (k e) -> p k e", e=128)
          ctx.mark("dve", "l2ev_prev")

          for j in range(8):
              ctx.wait(SY, "dve", "cwodone")
              if j >= 2:
                  ctx.wait(SY, "dve", ("cx1", j - 2))
              ctx.dma(SY, "qs", xot[j % 2][:],
                      ap["x_own"][128 * j:128 * (j + 1), :])
              ctx.mark("qs", ("cx", j))
              ctx.wait(T, "dve", "cwodone")
              if j >= 1:
                  ctx.wait(T, "act", ("cwoev", j - 1))
              last = None
              for ic in range(16):
                  for db in range(4):
                      last = T.matmul(
                          psl[db],
                          comb[:, NT * ic + 128 * j:NT * ic + 128 * (j + 1)],
                          wo_bf[:, D * ic + 512 * db:D * ic + 512 * (db + 1)],
                          start=(ic == 0), stop=(ic == 15))
              ctx.inc("pe", last)
              ctx.mark("pe", ("cwo", j))
              ctx.wait(V, "pe", ("cwo", j))
              ctx.wait(A, "pe", ("cwo", j))
              if CDBG == 1:
                  ctx.dve("memset", st4[:, 0:4], 0.0)
                  ctx.dve("memset", sq4[:, 0:4], float(D))
              elif CDBG == 2:
                  for db in range(4):
                      ctx.dve("reduce_sum", st4[:, db:db + 1], psl[db], AX.X)
                  ctx.dve("memset", sq4[:, 0:4], float(D))
              elif CDBG == 3:
                  ctx.dve("memset", st4[:, 0:4], 0.0)
                  for db in range(4):
                      ctx.act("activation", sb["junk"][:, 0:512], psl[db],
                              AF.Square, accum_out=sq4[:, db:db + 1])
              else:
                  for db in range(4):
                      ctx.dve("reduce_sum", st4[:, db:db + 1], psl[db], AX.X)
                  ctx.mark("dve", ("cred", j))
                  ctx.wait(A, "dve", ("cred", j))
                  for db in range(4):
                      ctx.act("activation", sb["junk"][:, 0:512], psl[db],
                              AF.Square, accum_out=sq4[:, db:db + 1])
              ctx.mark("act", ("csq", j))
              ctx.dve("reduce_sum", st4[:, 8:9], st4[:, 0:4], AX.X)
              ctx.wait(V, "act", ("csq", j))
              ctx.dve("reduce_sum", sq4[:, 8:9], sq4[:, 0:4], AX.X)
              ctx.dve("tensor_scalar", st4[:, 9:10], st4[:, 8:9], 1.0 / D, None,
                      ALU.mult)
              ctx.dve("tensor_scalar", sq4[:, 9:10], sq4[:, 8:9], 1.0 / D, None,
                      ALU.mult)
              ctx.dve("tensor_tensor", st4[:, 10:11], st4[:, 9:10], st4[:, 9:10],
                      ALU.mult)
              ctx.dve("tensor_tensor", sq4[:, 10:11], sq4[:, 9:10],
                      st4[:, 10:11], ALU.subtract)
              ctx.dve("tensor_scalar", sq4[:, 11:12], sq4[:, 10:11], LN_EPS,
                      None, ALU.add)
              ctx.dve("reciprocal", sq4[:, 12:13], sq4[:, 11:12])
              ctx.mark("dve", ("crv", j))
              ctx.wait(A, "dve", ("crv", j))
              ctx.act("activation", st4[:, 11:12], sq4[:, 12:13], AF.Sqrt)
              ctx.mark("act", ("crstd", j))
              ctx.wait(V, "act", ("crstd", j))
              ctx.dve("scalar_tensor_tensor", st4[:, 12:13], st4[:, 9:10], -1.0,
                      st4[:, 11:12], ALU.mult, ALU.mult)
              ctx.mark("dve", ("cbias", j))
              ctx.wait(A, "dve", ("cbias", j))
              for db in range(4):
                  ctx.act("activation", tC[:, 512 * db:512 * (db + 1)], psl[db],
                          AF.Identity, scale=st4[:, 11:12], bias=st4[:, 12:13])
              ctx.mark("act", ("cwoev", j))
              ctx.wait(V, "act", ("cwoev", j))
              ctx.wait(V, "qs", ("cx", j))
              if j >= 1:
                  ctx.wait(V, "qs", ("cx1s", j - 1))
              ctx.dve("tensor_tensor", x1t[:], tC[:], xot[j % 2][:], ALU.add)
              ctx.mark("dve", ("cx1", j))
              ctx.wait(SY, "dve", ("cx1", j))
              ctx.dma(SY, "qs", ap["x1d"][128 * j:128 * (j + 1), :], x1t[:])
              if "tap_x1" in taps:
                  ctx.dma(SY, "st", taps["tap_x1"][128 * j:128 * (j + 1), :],
                          x1t[:])
              ctx.mark("qs", ("cx1s", j))
              ctx.dve("reduce_sum", st4[:, 13:14], x1t[:], AX.X)
              ctx.wait(A, "dve", ("cx1", j))
              ctx.act("activation", sb["junk"][:], x1t[:], AF.Square,
                      accum_out=sq4[:, 13:14])
              ctx.mark("act", ("l2sq", j))
              ctx.dve("tensor_scalar", st4[:, 9:10], st4[:, 13:14], 1.0 / D,
                      None, ALU.mult)
              ctx.wait(V, "act", ("l2sq", j))
              ctx.dve("tensor_scalar", sq4[:, 9:10], sq4[:, 13:14], 1.0 / D,
                      None, ALU.mult)
              ctx.dve("tensor_tensor", st4[:, 10:11], st4[:, 9:10], st4[:, 9:10],
                      ALU.mult)
              ctx.dve("tensor_tensor", sq4[:, 10:11], sq4[:, 9:10],
                      st4[:, 10:11], ALU.subtract)
              ctx.dve("tensor_scalar", sq4[:, 11:12], sq4[:, 10:11], LN_EPS,
                      None, ALU.add)
              ctx.dve("reciprocal", sq4[:, 12:13], sq4[:, 11:12])
              ctx.mark("dve", ("l2rv", j))
              ctx.wait(A, "dve", ("l2rv", j))
              ctx.act("activation", st4[:, 11:12], sq4[:, 12:13], AF.Sqrt)
              ctx.mark("act", ("l2rstd", j))
              ctx.wait(V, "act", ("l2rstd", j))
              ctx.dve("scalar_tensor_tensor", st4[:, 12:13], st4[:, 9:10], -1.0,
                      st4[:, 11:12], ALU.mult, ALU.mult)
              ctx.mark("dve", ("l2bias", j))
              ctx.wait(A, "dve", ("l2bias", j))
              ctx.act("activation", l2t[:], x1t[:], AF.Identity,
                      scale=st4[:, 11:12], bias=st4[:, 12:13])
              ctx.mark("act", ("l2t", j))
              for rnd in range(2):
                  ctx.wait(T, "act", ("l2t", j))
                  ctx.wait(T, "dve", "l2ev_prev")
                  last = None
                  for k in range(8):
                      fc = 8 * rnd + k
                      last = T.transpose(ps_t[:, 128 * k:128 * (k + 1)],
                                         l2t[:, 128 * fc:128 * (fc + 1)],
                                         sb["identb"][:])
                  ctx.inc("pe", last)
                  ctx.mark("pe", ("l2T", j, rnd))
                  ctx.wait(V, "pe", ("l2T", j, rnd))
                  ctx.dve("tensor_copy",
                          ln2f_3d[:, 8 * rnd:8 * rnd + 8, 128 * j:128 * (j + 1)],
                          pst_3d[:, :, :])
                  ctx.mark("dve", "l2ev_prev")
          ctx.mark("dve", "ln2done")
          if "tap_ln2" in taps:
              ctx.wait(SY, "dve", "ln2done")
              ctx.dma(SY, "st", taps["tap_ln2"], ln2f[:])

    sBC.close()

    if PHASE_LIMIT >= 9:
      with ExitStack() as sD:
          dga = sD.enter_context(nc.psum_tensor([128, 1024], F32))
          dgb = sD.enter_context(nc.psum_tensor([128, 1024], F32))
          ps_t = sD.enter_context(nc.psum_tensor([128, 1024], BF16))
          dse = sD.enter_context(nc.psum_tensor([128, 512], F32))

          act_q = alloc(sD, "act_q", [128, 16 * NT], BF16)
          out_acc = alloc(sD, "out_acc", [128, 16 * NT], BF16)
          wgst = [alloc(sD, f"wgst{i}", [128, D], F32) for i in range(2)]
          wqh = [alloc(sD, f"wqh{i}", [128, D], BF16) for i in range(2)]
          qwb = alloc(sD, "qwb", [128, D], BF16)
          qtb = alloc(sD, "qtb", [128, D], BF16)
          qpb = alloc(sD, "qpb", [128, D], BF16)
          qsc = alloc(sD, "qsc", [128, 32], F32)
          qscb = alloc(sD, "qscb", [128, 32], BF16)
          qscn = alloc(sD, "qscn", [128, 32], BF16)
          qrec = alloc(sD, "qrec", [128, 32], F32)
          qrf = alloc(sD, "qrf", [128, 32], F32)
          wfm = [alloc(sD, f"wfm{i}", [128, 16 * 128], BF16) for i in range(2)]
          wpTb = alloc(sD, "wpTb", [128, 16 * 128], BF16)
          sgt = alloc(sD, "sgt", [128, 1024], F32)
          sqb = alloc(sD, "sqb", [128, 1024], BF16)
          dtmp = alloc(sD, "dtmp", [128, 1024], BF16)
          invb = alloc(sD, "invb", [128, 16], F32)
          ftm = alloc(sD, "ftm", [128, D], BF16)
          ftf = alloc(sD, "ftf", [128, D], F32)
          fln = alloc(sD, "fln", [128, D], F32)
          x1t2 = alloc(sD, "x1t2", [128, D], F32)
          outt = alloc(sD, "outt", [128, D], F32)

          ctx.mark("dve", "dq_prev")
          ctx.mark("dve", "t16_prev")
          ctx.mark("pe", "t16pe_prev")

          def quant(key, ci):
              nG = 32
              ctx.wait(V, "qs", ("dwl", key))
              w3i = qwb[:, :].rearrange("p (g k) -> p g k", k=64)
              ctx.dve("tensor_copy", qwb[:], wgst[ci % 2][:])
              ctx.mark("dve", ("qcast", ci))
              ctx.dve("tensor_reduce", qrf[:, 0:nG], w3i, AX.X, ALU.add,
                      apply_absolute_value=True)
              ctx.dve("tensor_scalar", qsc[:, 0:nG], qrf[:, 0:nG], 1.0 / 64,
                      1e-8, ALU.mult, ALU.max)
              ctx.dve("tensor_copy", qscb[:, 0:nG], qsc[:, 0:nG])
              ctx.dve("tensor_scalar", qscn[:, 0:nG], qscb[:, 0:nG], -1.0, None,
                      ALU.mult)
              ctx.dve("tensor_copy", qsc[:, 0:nG], qscb[:, 0:nG])
              ctx.dve("reciprocal", qrec[:, 0:nG], qsc[:, 0:nG])
              ctx.dve("tensor_tensor",
                      qtb[:, :].rearrange("p (g k) -> p g k", k=64), w3i,
                      qrec[:, 0:nG].unsqueeze(2).broadcast_to([128, nG, 64]),
                      ALU.mult)
              ctx.dve("tensor_scalar", qpb[:], qtb[:], 0.5, None, ALU.is_gt)
              ctx.dve("scalar_tensor_tensor", qwb[:], qtb[:], -0.5, qpb[:],
                      ALU.is_lt, ALU.subtract)
              if ci >= 2:
                  ctx.wait(V, "pe", ("t16L", ci - 2))
              ctx.dve("tensor_tensor",
                      wqh[ci % 2][:, :].rearrange("p (g k) -> p g k", k=64),
                      qwb[:, :].rearrange("p (g k) -> p g k", k=64),
                      qscn[:, 0:nG].unsqueeze(2).broadcast_to([128, nG, 64]),
                      ALU.mult)
              ctx.mark("dve", ("quant", key))
              ctx.mark("dve", "dq_prev")

          def transpose16(key, ci, dst, prev_reader_mark):
              d3 = dst[:].rearrange("p (k e) -> p k e", e=128)
              p3 = ps_t[:].rearrange("p (k e) -> p k e", e=128)
              for rnd in range(2):
                  ctx.wait(T, "dve", ("quant", key))
                  ctx.wait(T, "dve", "t16_prev")
                  last = None
                  for k in range(8):
                      fc = 8 * rnd + k
                      last = T.transpose(ps_t[:, 128 * k:128 * (k + 1)],
                                         wqh[ci % 2][:, 128 * fc:128 * (fc + 1)],
                                         sb["identb"][:])
                  ctx.inc("pe", last)
                  ctx.mark("pe", ("t16", key, rnd))
                  ctx.mark("pe", "t16pe_prev")
                  if rnd == 1:
                      ctx.mark("pe", ("t16L", ci))
                  ctx.wait(V, "pe", ("t16", key, rnd))
                  if prev_reader_mark is not None:
                      ctx.wait(V, "pe", val=prev_reader_mark)
                  ctx.dve("tensor_copy", d3[:, 8 * rnd:8 * rnd + 8, :],
                          p3[:, :, :])
                  ctx.mark("dve", "t16_prev")
              ctx.mark("dve", ("t16d", key))

          psl_d = {0: dga, 1: dgb}
          for q in range(NQ):
              for hp in range(16):
                  for wi, row0 in ((0, HQ * q + 128 * hp),
                                   (1, FH + HQ * q + 128 * hp)):
                      key = ("gu", q, hp, wi)
                      ci = q * 48 + hp * 2 + wi
                      if ci >= 2:
                          ctx.wait(SY, "dve", ("qcast", ci - 2))
                      ctx.dma(SY, "qs", wgst[ci % 2][:],
                              ap["wgu"][row0:row0 + 128, :])
                      ctx.mark("qs", ("dwl", key))
                      quant(key, ci)
                      prev = None
                      if (q, hp) != (0, 0):
                          prevkey = ("dgu", q, hp - 1, wi) if hp >= 1 else \
                              ("dgu", q - 1, 15, wi)
                          prev = ctx.mval("pe", prevkey)
                      transpose16(key, ci, wfm[wi], prev)
                      ps = psl_d[wi]
                      ctx.wait(T, "dve", ("t16d", key))
                      ctx.wait(T, "dve", "ln2done")
                      if (q, hp) != (0, 0):
                          ctx.wait(T, "dve", ("dact", q, hp - 1) if hp >= 1
                                   else ("dact", q - 1, 15))
                      last = None
                      for fc in range(16):
                          last = T.matmul(ps[:, 0:512],
                                          wfm[wi][:, 128 * fc:128 * (fc + 1)],
                                          ln2f[:, NT * fc:NT * fc + 512],
                                          start=(fc == 0), stop=(fc == 15))
                          last = T.matmul(ps[:, 512:1024],
                                          wfm[wi][:, 128 * fc:128 * (fc + 1)],
                                          ln2f[:, NT * fc + 512:NT * fc + 1024],
                                          start=(fc == 0), stop=(fc == 15))
                      ctx.inc("pe", last)
                      ctx.mark("pe", ("dgu", q, hp, wi))
                  ctx.wait(A, "pe", ("dgu", q, hp, 0))
                  ctx.act("activation", sgt[:], dga[:], AF.Silu)
                  ctx.mark("act", ("dsilu", q, hp))
                  ctx.wait(V, "act", ("dsilu", q, hp))
                  ctx.wait(V, "pe", ("dgu", q, hp, 1))
                  ctx.dve("tensor_tensor", act_q[:, NT * hp:NT * (hp + 1)],
                          dgb[:], sgt[:], ALU.mult)
                  ctx.mark("dve", ("dact", q, hp))
                  ctx.wait(A, "dve", ("dact", q, hp))
                  ctx.act("activation", sqb[:], act_q[:, NT * hp:NT * (hp + 1)],
                          AF.Square)
                  ctx.mark("act", ("dsq", q, hp))
                  ctx.wait(T, "act", ("dsq", q, hp))
                  last = None
                  for jj in range(8):
                      last = T.matmul(dse[:, jj:jj + 1],
                                      sqb[:, 128 * jj:128 * (jj + 1)],
                                      sb["ones_col"][:],
                                      start=(q == 0 and hp == 0 and jj == 0),
                                      stop=(q == NQ - 1 and hp == 15))
                  ctx.inc("pe", last)
                  ctx.mark("pe", ("dssq", q, hp))

              for dc in range(16):
                  key = ("pj", q, dc)
                  ci = q * 48 + 32 + dc
                  ctx.wait(SY, "dve", ("qcast", ci - 2))
                  ctx.dma(SY, "qs", wgst[ci % 2][:],
                          ap["wp"][128 * dc:128 * (dc + 1), HQ * q:HQ * (q + 1)])
                  ctx.mark("qs", ("dwl", key))
                  quant(key, ci)
                  prev = ctx.mval("pe", ("dpj", q, dc - 1)) if dc >= 1 else \
                      (ctx.mval("pe", ("dpj", q - 1, 15)) if q >= 1 else None)
                  transpose16(key, ci, wpTb, prev)
                  ps = psl_d[dc % 2]
                  ctx.wait(T, "dve", ("t16d", key))
                  if dc >= 2:
                      ctx.wait(T, "dve", ("dpe", q, dc - 2))
                  elif q >= 1:
                      ctx.wait(T, "dve", ("dpe", q - 1, 14 + dc))
                  last = None
                  for hc in range(16):
                      last = T.matmul(ps[:, 0:512],
                                      wpTb[:, 128 * hc:128 * (hc + 1)],
                                      act_q[:, NT * hc:NT * hc + 512],
                                      start=(hc == 0), stop=(hc == 15))
                      last = T.matmul(ps[:, 512:1024],
                                      wpTb[:, 128 * hc:128 * (hc + 1)],
                                      act_q[:, NT * hc + 512:NT * hc + 1024],
                                      start=(hc == 0), stop=(hc == 15))
                  ctx.inc("pe", last)
                  ctx.mark("pe", ("dpj", q, dc))
                  ctx.wait(V, "pe", ("dpj", q, dc))
                  if q == 0:
                      ctx.dve("tensor_copy", out_acc[:, NT * dc:NT * (dc + 1)],
                              ps[:])
                  else:
                      ctx.dve("tensor_tensor", dtmp[:], ps[:],
                              out_acc[:, NT * dc:NT * (dc + 1)], ALU.add)
                      ctx.dve("tensor_copy", out_acc[:, NT * dc:NT * (dc + 1)],
                              dtmp[:])
                  ctx.mark("dve", ("dpe", q, dc))

          ctx.wait(V, "pe", ("dssq", NQ - 1, 15))
          ctx.dve("tensor_scalar", invb[:, 0:8], dse[:, 0:8], 1.0 / FH, RMS_EPS,
                  ALU.mult, ALU.add)
          ctx.dve("reciprocal", invb[:, 8:16], invb[:, 0:8])
          ctx.mark("dve", "drv")
          ctx.wait(A, "dve", "drv")
          ctx.act("activation", invb[:, 0:8], invb[:, 8:16], AF.Sqrt)
          ctx.mark("act", "dinv")

          if "tap_ffn" in taps:
              ctx.wait(SY, "dve", ("dpe", NQ - 1, 15))
              ctx.dma(SY, "st", taps["tap_ffn"], out_acc[:])

          pst_3d = ps_t[:].rearrange("p (k e) -> p k e", e=128)
          ftm_3d = ftm[:].rearrange("p (k e) -> p k e", e=128)
          for j in range(8):
              if j >= 1:
                  ctx.wait(SY, "dve", ("eout", j - 1))
              ctx.dma(SY, "qs", x1t2[:], ap["x1d"][128 * j:128 * (j + 1), :])
              ctx.mark("qs", ("ex1", j))
              for rnd in range(2):
                  ctx.wait(T, "dve", ("dpe", NQ - 1, 15))
                  ctx.wait(T, "dve", "t16_prev")
                  last = None
                  for k in range(8):
                      dc = 8 * rnd + k
                      last = T.transpose(
                          ps_t[:, 128 * k:128 * (k + 1)],
                          out_acc[:, NT * dc + 128 * j:NT * dc + 128 * (j + 1)],
                          sb["identb"][:])
                  ctx.inc("pe", last)
                  ctx.mark("pe", ("eT", j, rnd))
                  ctx.wait(V, "pe", ("eT", j, rnd))
                  if j >= 1 and rnd == 0:
                      ctx.wait(V, "act", ("eln", j - 1))
                  ctx.dve("tensor_copy", ftm_3d[:, 8 * rnd:8 * rnd + 8, :],
                          pst_3d[:, :, :])
                  ctx.mark("dve", "t16_prev")
              ctx.wait(V, "act", "dinv")
              ctx.dve("tensor_scalar", ftf[:], ftm[:], invb[:, j:j + 1], None,
                      ALU.mult)
              ctx.mark("dve", ("eff", j))
              ctx.dve("reduce_sum", sb["sc_a"][:, 0:1], ftf[:], AX.X)
              ctx.wait(A, "dve", ("eff", j))
              ctx.act("activation", sb["junk"][:], ftf[:], AF.Square,
                      accum_out=sb["sc_a"][:, 1:2])
              ctx.mark("act", ("esq", j))
              ctx.dve("tensor_scalar", sb["sc_a"][:, 2:3], sb["sc_a"][:, 0:1],
                      1.0 / D, None, ALU.mult)
              ctx.wait(V, "act", ("esq", j))
              ctx.dve("tensor_scalar", sb["sc_a"][:, 3:4], sb["sc_a"][:, 1:2],
                      1.0 / D, None, ALU.mult)
              ctx.dve("tensor_tensor", sb["sc_a"][:, 4:5], sb["sc_a"][:, 2:3],
                      sb["sc_a"][:, 2:3], ALU.mult)
              ctx.dve("tensor_tensor", sb["sc_a"][:, 5:6], sb["sc_a"][:, 3:4],
                      sb["sc_a"][:, 4:5], ALU.subtract)
              ctx.dve("tensor_scalar", sb["sc_a"][:, 6:7], sb["sc_a"][:, 5:6],
                      LN_EPS, None, ALU.add)
              ctx.dve("reciprocal", sb["sc_a"][:, 7:8], sb["sc_a"][:, 6:7])
              ctx.mark("dve", ("erv", j))
              ctx.wait(A, "dve", ("erv", j))
              ctx.act("activation", sb["sc_a"][:, 8:9], sb["sc_a"][:, 7:8],
                      AF.Sqrt)
              ctx.mark("act", ("erstd", j))
              ctx.wait(V, "act", ("erstd", j))
              ctx.dve("scalar_tensor_tensor", sb["sc_a"][:, 9:10],
                      sb["sc_a"][:, 2:3], -1.0, sb["sc_a"][:, 8:9],
                      ALU.mult, ALU.mult)
              ctx.mark("dve", ("ebias", j))
              ctx.wait(A, "dve", ("ebias", j))
              ctx.act("activation", fln[:], ftf[:], AF.Identity,
                      scale=sb["sc_a"][:, 8:9], bias=sb["sc_a"][:, 9:10])
              ctx.mark("act", ("eln", j))
              ctx.wait(V, "act", ("eln", j))
              ctx.wait(V, "qs", ("ex1", j))
              ctx.dve("tensor_tensor", outt[:], fln[:], x1t2[:], ALU.add)
              ctx.mark("dve", ("eout", j))
              ctx.wait(SY, "dve", ("eout", j))
              ctx.dma(SY, "st", ap["y"][128 * j:128 * (j + 1), :], outt[:])
              ctx.mark("st", ("y", j))

    if ctx.cnt["st"] > 0:
        ctx.wait(SY, "st", val=ctx.cnt["st"])
        gp.wait_ge(ctx.sems["st"], ctx.cnt["st"])
    else:
        gp.nop()


def kernel(L0_attn_hidden_states, L0_attn_Wq, L0_attn_Wk, L0_attn_Wv,
           L0_attn_Wo, L0_attn_gate, L0_ffn_w_gate_up, L0_ffn_w_proj):
    X = np.ascontiguousarray(np.asarray(L0_attn_hidden_states,
                                        np.float32).reshape(S, D))
    wqT = np.ascontiguousarray(np.asarray(L0_attn_Wq, np.float32).T)
    wkT = np.ascontiguousarray(np.asarray(L0_attn_Wk, np.float32).T)
    wvT = np.ascontiguousarray(np.asarray(L0_attn_Wv, np.float32).T)
    woT = np.ascontiguousarray(np.asarray(L0_attn_Wo, np.float32).T)
    wgu = np.ascontiguousarray(np.asarray(L0_ffn_w_gate_up, np.float32))
    wp = np.ascontiguousarray(np.asarray(L0_ffn_w_proj, np.float32))
    gate = np.ascontiguousarray(
        np.asarray(L0_attn_gate, np.float32).reshape(1, NH))

    inv_freq = (1.0 / 10000.0 ** (np.arange(0, D, 2, dtype=np.float64) / D)
                ).astype(np.float32)
    invf = np.ascontiguousarray(inv_freq.reshape(8, 128).T)
    invf2 = np.ascontiguousarray(
        (invf.astype(np.float64) / (2 * np.pi)).astype(np.float32))
    ident = np.eye(128, dtype=np.float32)
    epoch = np.array([[np.random.randint(1, 2 ** 31 - 1)]], np.int32)
    ar512 = np.arange(512, dtype=np.float32)

    nc, taps = build_program()
    in_maps = []
    for c in range(8):
        role = c & 1
        in_maps.append({
            "x_own": np.ascontiguousarray(X[NT * c:NT * (c + 1)]),
            "x_kv": np.ascontiguousarray(X[KT * role:KT * (role + 1)]),
            "wqT": wqT, "wkT": wkT, "wvT": wvT, "woT": woT,
            "wgu": wgu, "wp": wp, "gate": gate,
            "iotak": (KT * role + ar512)[None, :],
            "iotaq": (NT * c + ar512)[None, :],
            "invf": invf, "invf2": invf2, "identm": ident,
            "role_t": np.array([[role]], np.int32),
            "epoch_t": epoch,
        })
    res = run_bass_kernel_spmd(nc, in_maps, core_ids=list(range(8)))
    out = np.concatenate([res.results[c]["y"] for c in range(8)], axis=0)
    return out.reshape(1, S, D)


# version 07
# speedup vs baseline: 1.0072x; 1.0072x over previous
import math
import numpy as np
import concourse.bass as bass
import concourse.mybir as mybir
from concourse.bass_utils import run_bass_kernel_spmd
from contextlib import ExitStack

F32 = mybir.dt.float32
BF16 = mybir.dt.bfloat16
I32 = mybir.dt.int32
AF = mybir.ActivationFunctionType
ALU = mybir.AluOpType
AX = mybir.AxisListType

S = 8192
D = 2048
NH = 16
HD = 128
NT = 1024
KT = 4096
FH = 8192
NQ = 4
HQ = FH // NQ

SCALE = 1.0 / math.sqrt(HD)
EXP_BIAS = -12.0
LN_EPS = 1e-5
RMS_EPS = float(np.finfo(np.float32).eps)
MAGIC = 12582912.0
C1 = 6.28125
C2 = 0.0019353071795864769
HALF_PI = math.pi / 2.0

DMA_SEMS = ("qs", "qg", "pl", "st")
DEBUG_TAPS = ()
PHASE_LIMIT = 99
PARANOID = False
CDBG = 0


class Ctx:
    def __init__(self, nc):
        self.nc = nc
        self.sems = {}
        self.cnt = {}
        self.marks = {}
        self.last_wait = {}

    def add_sem(self, stack, name):
        self.sems[name] = stack.enter_context(self.nc.semaphore(name))
        self.cnt[name] = 0

    def inc(self, name, instr, n=1):
        step = 16 if name in DMA_SEMS else 1
        self.cnt[name] += n * step
        instr.then_inc(self.sems[name], n * step)
        return self.cnt[name]

    def mark(self, name, key):
        self.marks[(name, key)] = self.cnt[name]

    def mval(self, name, key):
        return self.marks[(name, key)]

    def wait(self, eng, name, key=None, val=None):
        if val is None:
            val = self.marks[(name, key)] if key is not None else self.cnt[name]
        if val <= 0:
            return
        k = (id(eng), name)
        if self.last_wait.get(k, -1) >= val:
            return
        self.last_wait[k] = val
        eng.wait_ge(self.sems[name], val)

    def gwait(self, gp, name, key=None, val=None):
        if val is None:
            val = self.marks[(name, key)] if key is not None else self.cnt[name]
        if val <= 0:
            return
        k = ("gp", name)
        if self.last_wait.get(k, -1) >= val:
            return
        self.last_wait[k] = val
        gp.wait_ge(self.sems[name], val)

    def dve(self, opname, *a, **kw):
        eng = self.nc.vector
        self.wait(eng, "dve", val=self.cnt["dve"])
        return self.inc("dve", getattr(eng, opname)(*a, **kw))

    def act(self, opname, *a, **kw):
        eng = self.nc.scalar
        self.wait(eng, "act", val=self.cnt["act"])
        return self.inc("act", getattr(eng, opname)(*a, **kw))

    def dma(self, eng, sem, out, in_, **kw):
        return self.inc(sem, eng.dma_start(out=out, in_=in_, **kw))


class StopEmission(Exception):
    pass


def build_program():
    nc = bass.Bass("TRN2", target_bir_lowering=False)
    ap = {}

    def din(name, shape, dt=F32):
        ap[name] = nc.dram_tensor(name, shape, dt, kind="ExternalInput").ap()

    din("x_own", [NT, D])
    din("x_kv", [KT, D])
    din("wqT", [D, D])
    din("wkT", [D, D])
    din("wvT", [D, D])
    din("woT", [D, D])
    din("wgu", [2 * FH, D])
    din("wp", [D, FH])
    din("gate", [1, NH])
    din("iotak", [1, 512])
    din("iotaq", [1, 512])
    din("invf", [128, 8])
    din("invf2", [128, 8])
    din("identm", [128, 128])
    din("role_t", [1, 1], I32)
    din("epoch_t", [1, 1], I32)
    ap["y"] = nc.dram_tensor("y", [NT, D], F32, kind="ExternalOutput").ap()

    ap["ksh"] = nc.dram_tensor("ksh", [D, S], BF16, kind="Internal",
                               addr_space="Shared").ap()
    ap["vsh"] = nc.dram_tensor("vsh", [NH, S, HD], BF16, kind="Internal",
                               addr_space="Shared").ap()
    ap["msh"] = nc.dram_tensor("msh", [2 * NH * 128, 129], F32, kind="Internal",
                               addr_space="Shared").ap()
    ap["fsh"] = nc.dram_tensor("fsh", [2, 8], I32, kind="Internal",
                               addr_space="Shared").ap()
    ap["x1d"] = nc.dram_tensor("x1d", [NT, D], F32, kind="Internal").ap()

    taps = {}

    def tout(name, shape, dt):
        taps[name] = nc.dram_tensor(name, shape, dt, kind="ExternalOutput").ap()

    if "ksh" in DEBUG_TAPS:
        tout("tap_ksh", [D, S], BF16)
        tout("tap_vsh", [NH * S, HD], BF16)
    if "q" in DEBUG_TAPS:
        tout("tap_q", [128, NH * NT], BF16)
    if "mem" in DEBUG_TAPS:
        tout("tap_mem", [2 * NH * 128, 129], F32)
    if "comb" in DEBUG_TAPS:
        tout("tap_comb", [128, NH * NT], BF16)
    if "x1" in DEBUG_TAPS:
        tout("tap_x1", [NT, D], F32)
    if "ln2" in DEBUG_TAPS:
        tout("tap_ln2", [128, 16 * NT], BF16)
    if "ffn" in DEBUG_TAPS:
        tout("tap_ffn", [128, 16 * NT], BF16)
    if "ob7" in DEBUG_TAPS:
        tout("tap_ob7", [128, 16 * 512], BF16)
    if "lnf" in DEBUG_TAPS:
        tout("tap_lnf", [128, 16 * 512], BF16)
        tout("tap_tblc", [128, 512], F32)
        tout("tap_tbls", [128, 512], F32)

    with ExitStack() as root:
        ctx = Ctx(nc)
        for s_ in ("qs", "qg", "pl", "st", "pe", "act", "dve"):
            ctx.add_sem(root, s_)
        blk = root.enter_context(nc.Block())

        @blk.gpsimd
        def _(gp):
            emit(nc, ctx, ap, taps, gp, root)

    return nc, taps


def emit(nc, ctx, ap, taps, gp, root):
    V, A, T, SY = nc.vector, nc.scalar, nc.tensor, nc.sync
    sb = {}

    def alloc(stack, name, shape, dt, side="left"):
        sb[name] = stack.enter_context(
            nc.sbuf_tensor("s_" + name, shape, dt, side=side))
        return sb[name]

    for nm, shp, dt in (
        ("invf", [128, 8], F32), ("invf2", [128, 8], F32),
        ("ones_col", [128, 1], BF16), ("ones_row", [1, 128], BF16),
        ("ones1f", [1, 128], F32),
        ("gate_raw", [1, NH], F32), ("gate_g", [1, NH], F32),
        ("gate_1mg", [1, NH], F32),
        ("role_sb", [1, 1], I32), ("epoch_sb", [1, 8], I32),
        ("flag_sb", [1, 8], I32),
        ("iotak", [1, 512], F32), ("iotaq", [1, 512], F32),
        ("st_sum", [128, 48], F32), ("st_ssq", [128, 48], F32),
        ("st_rstd", [128, 48], F32), ("st_bias", [128, 48], F32),
        ("sc_a", [128, 48], F32), ("sc_b", [128, 48], F32),
        ("identb", [128, 128], BF16), ("identf", [128, 128], F32),
        ("junk", [128, D], BF16), ("cexpb", [128, 1], F32),
    ):
        alloc(root, nm, shp, dt)

    V.memset(sb["ones_col"][:], 1.0)
    V.memset(sb["ones_row"][:], 1.0)
    V.memset(sb["ones1f"][:], 1.0)
    V.memset(sb["cexpb"][:], EXP_BIAS)
    nc.const_aps.register(F32, EXP_BIAS, sb["cexpb"][:])

    for nm in ("invf", "invf2", "iotak", "iotaq", "identm"):
        dst = "identf" if nm == "identm" else nm
        ctx.dma(SY, "qs", sb[dst][:], ap[nm])
    ctx.dma(SY, "qs", sb["gate_raw"][:], ap["gate"])
    ctx.dma(SY, "qs", sb["role_sb"][:], ap["role_t"])
    ctx.dma(SY, "qs", sb["epoch_sb"][:, 0:1], ap["epoch_t"])
    ctx.mark("qs", "consts")
    ctx.wait(V, "qs", "consts")
    ctx.dve("tensor_copy", sb["identb"][:], sb["identf"][:])
    ctx.mark("dve", "identb")

    rg = gp.register("r_role").__enter__()
    ctx.gwait(gp, "qs", "consts")
    gp.reg_load(rg, sb["role_sb"][0:1, 0:1])

    NKT, NOT_ = KT // 128, NT // 128
    NTILES = NKT + NOT_
    with ExitStack() as sA0:
        xb0 = [alloc(sA0, f"a0x{i}", [128, D], F32) for i in range(2)]
        ctx.wait(A, "qs", "consts")
        ctx.act("activation", sb["gate_g"][:], sb["gate_raw"][:], AF.Sigmoid)
        ctx.mark("act", "gate")
        ctx.wait(V, "act", "gate")
        ctx.dve("tensor_scalar", sb["gate_1mg"][:], sb["gate_g"][:],
                -1.0, 1.0, ALU.mult, ALU.add)
        for t in range(NTILES):
            src = (ap["x_kv"][128 * t:128 * (t + 1), :] if t < NKT else
                   ap["x_own"][128 * (t - NKT):128 * (t - NKT + 1), :])
            if t >= 2:
                ctx.wait(SY, "act", ("a0", t - 2))
                ctx.wait(SY, "dve", ("a0v", t - 2))
            ctx.dma(SY, "qs", xb0[t % 2][:], src)
            ctx.mark("qs", ("a0x", t))
            ctx.wait(V, "qs", ("a0x", t))
            ctx.dve("reduce_sum", sb["st_sum"][:, t:t + 1], xb0[t % 2][:], AX.X)
            ctx.mark("dve", ("a0v", t))
            ctx.wait(A, "qs", ("a0x", t))
            ctx.act("activation", sb["junk"][:], xb0[t % 2][:], AF.Square,
                    accum_out=sb["st_ssq"][:, t:t + 1])
            ctx.mark("act", ("a0", t))
        n = NTILES
        ctx.dve("tensor_scalar", sb["sc_a"][:, 0:n], sb["st_sum"][:, 0:n],
                1.0 / D, None, ALU.mult)
        ctx.wait(V, "act", ("a0", NTILES - 1))
        ctx.dve("tensor_scalar", sb["sc_b"][:, 0:n], sb["st_ssq"][:, 0:n],
                1.0 / D, None, ALU.mult)
        ctx.dve("tensor_tensor", sb["st_rstd"][:, 0:n], sb["sc_a"][:, 0:n],
                sb["sc_a"][:, 0:n], ALU.mult)
        ctx.dve("tensor_tensor", sb["st_bias"][:, 0:n], sb["sc_b"][:, 0:n],
                sb["st_rstd"][:, 0:n], ALU.subtract)
        ctx.dve("tensor_scalar", sb["sc_b"][:, 0:n], sb["st_bias"][:, 0:n],
                LN_EPS, None, ALU.add)
        ctx.dve("reciprocal", sb["st_rstd"][:, 0:n], sb["sc_b"][:, 0:n])
        ctx.mark("dve", "a0rv")
        ctx.wait(A, "dve", "a0rv")
        ctx.act("activation", sb["st_bias"][:, 0:n], sb["st_rstd"][:, 0:n],
                AF.Sqrt)
        ctx.mark("act", "a0rstd")
        ctx.wait(V, "act", "a0rstd")
        ctx.dve("tensor_copy", sb["st_rstd"][:, 0:n], sb["st_bias"][:, 0:n])
        ctx.dve("scalar_tensor_tensor", sb["st_bias"][:, 0:n],
                sb["sc_a"][:, 0:n], -1.0, sb["st_rstd"][:, 0:n],
                ALU.mult, ALU.mult)
        ctx.mark("dve", "stats")

    sAB = ExitStack()
    q_dm = alloc(sAB, "q_dm", [128, NH * NT], BF16)

    with ExitStack() as sA:
        ps_1 = sA.enter_context(nc.psum_tensor([128, 1024], F32))
        ps_2 = sA.enter_context(nc.psum_tensor([128, 1024], F32))
        ps_t = sA.enter_context(nc.psum_tensor([128, 1024], BF16))
        ps_3 = sA.enter_context(nc.psum_tensor([128, 512], F32))

        ln1f = alloc(sA, "ln1f", [128, 16 * 512], BF16)
        obuf = alloc(sA, "obuf", [128, 16 * 512], BF16)
        ln1f_3d = ln1f[:].rearrange("p (fc t) -> p fc t", t=512)
        pst_3d = ps_t[:].rearrange("p (k e) -> p k e", e=128)
        ksh3 = ap["ksh"].rearrange("(dc p) t -> p dc t", p=128)
        prev_pass = {"k": None, "v": "k", "q": "v"}

        sA1 = ExitStack()
        wslot = alloc(sA1, "wslot", [128, 16 * D], BF16)
        xb = [alloc(sA1, f"xab{i}", [128, D], F32) for i in range(2)]
        lt = [alloc(sA1, f"lt{i}", [128, D], BF16) for i in range(2)]
        tsb = alloc(sA1, "tsb", [128, 512], F32)
        tscr = [alloc(sA1, f"tscr{i}", [128, 512], F32) for i in range(4)]
        sarg = alloc(sA1, "sarg", [128, 512], F32)
        carg = alloc(sA1, "carg", [128, 512], F32)
        tblc = alloc(sA1, "tblc", [128, 512], BF16)
        tbls = alloc(sA1, "tbls", [128, 512], BF16)
        kei = alloc(sA1, "kei", [128, 512], BF16)
        kei8 = alloc(sA1, "kei8", [128, 512], BF16)
        rscrb = [alloc(sA1, f"rscrb{i}", [128, 512], BF16) for i in range(4)]

        ctx.mark("dve", "leprev")
        ctx.mark("qg", "kspill_prev")
        ctx.mark("act", "vspill_guard")

        def load_weight(wname, key):
            ctx.wait(V, "pe", val=ctx.cnt["pe"])
            ctx.wait(SY, "act", val=ctx.cnt["act"])
            ctx.wait(SY, "dve", val=ctx.cnt["dve"])
            for i in range(16):
                if i >= 2:
                    ctx.wait(SY, "dve", ("wc", key, i - 2))
                ctx.dma(SY, "qs", xb[i % 2][:],
                        ap[wname][128 * i:128 * (i + 1), :])
                ctx.mark("qs", ("wl", key, i))
                ctx.wait(V, "qs", ("wl", key, i))
                ctx.dve("tensor_copy", wslot[:, D * i:D * (i + 1)],
                        xb[i % 2][:])
                ctx.mark("dve", ("wc", key, i))
            ctx.mark("dve", ("wdone", key))

        def ln_block(key, blk, tile0, xsrc, prev_mm_mark=None,
                     prev_lt_mark=None):
            ctx.wait(SY, "dve", ("wdone", key))
            for tt in range(4):
                t = tile0 + 4 * blk + tt
                gi = 4 * blk + tt
                r0 = 512 * blk + 128 * tt
                if gi >= 2:
                    ctx.wait(SY, "act", ("lnc", key, gi - 2))
                ctx.dma(SY, "qs", xb[gi % 2][:], xsrc[r0:r0 + 128, :])
                ctx.mark("qs", ("xa", key, gi))
                ctx.wait(A, "qs", ("xa", key, gi))
                ctx.wait(A, "dve", "stats")
                if gi >= 2:
                    ctx.wait(A, "pe", ("lt", key, gi - 2, 1))
                elif prev_lt_mark is not None:
                    ctx.wait(A, "pe", val=prev_lt_mark)
                ctx.act("activation", lt[gi % 2][:], xb[gi % 2][:], AF.Identity,
                        bias=sb["st_bias"][:, t:t + 1],
                        scale=sb["st_rstd"][:, t:t + 1])
                ctx.mark("act", ("lnc", key, gi))
                for rnd in range(2):
                    ctx.wait(T, "act", ("lnc", key, gi))
                    ctx.wait(T, "dve", "leprev")
                    last = None
                    for k in range(8):
                        fc = 8 * rnd + k
                        last = T.transpose(ps_t[:, 128 * k:128 * (k + 1)],
                                           lt[gi % 2][:, 128 * fc:128 * (fc + 1)],
                                           sb["identb"][:])
                    ctx.inc("pe", last)
                    ctx.mark("pe", ("lt", key, gi, rnd))
                    ctx.wait(V, "pe", ("lt", key, gi, rnd))
                    if tt == 0 and rnd == 0 and prev_mm_mark is not None:
                        ctx.wait(V, "pe", val=prev_mm_mark)
                    ctx.dve("tensor_copy",
                            ln1f_3d[:, 8 * rnd:8 * rnd + 8,
                                    128 * tt:128 * (tt + 1)],
                            pst_3d[:, :, :])
                    ctx.mark("dve", "leprev")
            ctx.mark("dve", ("lnf", key, blk))

        def gen_block_bcast(iota_name):
            if ("act", "tsb") in ctx.marks:
                ctx.wait(T, "act", "tsb")
            mm = T.matmul(ps_3[:, 0:512], sb["ones1f"][:],
                          sb[iota_name][0:1, :], start=True, stop=True)
            ctx.inc("pe", mm)
            ctx.mark("pe", "tbc")
            ctx.wait(A, "pe", "tbc")
            ctx.act("copy", tsb[:], ps_3[:, 0:512])
            ctx.mark("act", "tsb")

        def gen_tables(pr, tlo):
            ctx.wait(V, "act", "tsb")
            th, u, kk, y1 = tscr
            ctx.dve("tensor_scalar", th[:], tsb[:], float(tlo),
                    sb["invf"][:, pr:pr + 1], ALU.add, ALU.mult)
            ctx.dve("tensor_scalar", u[:], tsb[:], float(tlo),
                    sb["invf2"][:, pr:pr + 1], ALU.add, ALU.mult)
            ctx.dve("tensor_scalar", kk[:], u[:], MAGIC, MAGIC,
                    ALU.add, ALU.subtract)
            ctx.dve("scalar_tensor_tensor", y1[:], kk[:], -C1, th[:],
                    ALU.mult, ALU.add)
            ctx.dve("scalar_tensor_tensor", sarg[:], kk[:], -C2, y1[:],
                    ALU.mult, ALU.add)
            ctx.dve("tensor_scalar", kk[:], u[:], 0.25, None, ALU.add)
            ctx.dve("tensor_scalar", u[:], kk[:], MAGIC, MAGIC,
                    ALU.add, ALU.subtract)
            ctx.dve("scalar_tensor_tensor", y1[:], u[:], -C1, th[:],
                    ALU.mult, ALU.add)
            ctx.dve("scalar_tensor_tensor", kk[:], u[:], -C2, y1[:],
                    ALU.mult, ALU.add)
            ctx.dve("tensor_scalar", carg[:], kk[:], HALF_PI, None, ALU.add)
            ctx.mark("dve", "tblarg")
            ctx.wait(A, "dve", "tblarg")
            ctx.act("activation", tbls[:], sarg[:], AF.Sin)
            ctx.act("activation", tblc[:], carg[:], AF.Sin)
            ctx.mark("act", "tbl")

        def kq_pass(key, wname, xsrc, nblk, tile0, iota_name, store_q):
            load_weight(wname, key)
            for blk in range(nblk):
                if not store_q:
                    ctx.wait(V, "qg", "kspill_prev")
                pm = ctx.mval("pe", ("mm", key, blk - 1, 7)) if blk >= 1 else \
                    None
                pl_ = None
                if blk == 0 and ("pe", ("lt", prev_pass[key], 31, 1)) in \
                        ctx.marks:
                    pl_ = ctx.mval("pe", ("lt", prev_pass[key], 31, 1))
                ln_block(key, blk, tile0, xsrc, pm, pl_)
                ctx.wait(T, "dve", ("lnf", key, blk))
                if key == "k" and blk == 7 and "tap_lnf" in taps:
                    ctx.wait(SY, "dve", ("lnf", key, blk))
                    ctx.dma(SY, "st", taps["tap_lnf"], ln1f[:])
                gen_block_bcast(iota_name)
                for pr in range(8):
                    if key == "q" and blk == 0 and pr < 2:
                        ctx.wait(T, "act", ("vev", 7, 15))
                    gen_tables(pr, 512 * blk)
                    slot = (pr % 2) * 512
                    if pr >= 2:
                        ctx.wait(T, "act", ("kev", key, blk, pr - 2))
                    elif blk >= 1:
                        ctx.wait(T, "act", ("kev", key, blk - 1, pr + 6))
                    for half, ps in ((0, ps_1), (1, ps_2)):
                        dc = pr + 8 * half
                        last = None
                        for fc in range(16):
                            last = T.matmul(
                                ps[:, slot:slot + 512],
                                wslot[:, D * fc + 128 * dc:
                                      D * fc + 128 * (dc + 1)],
                                ln1f[:, 512 * fc:512 * (fc + 1)],
                                start=(fc == 0), stop=(fc == 15))
                        ctx.inc("pe", last)
                    ctx.mark("pe", ("mm", key, blk, pr))
                    ctx.wait(A, "pe", ("mm", key, blk, pr))
                    i, i8 = pr, pr + 8
                    if store_q:
                        d0 = q_dm[:, NT * i + 512 * blk:
                                  NT * i + 512 * blk + 512]
                        d1 = q_dm[:, NT * i8 + 512 * blk:
                                  NT * i8 + 512 * blk + 512]
                    else:
                        d0 = obuf[:, 512 * i:512 * (i + 1)]
                        d1 = obuf[:, 512 * i8:512 * (i8 + 1)]
                    ctx.act("copy", kei[:], ps_1[:, slot:slot + 512])
                    ctx.act("copy", kei8[:], ps_2[:, slot:slot + 512])
                    ctx.mark("act", ("kev", key, blk, pr))
                    ctx.wait(V, "act", ("kev", key, blk, pr))
                    r0, r1, r2, r3 = rscrb
                    ctx.dve("tensor_tensor", r0[:], kei[:], tblc[:], ALU.mult)
                    ctx.dve("tensor_tensor", r1[:], kei8[:], tbls[:], ALU.mult)
                    ctx.dve("tensor_tensor", d0, r0[:], r1[:], ALU.subtract)
                    ctx.dve("tensor_tensor", r2[:], kei8[:], tblc[:], ALU.mult)
                    ctx.dve("tensor_tensor", r3[:], kei[:], tbls[:], ALU.mult)
                    ctx.dve("tensor_tensor", d1, r2[:], r3[:], ALU.add)
                    ctx.mark("dve", ("rope", key, blk, pr))
                    if key == "k" and blk == 7 and pr == 0 and "tap_lnf" in taps:
                        ctx.wait(SY, "dve", "tbl")
                        ctx.dma(SY, "st", taps["tap_tblc"], tblc[:])
                        ctx.dma(SY, "st", taps["tap_tbls"], tbls[:])
                if not store_q:
                    ctx.mark("dve", ("kblk", blk))
                    ctx.gwait(gp, "dve", ("kblk", blk))
                    if blk == 7 and "tap_ob7" in taps:
                        gp.dma_start(out=taps["tap_ob7"],
                                     in_=obuf[:]).then_inc(ctx.sems["st"], 16)
                        ctx.cnt["st"] += 16
                    src = obuf[:].rearrange("p (dc t) -> p dc t", t=512)
                    with gp.If_eq(rg, 0):
                        gp.dma_start(
                            out=ksh3[:, :, 512 * blk:512 * (blk + 1)],
                            in_=src).then_inc(ctx.sems["qg"], 16)
                    with gp.Else():
                        gp.dma_start(
                            out=ksh3[:, :, KT + 512 * blk:KT + 512 * (blk + 1)],
                            in_=src).then_inc(ctx.sems["qg"], 16)
                    ctx.cnt["qg"] += 16
                    ctx.mark("qg", "kspill_prev")

        if PHASE_LIMIT >= 2:
            kq_pass("k", "wkT", ap["x_kv"], 8, 0, "iotak", False)

        if "tap_ksh" in taps and PHASE_LIMIT < 6:
            ctx.gwait(gp, "qg", "kspill_prev")
            gp.dma_start(out=taps["tap_ksh"], in_=ap["ksh"]).then_inc(
                ctx.sems["st"], 16)
            ctx.cnt["st"] += 16
            if PHASE_LIMIT >= 3:
                pass

        if PHASE_LIMIT >= 3:
          load_weight("wvT", "v")
          slots = [(ps_1, 0), (ps_1, 512), (ps_2, 0), (ps_2, 512)]
          for blk in range(8):
              ctx.wait(A, "qg", "kspill_prev")
              ctx.wait(A, "act", "vspill_guard")
              ln_block("v", blk, 0, ap["x_kv"])
              ctx.wait(T, "dve", ("lnf", "v", blk))
              it = 0
              for ssub in range(4):
                  for eb in range(4):
                      ps, off = slots[it % 4]
                      if it >= 4:
                          ctx.wait(T, "act", ("vev", blk, it - 4))
                      elif blk >= 1:
                          ctx.wait(T, "act", ("vev", blk - 1, 12 + it))
                      else:
                          ctx.wait(T, "dve", ("rope", "k", 7, 7))
                      last = None
                      for fc in range(16):
                          last = T.matmul(
                              ps[:, off:off + 512],
                              ln1f[:, 512 * fc + 128 * ssub:
                                   512 * fc + 128 * (ssub + 1)],
                              wslot[:, D * fc + 512 * eb:D * fc + 512 * (eb + 1)],
                              start=(fc == 0), stop=(fc == 15))
                      ctx.inc("pe", last)
                      ctx.mark("pe", ("vmm", blk, it))
                      ctx.wait(A, "pe", ("vmm", blk, it))
                      ctx.act("copy",
                              obuf[:, 2048 * ssub + 512 * eb:
                                   2048 * ssub + 512 * (eb + 1)],
                              ps[:, off:off + 512])
                      ctx.mark("act", ("vev", blk, it))
                      it += 1
              ctx.mark("act", ("vblk", blk))
              ctx.gwait(gp, "act", ("vblk", blk))
              trow0, trow1 = 512 * blk, KT + 512 * blk
              with gp.If_eq(rg, 0):
                  for cc in range(4):
                      src3 = obuf[:, 2048 * cc:2048 * (cc + 1)].rearrange(
                          "p (h e) -> p h e", e=HD)
                      r0 = trow0 + 128 * cc
                      gp.dma_start(
                          out=ap["vsh"][:, r0:r0 + 128, :].rearrange(
                              "h p e -> p h e"),
                          in_=src3).then_inc(ctx.sems["qg"], 16)
              with gp.Else():
                  for cc in range(4):
                      src3 = obuf[:, 2048 * cc:2048 * (cc + 1)].rearrange(
                          "p (h e) -> p h e", e=HD)
                      r1 = trow1 + 128 * cc
                      gp.dma_start(
                          out=ap["vsh"][:, r1:r1 + 128, :].rearrange(
                              "h p e -> p h e"),
                          in_=src3).then_inc(ctx.sems["qg"], 16)
              ctx.cnt["qg"] += 64
              ctx.mark("qg", ("vspill", blk))
              ctx.mark("qg", "kspill_prev")
          ctx.mark("qg", "vspill_done")

        if PHASE_LIMIT >= 4:
            kq_pass("q", "wqT", ap["x_own"], 2, NKT, "iotaq", True)
            ctx.mark("dve", "qdone")
            if "tap_q" in taps:
                ctx.wait(SY, "dve", "qdone")
                ctx.dma(SY, "st", taps["tap_q"], q_dm[:])

        sA1.close()
        if PHASE_LIMIT >= 5:
          v129 = alloc(sA, "v129", [128, 32 * 129 + 64], BF16)
          kfb = alloc(sA, "kfb", [128, 512], BF16)
          kfs = [alloc(sA, f"kfs{i}", [128, 512], F32) for i in range(2)]
          mstg = alloc(sA, "mstg", [128, 129], F32)
          v129_3 = v129[:, 0:4128].rearrange("p (c e) -> p c e", e=129) \
              if False else None

          krh = obuf
          ctx.mark("qg", "mspill_prev")
          for h in range(NH):
              ctx.gwait(gp, "dve", "qdone")
              ctx.gwait(gp, "qg", "vspill_done")
              if h >= 1:
                  ctx.gwait(gp, "pe", ("a2mm", h - 1))
              with gp.If_eq(rg, 0):
                  gp.dma_start(out=krh[:, 0:KT],
                               in_=ap["ksh"][128 * h:128 * (h + 1), 0:KT]
                               ).then_inc(ctx.sems["qg"], 16)
                  gp.dma_start(
                      out=v129[:, 0:4128].rearrange("p (c e) -> p c e", e=129)[:, 0:32, 0:128],
                      in_=ap["vsh"][h, 0:KT, :].rearrange("(c p) e -> p c e",
                                                          p=128)
                      ).then_inc(ctx.sems["qg"], 16)
              with gp.Else():
                  gp.dma_start(out=krh[:, 0:KT],
                               in_=ap["ksh"][128 * h:128 * (h + 1), KT:S]
                               ).then_inc(ctx.sems["qg"], 16)
                  gp.dma_start(
                      out=v129[:, 0:4128].rearrange("p (c e) -> p c e", e=129)[:, 0:32, 0:128],
                      in_=ap["vsh"][h, KT:S, :].rearrange("(c p) e -> p c e",
                                                          p=128)
                      ).then_inc(ctx.sems["qg"], 16)
              ctx.cnt["qg"] += 32
              ctx.mark("qg", ("a2kv", h))
              if h == 0:
                  ctx.wait(V, "qg", ("a2kv", h))
                  ctx.dve("memset",
                          v129[:, 0:4128].rearrange(
                              "p (c e) -> p c e", e=129)[:, 0:32, 128:129], 1.0)
                  ctx.mark("dve", "v129ones")
              for grp in range(8):
                  ctx.wait(T, "qg", ("a2kv", h))
                  ctx.wait(T, "dve", "leprev")
                  last = None
                  for g in range(4):
                      j = 4 * grp + g
                      last = T.transpose(ps_t[:, 128 * g:128 * (g + 1)],
                                         krh[:, 128 * j:128 * (j + 1)],
                                         sb["identb"][:])
                  ctx.inc("pe", last)
                  ctx.mark("pe", ("a2t", h, grp))
                  ctx.wait(V, "pe", ("a2t", h, grp))
                  ctx.dve("tensor_scalar", kfs[0][:], ps_t[:, 0:512], 0.0, None,
                          ALU.min)
                  ctx.mark("dve", ("a2m", h, grp))
                  ctx.wait(A, "dve", ("a2m", h, grp))
                  ctx.act("activation", kfs[1][:], kfs[0][:], AF.Exp)
                  ctx.mark("act", ("a2e", h, grp))
                  ctx.wait(V, "act", ("a2e", h, grp))
                  ctx.dve("scalar_tensor_tensor", kfb[:], ps_t[:, 0:512], 0.0,
                          kfs[1][:], ALU.max, ALU.add)
                  ctx.mark("dve", "leprev")
                  ctx.wait(T, "dve", "leprev")
                  ctx.wait(T, "dve", "v129ones")
                  last = None
                  for g in range(4):
                      j = 4 * grp + g
                      last = T.matmul(ps_3[:, 0:129],
                                      kfb[:, 128 * g:128 * (g + 1)],
                                      v129[:, 129 * j:129 * j + 129],
                                      start=(grp == 0 and g == 0),
                                      stop=(grp == 7 and g == 3))
                  ctx.inc("pe", last)
              ctx.mark("pe", ("a2mm", h))
              ctx.wait(A, "pe", ("a2mm", h))
              ctx.wait(A, "qg", "mspill_prev")
              ctx.act("copy", mstg[:], ps_3[:, 0:129])
              ctx.mark("act", ("a2ev", h))
              ctx.gwait(gp, "act", ("a2ev", h))
              with gp.If_eq(rg, 0):
                  gp.dma_start(out=ap["msh"][128 * h:128 * (h + 1), :],
                               in_=mstg[:]).then_inc(ctx.sems["qg"], 16)
              with gp.Else():
                  gp.dma_start(
                      out=ap["msh"][NH * 128 + 128 * h:NH * 128 + 128 * (h + 1), :],
                      in_=mstg[:]).then_inc(ctx.sems["qg"], 16)
              ctx.cnt["qg"] += 16
              ctx.mark("qg", "mspill_prev")

        if PHASE_LIMIT >= 6:
          with (gp.register("r_epoch") as r_epoch,
                gp.register("r_f") as r_f,
                gp.register("r_cont") as r_cont,
                gp.register("r_tgt") as r_tgt):
              gp.reg_load(r_epoch, sb["epoch_sb"][0:1, 0:1])
              with gp.If_eq(rg, 0):
                  gp.dma_start(out=ap["fsh"][0:1, 0:1],
                               in_=sb["epoch_sb"][0:1, 0:1]
                               ).then_inc(ctx.sems["qg"], 16)
              with gp.Else():
                  gp.dma_start(out=ap["fsh"][1:2, 0:1],
                               in_=sb["epoch_sb"][0:1, 0:1]
                               ).then_inc(ctx.sems["qg"], 16)
              ctx.cnt["qg"] += 16
              gp.wait_ge(ctx.sems["qg"], ctx.cnt["qg"])
              gp.reg_mov(r_cont, 1)
              gp.reg_mov(r_tgt, 16)
              with gp.While(r_cont):
                  with gp.If_eq(rg, 0):
                      gp.dma_start(out=sb["flag_sb"][0:1, 0:1],
                                   in_=ap["fsh"][1:2, 0:1]
                                   ).then_inc(ctx.sems["pl"], 16)
                  with gp.Else():
                      gp.dma_start(out=sb["flag_sb"][0:1, 0:1],
                                   in_=ap["fsh"][0:1, 0:1]
                                   ).then_inc(ctx.sems["pl"], 16)
                  gp.wait_ge(ctx.sems["pl"], r_tgt)
                  gp.reg_add(r_tgt, r_tgt, 16)
                  gp.reg_load(r_f, sb["flag_sb"][0:1, 0:1])
                  gp.reg_alu(r_cont, r_f, r_epoch, ALU.not_equal)

        if "tap_ksh" in taps:
            gp.dma_start(out=taps["tap_ksh"], in_=ap["ksh"]).then_inc(
                ctx.sems["st"], 16)
            gp.dma_start(out=taps["tap_vsh"],
                         in_=ap["vsh"].rearrange("h s e -> (h s) e")
                         ).then_inc(ctx.sems["st"], 16)
            ctx.cnt["st"] += 32
        if "tap_mem" in taps:
            gp.dma_start(out=taps["tap_mem"], in_=ap["msh"]).then_inc(
                ctx.sems["st"], 16)
            ctx.cnt["st"] += 16

    sBC = ExitStack()
    if PHASE_LIMIT >= 7:
      comb = alloc(sBC, "comb", [128, NH * NT], BF16, side="right")
      with ExitStack() as sB:
          ps_s0 = sB.enter_context(nc.psum_tensor([128, 1024], F32))
          ps_s1 = sB.enter_context(nc.psum_tensor([128, 1024], F32))
          ps_pv = sB.enter_context(nc.psum_tensor([128, 1024], F32))
          ps_den = sB.enter_context(nc.psum_tensor([1, 1024], F32))
          ps_s = [ps_s0, ps_s1]

          kh = [alloc(sB, f"kh{i}", [128, S], BF16) for i in range(2)]
          vt = [alloc(sB, f"vt{i}", [128, S], BF16) for i in range(2)]
          pT = [alloc(sB, f"pT{i}", [128, 1024], BF16) for i in range(2)]
          qfb = alloc(sB, "qfb", [128, 1024], BF16)
          qfs = [alloc(sB, f"qfs{i}", [128, 1024], F32) for i in range(2)]
          mem_sb = alloc(sB, "mem_sb", [128, 1024], F32)
          mp = alloc(sB, "mp", [128, 258], F32)
          mn = alloc(sB, "mn", [128, 129], BF16)
          Bsb = alloc(sB, "Bsb", [128, 1024], BF16)
          Blb = alloc(sB, "Blb", [128, 1024], BF16)
          u1 = alloc(sB, "u1", [128, 1024], F32)
          u2 = alloc(sB, "u2", [128, 1024], F32)
          rlr = alloc(sB, "rlr", [1, 1024], F32)
          rsr = alloc(sB, "rsr", [1, 1024], F32)
          rsb = alloc(sB, "rsb", [1, 1024], BF16)
          rlb = alloc(sB, "rlb", [1, 1024], BF16)

          for h in range(NH):
              if h >= 2:
                  ctx.gwait(gp, "pe", ("S", h - 2, 63))
                  ctx.gwait(gp, "pe", ("PV", h - 2, 63))
              if h >= 1:
                  ctx.gwait(gp, "dve", ("mn", h - 1))
              gp.dma_start(out=kh[h % 2][:],
                           in_=ap["ksh"][128 * h:128 * (h + 1), :]
                           ).then_inc(ctx.sems["qg"], 16)
              gp.dma_start(out=vt[h % 2][:].rearrange("p (c e) -> p c e", e=HD),
                           in_=ap["vsh"][h, :, :].rearrange("(c p) e -> p c e",
                                                            p=128)
                           ).then_inc(ctx.sems["qg"], 16)
              gp.dma_start(out=mp[:, 0:129],
                           in_=ap["msh"][128 * h:128 * (h + 1), :]
                           ).then_inc(ctx.sems["qg"], 16)
              gp.dma_start(out=mp[:, 129:258],
                           in_=ap["msh"][NH * 128 + 128 * h:
                                         NH * 128 + 128 * (h + 1), :]
                           ).then_inc(ctx.sems["qg"], 16)
              ctx.cnt["qg"] += 64
              ctx.mark("qg", ("bkv", h))

              ctx.wait(V, "qg", ("bkv", h))
              ctx.dve("tensor_tensor", mn[:], mp[:, 0:129], mp[:, 129:258],
                      ALU.add)
              ctx.mark("dve", ("mn", h))
              if h >= 1:
                  ctx.wait(V, "pe", ("bdl", h - 1))
              ctx.dve("tensor_scalar", qfs[0][:], q_dm[:, NT * h:NT * (h + 1)],
                      0.0, None, ALU.min)
              ctx.mark("dve", ("bqf0", h))
              ctx.wait(A, "dve", ("bqf0", h))
              ctx.act("activation", qfs[1][:], qfs[0][:], AF.Exp)
              ctx.mark("act", ("bqf1", h))
              ctx.wait(V, "act", ("bqf1", h))
              ctx.dve("scalar_tensor_tensor", qfb[:],
                      q_dm[:, NT * h:NT * (h + 1)], 0.0, qfs[1][:],
                      ALU.max, ALU.add)
              ctx.mark("dve", ("bqf", h))

              if h >= 1:
                  ctx.wait(T, "dve", ("comb", h - 1))
                  ctx.wait(T, "dve", ("rs", h - 1))
              ctx.wait(T, "dve", ("bqf", h))
              T.matmul(ps_pv[:, 0:512], mn[:, 0:128], qfb[:, 0:512],
                       start=True, stop=True)
              last = T.matmul(ps_pv[:, 512:1024], mn[:, 0:128], qfb[:, 512:1024],
                              start=True, stop=True)
              ctx.inc("pe", last)
              ctx.mark("pe", ("bmo", h))
              T.matmul(ps_den[0:1, 0:512], mn[:, 128:129], qfb[:, 0:512],
                       start=True, stop=True)
              last = T.matmul(ps_den[0:1, 512:1024], mn[:, 128:129],
                              qfb[:, 512:1024], start=True, stop=True)
              ctx.inc("pe", last)
              ctx.mark("pe", ("bdl", h))
              ctx.wait(A, "pe", ("bmo", h))
              if h >= 1:
                  ctx.wait(A, "dve", ("comb", h - 1))
              ctx.act("copy", mem_sb[:], ps_pv[:])
              ctx.mark("act", ("bmev", h))
              ctx.wait(V, "pe", ("bdl", h))
              ctx.dve("reciprocal", rlr[:], ps_den[0:1, :])
              ctx.mark("dve", ("brl", h))

              def emit_pv(j):
                  ctx.wait(T, "act", ("exp", h, j))
                  if j == 0:
                      ctx.wait(T, "act", ("bmev", h))
                      ctx.wait(T, "dve", ("brl", h))
                  T.matmul(ps_pv[:, 0:512], vt[h % 2][:, 128 * j:128 * (j + 1)],
                           pT[j % 2][:, 0:512], start=(j == 0), stop=(j == 63))
                  T.matmul(ps_pv[:, 512:1024],
                           vt[h % 2][:, 128 * j:128 * (j + 1)],
                           pT[j % 2][:, 512:1024], start=(j == 0), stop=(j == 63))
                  T.matmul(ps_den[0:1, 0:512], sb["ones_col"][:],
                           pT[j % 2][:, 0:512], start=(j == 0), stop=(j == 63))
                  last = T.matmul(ps_den[0:1, 512:1024], sb["ones_col"][:],
                                  pT[j % 2][:, 512:1024], start=(j == 0),
                                  stop=(j == 63))
                  ctx.inc("pe", last)
                  ctx.mark("pe", ("PV", h, j))

              for j in range(64):
                  if j >= 2:
                      ctx.wait(T, "act", ("exp", h, j - 2))
                  elif h >= 1:
                      ctx.wait(T, "act", ("bce", h - 1))
                  T.matmul(ps_s[j % 2][:, 0:512],
                           kh[h % 2][:, 128 * j:128 * (j + 1)],
                           q_dm[:, NT * h:NT * h + 512], start=True, stop=True)
                  last = T.matmul(ps_s[j % 2][:, 512:1024],
                                  kh[h % 2][:, 128 * j:128 * (j + 1)],
                                  q_dm[:, NT * h + 512:NT * h + 1024],
                                  start=True, stop=True)
                  ctx.inc("pe", last)
                  ctx.mark("pe", ("S", h, j))
                  ctx.wait(A, "pe", ("S", h, j))
                  ctx.act("activation", pT[j % 2][:], ps_s[j % 2][:], AF.Exp,
                          scale=SCALE, bias=EXP_BIAS)
                  ctx.mark("act", ("exp", h, j))
                  if j >= 1:
                      emit_pv(j - 1)
              emit_pv(63)

              ctx.wait(V, "pe", ("PV", h, 63))
              ctx.dve("reciprocal", rsr[:], ps_den[0:1, :])
              ctx.mark("dve", ("rs", h))
              ctx.dve("tensor_scalar", rsb[:], rsr[:],
                      sb["gate_1mg"][0:1, h:h + 1], None, ALU.mult)
              ctx.dve("tensor_scalar", rlb[:], rlr[:],
                      sb["gate_g"][0:1, h:h + 1], None, ALU.mult)
              ctx.mark("dve", ("brows", h))
              ctx.wait(T, "act", ("exp", h, 63))
              ctx.wait(T, "dve", ("brows", h))
              T.matmul(ps_s0[:, 0:512], sb["ones_row"][:], rsb[0:1, 0:512],
                       start=True, stop=True)
              T.matmul(ps_s0[:, 512:1024], sb["ones_row"][:], rsb[0:1, 512:1024],
                       start=True, stop=True)
              T.matmul(ps_s1[:, 0:512], sb["ones_row"][:], rlb[0:1, 0:512],
                       start=True, stop=True)
              last = T.matmul(ps_s1[:, 512:1024], sb["ones_row"][:],
                              rlb[0:1, 512:1024], start=True, stop=True)
              ctx.inc("pe", last)
              ctx.mark("pe", ("bc", h))
              ctx.wait(A, "pe", ("bc", h))
              ctx.act("copy", Bsb[:], ps_s0[:])
              ctx.act("copy", Blb[:], ps_s1[:])
              ctx.mark("act", ("bce", h))
              ctx.wait(V, "act", ("bce", h))
              ctx.dve("tensor_tensor", u1[:], ps_pv[:], Bsb[:], ALU.mult)
              ctx.dve("tensor_tensor", u2[:], mem_sb[:], Blb[:], ALU.mult)
              ctx.dve("tensor_tensor", comb[:, NT * h:NT * (h + 1)], u1[:],
                      u2[:], ALU.add)
              ctx.mark("dve", ("comb", h))

          ctx.mark("dve", "combdone")
          if "tap_comb" in taps:
              ctx.wait(SY, "dve", "combdone")
              ctx.dma(SY, "st", taps["tap_comb"], comb[:])

    sAB.close()

    ln2f = alloc(root, "ln2f", [128, 16 * NT], BF16)
    if PHASE_LIMIT >= 8:
      with ExitStack() as sC:
          cpa = sC.enter_context(nc.psum_tensor([128, 1024], F32))
          cpb = sC.enter_context(nc.psum_tensor([128, 1024], F32))
          ps_t = sC.enter_context(nc.psum_tensor([128, 1024], BF16))

          wo_bf = alloc(sC, "wo_bf", [128, 16 * D], BF16)
          xot = [alloc(sC, f"xot{i}", [128, D], F32) for i in range(2)]
          tC = alloc(sC, "tC", [128, D], F32)
          x1t = alloc(sC, "x1t", [128, D], F32)
          l2t = alloc(sC, "l2t", [128, D], BF16)
          st4 = alloc(sC, "st4", [128, 16], F32)
          sq4 = alloc(sC, "sq4", [128, 16], F32)

          ctx.wait(SY, "dve", "combdone")
          for i in range(16):
              if i >= 2:
                  ctx.wait(SY, "dve", ("cwo", i - 2))
              ctx.dma(SY, "qs", xot[i % 2][:], ap["woT"][128 * i:128 * (i + 1), :])
              ctx.mark("qs", ("cwl", i))
              ctx.wait(V, "qs", ("cwl", i))
              ctx.dve("tensor_copy", wo_bf[:, D * i:D * (i + 1)], xot[i % 2][:])
              ctx.mark("dve", ("cwo", i))
          ctx.mark("dve", "cwodone")

          psl = [cpa[:, 0:512], cpa[:, 512:1024], cpb[:, 0:512], cpb[:, 512:1024]]
          ln2f_3d = ln2f[:].rearrange("p (fc t) -> p fc t", t=NT)
          pst_3d = ps_t[:].rearrange("p (k e) -> p k e", e=128)
          ctx.mark("dve", "l2ev_prev")

          for j in range(8):
              ctx.wait(SY, "dve", "cwodone")
              if j >= 2:
                  ctx.wait(SY, "dve", ("cx1", j - 2))
              ctx.dma(SY, "qs", xot[j % 2][:],
                      ap["x_own"][128 * j:128 * (j + 1), :])
              ctx.mark("qs", ("cx", j))
              ctx.wait(T, "dve", "cwodone")
              if j >= 1:
                  ctx.wait(T, "act", ("cwoev", j - 1))
              last = None
              for ic in range(16):
                  for db in range(4):
                      last = T.matmul(
                          psl[db],
                          comb[:, NT * ic + 128 * j:NT * ic + 128 * (j + 1)],
                          wo_bf[:, D * ic + 512 * db:D * ic + 512 * (db + 1)],
                          start=(ic == 0), stop=(ic == 15))
              ctx.inc("pe", last)
              ctx.mark("pe", ("cwo", j))
              ctx.wait(V, "pe", ("cwo", j))
              ctx.wait(A, "pe", ("cwo", j))
              if CDBG == 1:
                  ctx.dve("memset", st4[:, 0:4], 0.0)
                  ctx.dve("memset", sq4[:, 0:4], float(D))
              elif CDBG == 2:
                  for db in range(4):
                      ctx.dve("reduce_sum", st4[:, db:db + 1], psl[db], AX.X)
                  ctx.dve("memset", sq4[:, 0:4], float(D))
              elif CDBG == 3:
                  ctx.dve("memset", st4[:, 0:4], 0.0)
                  for db in range(4):
                      ctx.act("activation", sb["junk"][:, 0:512], psl[db],
                              AF.Square, accum_out=sq4[:, db:db + 1])
              else:
                  for db in range(4):
                      ctx.dve("reduce_sum", st4[:, db:db + 1], psl[db], AX.X)
                  ctx.mark("dve", ("cred", j))
                  ctx.wait(A, "dve", ("cred", j))
                  for db in range(4):
                      ctx.act("activation", sb["junk"][:, 0:512], psl[db],
                              AF.Square, accum_out=sq4[:, db:db + 1])
              ctx.mark("act", ("csq", j))
              ctx.dve("reduce_sum", st4[:, 8:9], st4[:, 0:4], AX.X)
              ctx.wait(V, "act", ("csq", j))
              ctx.dve("reduce_sum", sq4[:, 8:9], sq4[:, 0:4], AX.X)
              ctx.dve("tensor_scalar", st4[:, 9:10], st4[:, 8:9], 1.0 / D, None,
                      ALU.mult)
              ctx.dve("tensor_scalar", sq4[:, 9:10], sq4[:, 8:9], 1.0 / D, None,
                      ALU.mult)
              ctx.dve("tensor_tensor", st4[:, 10:11], st4[:, 9:10], st4[:, 9:10],
                      ALU.mult)
              ctx.dve("tensor_tensor", sq4[:, 10:11], sq4[:, 9:10],
                      st4[:, 10:11], ALU.subtract)
              ctx.dve("tensor_scalar", sq4[:, 11:12], sq4[:, 10:11], LN_EPS,
                      None, ALU.add)
              ctx.dve("reciprocal", sq4[:, 12:13], sq4[:, 11:12])
              ctx.mark("dve", ("crv", j))
              ctx.wait(A, "dve", ("crv", j))
              ctx.act("activation", st4[:, 11:12], sq4[:, 12:13], AF.Sqrt)
              ctx.mark("act", ("crstd", j))
              ctx.wait(V, "act", ("crstd", j))
              ctx.dve("scalar_tensor_tensor", st4[:, 12:13], st4[:, 9:10], -1.0,
                      st4[:, 11:12], ALU.mult, ALU.mult)
              ctx.mark("dve", ("cbias", j))
              ctx.wait(A, "dve", ("cbias", j))
              for db in range(4):
                  ctx.act("activation", tC[:, 512 * db:512 * (db + 1)], psl[db],
                          AF.Identity, scale=st4[:, 11:12], bias=st4[:, 12:13])
              ctx.mark("act", ("cwoev", j))
              ctx.wait(V, "act", ("cwoev", j))
              ctx.wait(V, "qs", ("cx", j))
              if j >= 1:
                  ctx.wait(V, "qs", ("cx1s", j - 1))
              ctx.dve("tensor_tensor", x1t[:], tC[:], xot[j % 2][:], ALU.add)
              ctx.mark("dve", ("cx1", j))
              ctx.wait(SY, "dve", ("cx1", j))
              ctx.dma(SY, "qs", ap["x1d"][128 * j:128 * (j + 1), :], x1t[:])
              if "tap_x1" in taps:
                  ctx.dma(SY, "st", taps["tap_x1"][128 * j:128 * (j + 1), :],
                          x1t[:])
              ctx.mark("qs", ("cx1s", j))
              ctx.dve("reduce_sum", st4[:, 13:14], x1t[:], AX.X)
              ctx.wait(A, "dve", ("cx1", j))
              ctx.act("activation", sb["junk"][:], x1t[:], AF.Square,
                      accum_out=sq4[:, 13:14])
              ctx.mark("act", ("l2sq", j))
              ctx.dve("tensor_scalar", st4[:, 9:10], st4[:, 13:14], 1.0 / D,
                      None, ALU.mult)
              ctx.wait(V, "act", ("l2sq", j))
              ctx.dve("tensor_scalar", sq4[:, 9:10], sq4[:, 13:14], 1.0 / D,
                      None, ALU.mult)
              ctx.dve("tensor_tensor", st4[:, 10:11], st4[:, 9:10], st4[:, 9:10],
                      ALU.mult)
              ctx.dve("tensor_tensor", sq4[:, 10:11], sq4[:, 9:10],
                      st4[:, 10:11], ALU.subtract)
              ctx.dve("tensor_scalar", sq4[:, 11:12], sq4[:, 10:11], LN_EPS,
                      None, ALU.add)
              ctx.dve("reciprocal", sq4[:, 12:13], sq4[:, 11:12])
              ctx.mark("dve", ("l2rv", j))
              ctx.wait(A, "dve", ("l2rv", j))
              ctx.act("activation", st4[:, 11:12], sq4[:, 12:13], AF.Sqrt)
              ctx.mark("act", ("l2rstd", j))
              ctx.wait(V, "act", ("l2rstd", j))
              ctx.dve("scalar_tensor_tensor", st4[:, 12:13], st4[:, 9:10], -1.0,
                      st4[:, 11:12], ALU.mult, ALU.mult)
              ctx.mark("dve", ("l2bias", j))
              ctx.wait(A, "dve", ("l2bias", j))
              ctx.act("activation", l2t[:], x1t[:], AF.Identity,
                      scale=st4[:, 11:12], bias=st4[:, 12:13])
              ctx.mark("act", ("l2t", j))
              for rnd in range(2):
                  ctx.wait(T, "act", ("l2t", j))
                  ctx.wait(T, "dve", "l2ev_prev")
                  last = None
                  for k in range(8):
                      fc = 8 * rnd + k
                      last = T.transpose(ps_t[:, 128 * k:128 * (k + 1)],
                                         l2t[:, 128 * fc:128 * (fc + 1)],
                                         sb["identb"][:])
                  ctx.inc("pe", last)
                  ctx.mark("pe", ("l2T", j, rnd))
                  ctx.wait(V, "pe", ("l2T", j, rnd))
                  ctx.dve("tensor_copy",
                          ln2f_3d[:, 8 * rnd:8 * rnd + 8, 128 * j:128 * (j + 1)],
                          pst_3d[:, :, :])
                  ctx.mark("dve", "l2ev_prev")
          ctx.mark("dve", "ln2done")
          if "tap_ln2" in taps:
              ctx.wait(SY, "dve", "ln2done")
              ctx.dma(SY, "st", taps["tap_ln2"], ln2f[:])

    sBC.close()

    if PHASE_LIMIT >= 9:
      with ExitStack() as sD:
          dga = sD.enter_context(nc.psum_tensor([128, 1024], F32))
          dgb = sD.enter_context(nc.psum_tensor([128, 1024], F32))
          ps_t = sD.enter_context(nc.psum_tensor([128, 1024], BF16))
          dse = sD.enter_context(nc.psum_tensor([128, 512], F32))

          act_q = alloc(sD, "act_q", [128, 16 * NT], BF16)
          out_acc = [alloc(sD, f"out_acc{i}", [128, 16 * NT], BF16)
                     for i in range(2)]
          wgst = [alloc(sD, f"wgst{i}", [128, D], F32) for i in range(2)]
          wqh = [alloc(sD, f"wqh{i}", [128, D], BF16) for i in range(2)]
          qwb = alloc(sD, "qwb", [128, D], BF16)
          qtb = alloc(sD, "qtb", [128, D], BF16)
          qpb = alloc(sD, "qpb", [128, D], BF16)
          qsc = alloc(sD, "qsc", [128, 32], F32)
          qscb = alloc(sD, "qscb", [128, 32], BF16)
          qscn = alloc(sD, "qscn", [128, 32], BF16)
          qrec = alloc(sD, "qrec", [128, 32], F32)
          qrf = alloc(sD, "qrf", [128, 32], F32)
          wfm = [alloc(sD, f"wfm{i}", [128, 16 * 128], BF16) for i in range(2)]
          wpTb = alloc(sD, "wpTb", [128, 16 * 128], BF16)
          sgt = alloc(sD, "sgt", [128, 1024], F32)
          sqb = alloc(sD, "sqb", [128, 1024], BF16)
          invb = alloc(sD, "invb", [128, 16], F32)
          ftm = alloc(sD, "ftm", [128, D], BF16)
          ftf = alloc(sD, "ftf", [128, D], F32)

          ctx.mark("dve", "dq_prev")
          ctx.mark("act", "t16_prev")
          ctx.mark("pe", "t16pe_prev")

          def quant(key, ci):
              nG = 32
              ctx.wait(A, "qs", ("dwl", key))
              ctx.wait(A, "dve", "dq_prev")
              w3i = qwb[:, :].rearrange("p (g k) -> p g k", k=64)
              ctx.act("activation", qwb[:], wgst[ci % 2][:], AF.Identity)
              ctx.mark("act", ("qcast", ci))
              ctx.wait(V, "act", ("qcast", ci))
              ctx.dve("tensor_reduce", qrf[:, 0:nG], w3i, AX.X, ALU.add,
                      apply_absolute_value=True)
              ctx.dve("tensor_scalar", qsc[:, 0:nG], qrf[:, 0:nG], 1.0 / 64,
                      1e-8, ALU.mult, ALU.max)
              ctx.dve("tensor_copy", qscb[:, 0:nG], qsc[:, 0:nG])
              ctx.dve("tensor_scalar", qscn[:, 0:nG], qscb[:, 0:nG], -1.0, None,
                      ALU.mult)
              ctx.dve("tensor_copy", qsc[:, 0:nG], qscb[:, 0:nG])
              ctx.dve("reciprocal", qrec[:, 0:nG], qsc[:, 0:nG])
              ctx.dve("tensor_tensor",
                      qtb[:, :].rearrange("p (g k) -> p g k", k=64), w3i,
                      qrec[:, 0:nG].unsqueeze(2).broadcast_to([128, nG, 64]),
                      ALU.mult)
              ctx.dve("tensor_scalar", qpb[:], qtb[:], 0.5, None, ALU.is_gt)
              ctx.dve("scalar_tensor_tensor", qwb[:], qtb[:], -0.5, qpb[:],
                      ALU.is_lt, ALU.subtract)
              if ci >= 2:
                  ctx.wait(V, "pe", ("t16L", ci - 2))
              ctx.dve("tensor_tensor",
                      wqh[ci % 2][:, :].rearrange("p (g k) -> p g k", k=64),
                      qwb[:, :].rearrange("p (g k) -> p g k", k=64),
                      qscn[:, 0:nG].unsqueeze(2).broadcast_to([128, nG, 64]),
                      ALU.mult)
              ctx.mark("dve", ("quant", key))
              ctx.mark("dve", "dq_prev")

          def transpose16(key, ci, dst, prev_reader_mark):
              for rnd in range(2):
                  ctx.wait(T, "dve", ("quant", key))
                  ctx.wait(T, "act", "t16_prev")
                  last = None
                  for k in range(8):
                      fc = 8 * rnd + k
                      last = T.transpose(ps_t[:, 128 * k:128 * (k + 1)],
                                         wqh[ci % 2][:, 128 * fc:128 * (fc + 1)],
                                         sb["identb"][:])
                  ctx.inc("pe", last)
                  ctx.mark("pe", ("t16", key, rnd))
                  ctx.mark("pe", "t16pe_prev")
                  if rnd == 1:
                      ctx.mark("pe", ("t16L", ci))
                  ctx.wait(A, "pe", ("t16", key, rnd))
                  if prev_reader_mark is not None:
                      ctx.wait(A, "pe", val=prev_reader_mark)
                  ctx.act("activation", dst[:, 1024 * rnd:1024 * (rnd + 1)],
                          ps_t[:], AF.Identity)
                  ctx.mark("act", "t16_prev")
              ctx.mark("act", ("t16d", key))

          psl_d = {0: dga, 1: dgb}
          for q in range(NQ):
              for hp in range(16):
                  for wi, row0 in ((0, HQ * q + 128 * hp),
                                   (1, FH + HQ * q + 128 * hp)):
                      key = ("gu", q, hp, wi)
                      ci = q * 48 + hp * 2 + wi
                      if ci >= 2:
                          ctx.wait(SY, "act", ("qcast", ci - 2))
                      else:
                          ctx.wait(SY, "dve", "dq_prev")
                          ctx.wait(SY, "pe", "t16pe_prev")
                          ctx.wait(SY, "act", "t16_prev")
                      ctx.dma(SY, "qs", wgst[ci % 2][:],
                              ap["wgu"][row0:row0 + 128, :])
                      ctx.mark("qs", ("dwl", key))
                      quant(key, ci)
                      prev = None
                      if (q, hp) != (0, 0):
                          prevkey = ("dgu", q, hp - 1, wi) if hp >= 1 else \
                              ("dgu", q - 1, 15, wi)
                          prev = ctx.mval("pe", prevkey)
                      transpose16(key, ci, wfm[wi], prev)
                      ps = psl_d[wi]
                      ctx.wait(T, "act", ("t16d", key))
                      ctx.wait(T, "dve", "ln2done")
                      if (q, hp) != (0, 0):
                          ctx.wait(T, "dve", ("dact", q, hp - 1) if hp >= 1
                                   else ("dact", q - 1, 15))
                      if hp == 0 and q >= 1:
                          ctx.wait(T, "dve", ("dpe", q - 1, 15))
                      last = None
                      for fc in range(16):
                          last = T.matmul(ps[:, 0:512],
                                          wfm[wi][:, 128 * fc:128 * (fc + 1)],
                                          ln2f[:, NT * fc:NT * fc + 512],
                                          start=(fc == 0), stop=(fc == 15))
                          last = T.matmul(ps[:, 512:1024],
                                          wfm[wi][:, 128 * fc:128 * (fc + 1)],
                                          ln2f[:, NT * fc + 512:NT * fc + 1024],
                                          start=(fc == 0), stop=(fc == 15))
                      ctx.inc("pe", last)
                      ctx.mark("pe", ("dgu", q, hp, wi))
                  ctx.wait(A, "pe", ("dgu", q, hp, 0))
                  ctx.act("activation", sgt[:], dga[:], AF.Silu)
                  ctx.mark("act", ("dsilu", q, hp))
                  ctx.wait(V, "act", ("dsilu", q, hp))
                  ctx.wait(V, "pe", ("dgu", q, hp, 1))
                  ctx.dve("tensor_tensor", act_q[:, NT * hp:NT * (hp + 1)],
                          dgb[:], sgt[:], ALU.mult)
                  ctx.mark("dve", ("dact", q, hp))
                  ctx.wait(A, "dve", ("dact", q, hp))
                  ctx.act("activation", sqb[:], act_q[:, NT * hp:NT * (hp + 1)],
                          AF.Square)
                  ctx.mark("act", ("dsq", q, hp))
                  ctx.wait(T, "act", ("dsq", q, hp))
                  last = None
                  for jj in range(8):
                      last = T.matmul(dse[:, jj:jj + 1],
                                      sqb[:, 128 * jj:128 * (jj + 1)],
                                      sb["ones_col"][:],
                                      start=(q == 0 and hp == 0 and jj == 0),
                                      stop=(q == NQ - 1 and hp == 15))
                  ctx.inc("pe", last)
                  ctx.mark("pe", ("dssq", q, hp))

              for dc in range(16):
                  key = ("pj", q, dc)
                  ci = q * 48 + 32 + dc
                  ctx.wait(SY, "act", ("qcast", ci - 2))
                  ctx.dma(SY, "qs", wgst[ci % 2][:],
                          ap["wp"][128 * dc:128 * (dc + 1), HQ * q:HQ * (q + 1)])
                  ctx.mark("qs", ("dwl", key))
                  quant(key, ci)
                  prev = ctx.mval("pe", ("dpj", q, dc - 1)) if dc >= 1 else \
                      (ctx.mval("pe", ("dpj", q - 1, 15)) if q >= 1 else None)
                  transpose16(key, ci, wpTb, prev)
                  ps = psl_d[dc % 2]
                  ctx.wait(T, "act", ("t16d", key))
                  if dc < 2:
                      ctx.wait(T, "dve", ("dact", q, 15))
                  if dc >= 2:
                      ctx.wait(T, "dve", ("dpe", q, dc - 2))
                  elif q >= 1:
                      ctx.wait(T, "dve", ("dpe", q - 1, 14 + dc))
                  last = None
                  for hc in range(16):
                      last = T.matmul(ps[:, 0:512],
                                      wpTb[:, 128 * hc:128 * (hc + 1)],
                                      act_q[:, NT * hc:NT * hc + 512],
                                      start=(hc == 0), stop=(hc == 15))
                      last = T.matmul(ps[:, 512:1024],
                                      wpTb[:, 128 * hc:128 * (hc + 1)],
                                      act_q[:, NT * hc + 512:NT * hc + 1024],
                                      start=(hc == 0), stop=(hc == 15))
                  ctx.inc("pe", last)
                  ctx.mark("pe", ("dpj", q, dc))
                  ctx.wait(V, "pe", ("dpj", q, dc))
                  if q == 0:
                      ctx.dve("tensor_copy",
                              out_acc[0][:, NT * dc:NT * (dc + 1)], ps[:])
                  else:
                      ctx.dve("tensor_tensor",
                              out_acc[q % 2][:, NT * dc:NT * (dc + 1)], ps[:],
                              out_acc[(q - 1) % 2][:, NT * dc:NT * (dc + 1)],
                              ALU.add)
                  ctx.mark("dve", ("dpe", q, dc))

          ctx.wait(V, "pe", ("dssq", NQ - 1, 15))
          ctx.dve("tensor_scalar", invb[:, 0:8], dse[:, 0:8], 1.0 / FH, RMS_EPS,
                  ALU.mult, ALU.add)
          ctx.dve("reciprocal", invb[:, 8:16], invb[:, 0:8])
          ctx.mark("dve", "drv")
          ctx.wait(A, "dve", "drv")
          ctx.act("activation", invb[:, 0:8], invb[:, 8:16], AF.Sqrt)
          ctx.mark("act", "dinv")

          if "tap_ffn" in taps:
              ctx.wait(SY, "dve", ("dpe", NQ - 1, 15))
              ctx.dma(SY, "st", taps["tap_ffn"], out_acc[(NQ - 1) % 2][:])

          pst_3d = ps_t[:].rearrange("p (k e) -> p k e", e=128)
          ftm_3d = ftm[:].rearrange("p (k e) -> p k e", e=128)
          ctx.mark("dve", "et16_prev")
          fln = wgst[0]
          x1t2 = wgst[1]
          last_ci = (NQ - 1) * 48 + 32 + 15
          for j in range(8):
              if j >= 1:
                  ctx.wait(SY, "dve", ("eout", j - 1))
              else:
                  ctx.wait(SY, "act", ("qcast", last_ci))
              ctx.dma(SY, "qs", x1t2[:], ap["x1d"][128 * j:128 * (j + 1), :])
              ctx.mark("qs", ("ex1", j))
              for rnd in range(2):
                  ctx.wait(T, "dve", ("dpe", NQ - 1, 15))
                  ctx.wait(T, "act", "t16_prev")
                  ctx.wait(T, "dve", "et16_prev")
                  last = None
                  for k in range(8):
                      dc = 8 * rnd + k
                      last = T.transpose(
                          ps_t[:, 128 * k:128 * (k + 1)],
                          out_acc[(NQ - 1) % 2][
                              :, NT * dc + 128 * j:NT * dc + 128 * (j + 1)],
                          sb["identb"][:])
                  ctx.inc("pe", last)
                  ctx.mark("pe", ("eT", j, rnd))
                  ctx.wait(V, "pe", ("eT", j, rnd))
                  if j >= 1 and rnd == 0:
                      ctx.wait(V, "act", ("eln", j - 1))
                  ctx.dve("tensor_copy", ftm_3d[:, 8 * rnd:8 * rnd + 8, :],
                          pst_3d[:, :, :])
                  ctx.mark("dve", "et16_prev")
              ctx.wait(V, "act", "dinv")
              if j >= 1:
                  ctx.wait(V, "st", ("y", j - 1))
              ctx.dve("tensor_scalar", ftf[:], ftm[:], invb[:, j:j + 1], None,
                      ALU.mult)
              ctx.mark("dve", ("eff", j))
              ctx.dve("reduce_sum", sb["sc_a"][:, 0:1], ftf[:], AX.X)
              ctx.wait(A, "dve", ("eff", j))
              ctx.act("activation", sb["junk"][:], ftf[:], AF.Square,
                      accum_out=sb["sc_a"][:, 1:2])
              ctx.mark("act", ("esq", j))
              ctx.dve("tensor_scalar", sb["sc_a"][:, 2:3], sb["sc_a"][:, 0:1],
                      1.0 / D, None, ALU.mult)
              ctx.wait(V, "act", ("esq", j))
              ctx.dve("tensor_scalar", sb["sc_a"][:, 3:4], sb["sc_a"][:, 1:2],
                      1.0 / D, None, ALU.mult)
              ctx.dve("tensor_tensor", sb["sc_a"][:, 4:5], sb["sc_a"][:, 2:3],
                      sb["sc_a"][:, 2:3], ALU.mult)
              ctx.dve("tensor_tensor", sb["sc_a"][:, 5:6], sb["sc_a"][:, 3:4],
                      sb["sc_a"][:, 4:5], ALU.subtract)
              ctx.dve("tensor_scalar", sb["sc_a"][:, 6:7], sb["sc_a"][:, 5:6],
                      LN_EPS, None, ALU.add)
              ctx.dve("reciprocal", sb["sc_a"][:, 7:8], sb["sc_a"][:, 6:7])
              ctx.mark("dve", ("erv", j))
              ctx.wait(A, "dve", ("erv", j))
              ctx.act("activation", sb["sc_a"][:, 8:9], sb["sc_a"][:, 7:8],
                      AF.Sqrt)
              ctx.mark("act", ("erstd", j))
              ctx.wait(V, "act", ("erstd", j))
              ctx.dve("scalar_tensor_tensor", sb["sc_a"][:, 9:10],
                      sb["sc_a"][:, 2:3], -1.0, sb["sc_a"][:, 8:9],
                      ALU.mult, ALU.mult)
              ctx.mark("dve", ("ebias", j))
              ctx.wait(A, "dve", ("ebias", j))
              ctx.act("activation", fln[:], ftf[:], AF.Identity,
                      scale=sb["sc_a"][:, 8:9], bias=sb["sc_a"][:, 9:10])
              ctx.mark("act", ("eln", j))
              ctx.wait(V, "act", ("eln", j))
              ctx.wait(V, "qs", ("ex1", j))
              ctx.dve("tensor_tensor", ftf[:], fln[:], x1t2[:], ALU.add)
              ctx.mark("dve", ("eout", j))
              ctx.wait(SY, "dve", ("eout", j))
              ctx.dma(SY, "st", ap["y"][128 * j:128 * (j + 1), :], ftf[:])
              ctx.mark("st", ("y", j))

    if ctx.cnt["st"] > 0:
        ctx.wait(SY, "st", val=ctx.cnt["st"])
        gp.wait_ge(ctx.sems["st"], ctx.cnt["st"])
    else:
        gp.nop()


def kernel(L0_attn_hidden_states, L0_attn_Wq, L0_attn_Wk, L0_attn_Wv,
           L0_attn_Wo, L0_attn_gate, L0_ffn_w_gate_up, L0_ffn_w_proj):
    X = np.ascontiguousarray(np.asarray(L0_attn_hidden_states,
                                        np.float32).reshape(S, D))
    wqT = np.ascontiguousarray(np.asarray(L0_attn_Wq, np.float32).T)
    wkT = np.ascontiguousarray(np.asarray(L0_attn_Wk, np.float32).T)
    wvT = np.ascontiguousarray(np.asarray(L0_attn_Wv, np.float32).T)
    woT = np.ascontiguousarray(np.asarray(L0_attn_Wo, np.float32).T)
    wgu = np.ascontiguousarray(np.asarray(L0_ffn_w_gate_up, np.float32))
    wp = np.ascontiguousarray(np.asarray(L0_ffn_w_proj, np.float32))
    gate = np.ascontiguousarray(
        np.asarray(L0_attn_gate, np.float32).reshape(1, NH))

    inv_freq = (1.0 / 10000.0 ** (np.arange(0, D, 2, dtype=np.float64) / D)
                ).astype(np.float32)
    invf = np.ascontiguousarray(inv_freq.reshape(8, 128).T)
    invf2 = np.ascontiguousarray(
        (invf.astype(np.float64) / (2 * np.pi)).astype(np.float32))
    ident = np.eye(128, dtype=np.float32)
    epoch = np.array([[np.random.randint(1, 2 ** 31 - 1)]], np.int32)
    ar512 = np.arange(512, dtype=np.float32)

    nc, taps = build_program()
    in_maps = []
    for c in range(8):
        role = c & 1
        in_maps.append({
            "x_own": np.ascontiguousarray(X[NT * c:NT * (c + 1)]),
            "x_kv": np.ascontiguousarray(X[KT * role:KT * (role + 1)]),
            "wqT": wqT, "wkT": wkT, "wvT": wvT, "woT": woT,
            "wgu": wgu, "wp": wp, "gate": gate,
            "iotak": (KT * role + ar512)[None, :],
            "iotaq": (NT * c + ar512)[None, :],
            "invf": invf, "invf2": invf2, "identm": ident,
            "role_t": np.array([[role]], np.int32),
            "epoch_t": epoch,
        })
    res = run_bass_kernel_spmd(nc, in_maps, core_ids=list(range(8)))
    out = np.concatenate([res.results[c]["y"] for c in range(8)], axis=0)
    return out.reshape(1, S, D)


# version 08
# speedup vs baseline: 1.2823x; 1.2731x over previous
import math
import numpy as np
import concourse.bass as bass
import concourse.mybir as mybir
from concourse.bass_utils import run_bass_kernel_spmd
from contextlib import ExitStack

F32 = mybir.dt.float32
BF16 = mybir.dt.bfloat16
I32 = mybir.dt.int32
AF = mybir.ActivationFunctionType
ALU = mybir.AluOpType
AX = mybir.AxisListType

S = 8192
D = 2048
NH = 16
HD = 128
NT = 1024
KT = 4096
FH = 8192
NQ = 4
HQ = FH // NQ

SCALE = 1.0 / math.sqrt(HD)
EXP_BIAS = -12.0
LN_EPS = 1e-5
RMS_EPS = float(np.finfo(np.float32).eps)
MAGIC = 12582912.0
C1 = 6.28125
C2 = 0.0019353071795864769
HALF_PI = math.pi / 2.0

DMA_SEMS = ("qs", "qg", "pl", "st")
DEBUG_TAPS = ()
PHASE_LIMIT = 99
PARANOID = False
CDBG = 0


class Ctx:
    def __init__(self, nc):
        self.nc = nc
        self.sems = {}
        self.cnt = {}
        self.marks = {}
        self.last_wait = {}

    def add_sem(self, stack, name):
        self.sems[name] = stack.enter_context(self.nc.semaphore(name))
        self.cnt[name] = 0

    def inc(self, name, instr, n=1):
        step = 16 if name in DMA_SEMS else 1
        self.cnt[name] += n * step
        instr.then_inc(self.sems[name], n * step)
        return self.cnt[name]

    def mark(self, name, key):
        self.marks[(name, key)] = self.cnt[name]

    def mval(self, name, key):
        return self.marks[(name, key)]

    def wait(self, eng, name, key=None, val=None):
        if val is None:
            val = self.marks[(name, key)] if key is not None else self.cnt[name]
        if val <= 0:
            return
        k = (id(eng), name)
        if self.last_wait.get(k, -1) >= val:
            return
        self.last_wait[k] = val
        eng.wait_ge(self.sems[name], val)

    def gwait(self, gp, name, key=None, val=None):
        if val is None:
            val = self.marks[(name, key)] if key is not None else self.cnt[name]
        if val <= 0:
            return
        k = ("gp", name)
        if self.last_wait.get(k, -1) >= val:
            return
        self.last_wait[k] = val
        gp.wait_ge(self.sems[name], val)

    def dve(self, opname, *a, **kw):
        eng = self.nc.vector
        self.wait(eng, "dve", val=self.cnt["dve"])
        return self.inc("dve", getattr(eng, opname)(*a, **kw))

    def act(self, opname, *a, **kw):
        eng = self.nc.scalar
        self.wait(eng, "act", val=self.cnt["act"])
        return self.inc("act", getattr(eng, opname)(*a, **kw))

    def dma(self, eng, sem, out, in_, **kw):
        return self.inc(sem, eng.dma_start(out=out, in_=in_, **kw))


class StopEmission(Exception):
    pass


def build_program():
    nc = bass.Bass("TRN2", target_bir_lowering=False)
    ap = {}

    def din(name, shape, dt=F32):
        ap[name] = nc.dram_tensor(name, shape, dt, kind="ExternalInput").ap()

    din("x_own", [NT, D])
    din("x_kv", [KT, D])
    din("wqT", [D, D])
    din("wkT", [D, D])
    din("wvT", [D, D])
    din("woT", [D, D])
    din("wgu", [2 * FH, D])
    din("wp", [D, FH])
    din("gate", [1, NH])
    din("iotak", [1, 512])
    din("iotaq", [1, 512])
    din("invf", [128, 8])
    din("invf2", [128, 8])
    din("identm", [128, 128])
    din("role_t", [1, 1], I32)
    din("epoch_t", [1, 1], I32)
    ap["y"] = nc.dram_tensor("y", [NT, D], F32, kind="ExternalOutput").ap()

    ap["ksh"] = nc.dram_tensor("ksh", [D, S], BF16, kind="Internal",
                               addr_space="Shared").ap()
    ap["vsh"] = nc.dram_tensor("vsh", [NH, S, HD], BF16, kind="Internal",
                               addr_space="Shared").ap()
    ap["msh"] = nc.dram_tensor("msh", [2 * NH * 128, 129], F32, kind="Internal",
                               addr_space="Shared").ap()
    ap["fsh"] = nc.dram_tensor("fsh", [2, 8], I32, kind="Internal",
                               addr_space="Shared").ap()
    ap["x1d"] = nc.dram_tensor("x1d", [NT, D], F32, kind="Internal").ap()

    taps = {}

    def tout(name, shape, dt):
        taps[name] = nc.dram_tensor(name, shape, dt, kind="ExternalOutput").ap()

    if "ksh" in DEBUG_TAPS:
        tout("tap_ksh", [D, S], BF16)
        tout("tap_vsh", [NH * S, HD], BF16)
    if "q" in DEBUG_TAPS:
        tout("tap_q", [128, NH * NT], BF16)
    if "mem" in DEBUG_TAPS:
        tout("tap_mem", [2 * NH * 128, 129], F32)
    if "comb" in DEBUG_TAPS:
        tout("tap_comb", [128, NH * NT], BF16)
    if "x1" in DEBUG_TAPS:
        tout("tap_x1", [NT, D], F32)
    if "ln2" in DEBUG_TAPS:
        tout("tap_ln2", [128, 16 * NT], BF16)
    if "ffn" in DEBUG_TAPS:
        tout("tap_ffn", [128, 16 * NT], BF16)
    if "ob7" in DEBUG_TAPS:
        tout("tap_ob7", [128, 16 * 512], BF16)
    if "lnf" in DEBUG_TAPS:
        tout("tap_lnf", [128, 16 * 512], BF16)
        tout("tap_tblc", [128, 512], F32)
        tout("tap_tbls", [128, 512], F32)

    with ExitStack() as root:
        ctx = Ctx(nc)
        for s_ in ("qs", "qg", "pl", "st", "pe", "act", "dve"):
            ctx.add_sem(root, s_)
        blk = root.enter_context(nc.Block())

        @blk.gpsimd
        def _(gp):
            emit(nc, ctx, ap, taps, gp, root)

    return nc, taps


def emit(nc, ctx, ap, taps, gp, root):
    V, A, T, SY = nc.vector, nc.scalar, nc.tensor, nc.sync
    sb = {}

    def alloc(stack, name, shape, dt, side="left"):
        sb[name] = stack.enter_context(
            nc.sbuf_tensor("s_" + name, shape, dt, side=side))
        return sb[name]

    for nm, shp, dt in (
        ("invf", [128, 8], F32), ("invf2", [128, 8], F32),
        ("ones_col", [128, 1], BF16), ("ones_row", [1, 128], BF16),
        ("ones1f", [1, 128], F32),
        ("gate_raw", [1, NH], F32), ("gate_g", [1, NH], F32),
        ("gate_1mg", [1, NH], F32),
        ("role_sb", [1, 1], I32), ("epoch_sb", [1, 8], I32),
        ("flag_sb", [1, 8], I32),
        ("iotak", [1, 512], F32), ("iotaq", [1, 512], F32),
        ("st_sum", [128, 48], F32), ("st_ssq", [128, 48], F32),
        ("st_rstd", [128, 48], F32), ("st_bias", [128, 48], F32),
        ("sc_a", [128, 48], F32), ("sc_b", [128, 48], F32),
        ("identb", [128, 128], BF16), ("identf", [128, 128], F32),
        ("junk", [128, D], BF16), ("cexpb", [128, 1], F32),
    ):
        alloc(root, nm, shp, dt)

    V.memset(sb["ones_col"][:], 1.0)
    V.memset(sb["ones_row"][:], 1.0)
    V.memset(sb["ones1f"][:], 1.0)
    V.memset(sb["cexpb"][:], EXP_BIAS)
    nc.const_aps.register(F32, EXP_BIAS, sb["cexpb"][:])

    for nm in ("invf", "invf2", "iotak", "iotaq", "identm"):
        dst = "identf" if nm == "identm" else nm
        ctx.dma(SY, "qs", sb[dst][:], ap[nm])
    ctx.dma(SY, "qs", sb["gate_raw"][:], ap["gate"])
    ctx.dma(SY, "qs", sb["role_sb"][:], ap["role_t"])
    ctx.dma(SY, "qs", sb["epoch_sb"][:, 0:1], ap["epoch_t"])
    ctx.mark("qs", "consts")
    ctx.wait(V, "qs", "consts")
    ctx.dve("tensor_copy", sb["identb"][:], sb["identf"][:])
    ctx.mark("dve", "identb")

    rg = gp.register("r_role").__enter__()
    ctx.gwait(gp, "qs", "consts")
    gp.reg_load(rg, sb["role_sb"][0:1, 0:1])

    NKT, NOT_ = KT // 128, NT // 128
    NTILES = NKT + NOT_
    with ExitStack() as sA0:
        xb0 = [alloc(sA0, f"a0x{i}", [128, D], F32) for i in range(2)]
        ctx.wait(A, "qs", "consts")
        ctx.act("activation", sb["gate_g"][:], sb["gate_raw"][:], AF.Sigmoid)
        ctx.mark("act", "gate")
        ctx.wait(V, "act", "gate")
        ctx.dve("tensor_scalar", sb["gate_1mg"][:], sb["gate_g"][:],
                -1.0, 1.0, ALU.mult, ALU.add)
        for t in range(NTILES):
            src = (ap["x_kv"][128 * t:128 * (t + 1), :] if t < NKT else
                   ap["x_own"][128 * (t - NKT):128 * (t - NKT + 1), :])
            if t >= 2:
                ctx.wait(SY, "act", ("a0", t - 2))
                ctx.wait(SY, "dve", ("a0v", t - 2))
            ctx.dma(SY, "qs", xb0[t % 2][:], src)
            ctx.mark("qs", ("a0x", t))
            ctx.wait(V, "qs", ("a0x", t))
            ctx.dve("reduce_sum", sb["st_sum"][:, t:t + 1], xb0[t % 2][:], AX.X)
            ctx.mark("dve", ("a0v", t))
            ctx.wait(A, "qs", ("a0x", t))
            ctx.act("activation", sb["junk"][:], xb0[t % 2][:], AF.Square,
                    accum_out=sb["st_ssq"][:, t:t + 1])
            ctx.mark("act", ("a0", t))
        n = NTILES
        ctx.dve("tensor_scalar", sb["sc_a"][:, 0:n], sb["st_sum"][:, 0:n],
                1.0 / D, None, ALU.mult)
        ctx.wait(V, "act", ("a0", NTILES - 1))
        ctx.dve("tensor_scalar", sb["sc_b"][:, 0:n], sb["st_ssq"][:, 0:n],
                1.0 / D, None, ALU.mult)
        ctx.dve("tensor_tensor", sb["st_rstd"][:, 0:n], sb["sc_a"][:, 0:n],
                sb["sc_a"][:, 0:n], ALU.mult)
        ctx.dve("tensor_tensor", sb["st_bias"][:, 0:n], sb["sc_b"][:, 0:n],
                sb["st_rstd"][:, 0:n], ALU.subtract)
        ctx.dve("tensor_scalar", sb["sc_b"][:, 0:n], sb["st_bias"][:, 0:n],
                LN_EPS, None, ALU.add)
        ctx.dve("reciprocal", sb["st_rstd"][:, 0:n], sb["sc_b"][:, 0:n])
        ctx.mark("dve", "a0rv")
        ctx.wait(A, "dve", "a0rv")
        ctx.act("activation", sb["st_bias"][:, 0:n], sb["st_rstd"][:, 0:n],
                AF.Sqrt)
        ctx.mark("act", "a0rstd")
        ctx.wait(V, "act", "a0rstd")
        ctx.dve("tensor_copy", sb["st_rstd"][:, 0:n], sb["st_bias"][:, 0:n])
        ctx.dve("scalar_tensor_tensor", sb["st_bias"][:, 0:n],
                sb["sc_a"][:, 0:n], -1.0, sb["st_rstd"][:, 0:n],
                ALU.mult, ALU.mult)
        ctx.mark("dve", "stats")

    sAB = ExitStack()
    q_dm = alloc(sAB, "q_dm", [128, NH * NT], BF16)

    with ExitStack() as sA:
        ps_1 = sA.enter_context(nc.psum_tensor([128, 1024], F32))
        ps_2 = sA.enter_context(nc.psum_tensor([128, 1024], F32))
        ps_t = sA.enter_context(nc.psum_tensor([128, 1024], BF16))
        ps_3 = sA.enter_context(nc.psum_tensor([128, 512], F32))

        ln1f = alloc(sA, "ln1f", [128, 16 * 512], BF16)
        obuf = alloc(sA, "obuf", [128, 16 * 512], BF16)
        ln1f_3d = ln1f[:].rearrange("p (fc t) -> p fc t", t=512)
        pst_3d = ps_t[:].rearrange("p (k e) -> p k e", e=128)
        ksh3 = ap["ksh"].rearrange("(dc p) t -> p dc t", p=128)
        prev_pass = {"k": None, "v": "k", "q": "v"}

        sA1 = ExitStack()
        wslot = alloc(sA1, "wslot", [128, 16 * D], BF16)
        xb = [alloc(sA1, f"xab{i}", [128, D], F32) for i in range(2)]
        lt = [alloc(sA1, f"lt{i}", [128, D], BF16) for i in range(2)]
        tsb = alloc(sA1, "tsb", [128, 512], F32)
        tscr = [alloc(sA1, f"tscr{i}", [128, 512], F32) for i in range(4)]
        sarg = alloc(sA1, "sarg", [128, 512], F32)
        carg = alloc(sA1, "carg", [128, 512], F32)
        tblc = alloc(sA1, "tblc", [128, 512], BF16)
        tbls = alloc(sA1, "tbls", [128, 512], BF16)
        kei = alloc(sA1, "kei", [128, 512], BF16)
        kei8 = alloc(sA1, "kei8", [128, 512], BF16)
        rscrb = [alloc(sA1, f"rscrb{i}", [128, 512], BF16) for i in range(4)]

        ctx.mark("dve", "leprev")
        ctx.mark("qg", "kspill_prev")
        ctx.mark("act", "vspill_guard")

        def load_weight(wname, key):
            ctx.wait(V, "pe", val=ctx.cnt["pe"])
            ctx.wait(SY, "act", val=ctx.cnt["act"])
            ctx.wait(SY, "dve", val=ctx.cnt["dve"])
            for i in range(16):
                if i >= 2:
                    ctx.wait(SY, "dve", ("wc", key, i - 2))
                ctx.dma(SY, "qs", xb[i % 2][:],
                        ap[wname][128 * i:128 * (i + 1), :])
                ctx.mark("qs", ("wl", key, i))
                ctx.wait(V, "qs", ("wl", key, i))
                ctx.dve("tensor_copy", wslot[:, D * i:D * (i + 1)],
                        xb[i % 2][:])
                ctx.mark("dve", ("wc", key, i))
            ctx.mark("dve", ("wdone", key))

        def ln_block(key, blk, tile0, xsrc, prev_mm_mark=None,
                     prev_lt_mark=None):
            ctx.wait(SY, "dve", ("wdone", key))
            for tt in range(4):
                t = tile0 + 4 * blk + tt
                gi = 4 * blk + tt
                r0 = 512 * blk + 128 * tt
                if gi >= 2:
                    ctx.wait(SY, "act", ("lnc", key, gi - 2))
                ctx.dma(SY, "qs", xb[gi % 2][:], xsrc[r0:r0 + 128, :])
                ctx.mark("qs", ("xa", key, gi))
                ctx.wait(A, "qs", ("xa", key, gi))
                ctx.wait(A, "dve", "stats")
                if gi >= 2:
                    ctx.wait(A, "pe", ("lt", key, gi - 2, 1))
                elif prev_lt_mark is not None:
                    ctx.wait(A, "pe", val=prev_lt_mark)
                ctx.act("activation", lt[gi % 2][:], xb[gi % 2][:], AF.Identity,
                        bias=sb["st_bias"][:, t:t + 1],
                        scale=sb["st_rstd"][:, t:t + 1])
                ctx.mark("act", ("lnc", key, gi))
                for rnd in range(2):
                    ctx.wait(T, "act", ("lnc", key, gi))
                    ctx.wait(T, "dve", "leprev")
                    last = None
                    for k in range(8):
                        fc = 8 * rnd + k
                        last = T.transpose(ps_t[:, 128 * k:128 * (k + 1)],
                                           lt[gi % 2][:, 128 * fc:128 * (fc + 1)],
                                           sb["identb"][:])
                    ctx.inc("pe", last)
                    ctx.mark("pe", ("lt", key, gi, rnd))
                    ctx.wait(V, "pe", ("lt", key, gi, rnd))
                    if tt == 0 and rnd == 0 and prev_mm_mark is not None:
                        ctx.wait(V, "pe", val=prev_mm_mark)
                    ctx.dve("tensor_copy",
                            ln1f_3d[:, 8 * rnd:8 * rnd + 8,
                                    128 * tt:128 * (tt + 1)],
                            pst_3d[:, :, :])
                    ctx.mark("dve", "leprev")
            ctx.mark("dve", ("lnf", key, blk))

        def gen_block_bcast(iota_name):
            if ("act", "tsb") in ctx.marks:
                ctx.wait(T, "act", "tsb")
            mm = T.matmul(ps_3[:, 0:512], sb["ones1f"][:],
                          sb[iota_name][0:1, :], start=True, stop=True)
            ctx.inc("pe", mm)
            ctx.mark("pe", "tbc")
            ctx.wait(A, "pe", "tbc")
            ctx.act("copy", tsb[:], ps_3[:, 0:512])
            ctx.mark("act", "tsb")

        def gen_tables(pr, tlo):
            ctx.wait(V, "act", "tsb")
            th, u, kk, y1 = tscr
            ctx.dve("tensor_scalar", th[:], tsb[:], float(tlo),
                    sb["invf"][:, pr:pr + 1], ALU.add, ALU.mult)
            ctx.dve("tensor_scalar", u[:], tsb[:], float(tlo),
                    sb["invf2"][:, pr:pr + 1], ALU.add, ALU.mult)
            ctx.dve("tensor_scalar", kk[:], u[:], MAGIC, MAGIC,
                    ALU.add, ALU.subtract)
            ctx.dve("scalar_tensor_tensor", y1[:], kk[:], -C1, th[:],
                    ALU.mult, ALU.add)
            ctx.dve("scalar_tensor_tensor", sarg[:], kk[:], -C2, y1[:],
                    ALU.mult, ALU.add)
            ctx.dve("tensor_scalar", kk[:], u[:], 0.25, None, ALU.add)
            ctx.dve("tensor_scalar", u[:], kk[:], MAGIC, MAGIC,
                    ALU.add, ALU.subtract)
            ctx.dve("scalar_tensor_tensor", y1[:], u[:], -C1, th[:],
                    ALU.mult, ALU.add)
            ctx.dve("scalar_tensor_tensor", kk[:], u[:], -C2, y1[:],
                    ALU.mult, ALU.add)
            ctx.dve("tensor_scalar", carg[:], kk[:], HALF_PI, None, ALU.add)
            ctx.mark("dve", "tblarg")
            ctx.wait(A, "dve", "tblarg")
            ctx.act("activation", tbls[:], sarg[:], AF.Sin)
            ctx.act("activation", tblc[:], carg[:], AF.Sin)
            ctx.mark("act", "tbl")

        def kq_pass(key, wname, xsrc, nblk, tile0, iota_name, store_q):
            load_weight(wname, key)
            for blk in range(nblk):
                if not store_q:
                    ctx.wait(V, "qg", "kspill_prev")
                pm = ctx.mval("pe", ("mm", key, blk - 1, 7)) if blk >= 1 else \
                    None
                pl_ = None
                if blk == 0 and ("pe", ("lt", prev_pass[key], 31, 1)) in \
                        ctx.marks:
                    pl_ = ctx.mval("pe", ("lt", prev_pass[key], 31, 1))
                ln_block(key, blk, tile0, xsrc, pm, pl_)
                ctx.wait(T, "dve", ("lnf", key, blk))
                if key == "k" and blk == 7 and "tap_lnf" in taps:
                    ctx.wait(SY, "dve", ("lnf", key, blk))
                    ctx.dma(SY, "st", taps["tap_lnf"], ln1f[:])
                gen_block_bcast(iota_name)
                for pr in range(8):
                    if key == "q" and blk == 0 and pr < 2:
                        ctx.wait(T, "act", ("vev", 7, 15))
                    gen_tables(pr, 512 * blk)
                    slot = (pr % 2) * 512
                    if pr >= 2:
                        ctx.wait(T, "act", ("kev", key, blk, pr - 2))
                    elif blk >= 1:
                        ctx.wait(T, "act", ("kev", key, blk - 1, pr + 6))
                    for half, ps in ((0, ps_1), (1, ps_2)):
                        dc = pr + 8 * half
                        last = None
                        for fc in range(16):
                            last = T.matmul(
                                ps[:, slot:slot + 512],
                                wslot[:, D * fc + 128 * dc:
                                      D * fc + 128 * (dc + 1)],
                                ln1f[:, 512 * fc:512 * (fc + 1)],
                                start=(fc == 0), stop=(fc == 15))
                        ctx.inc("pe", last)
                    ctx.mark("pe", ("mm", key, blk, pr))
                    ctx.wait(A, "pe", ("mm", key, blk, pr))
                    i, i8 = pr, pr + 8
                    if store_q:
                        d0 = q_dm[:, NT * i + 512 * blk:
                                  NT * i + 512 * blk + 512]
                        d1 = q_dm[:, NT * i8 + 512 * blk:
                                  NT * i8 + 512 * blk + 512]
                    else:
                        d0 = obuf[:, 512 * i:512 * (i + 1)]
                        d1 = obuf[:, 512 * i8:512 * (i8 + 1)]
                    ctx.act("copy", kei[:], ps_1[:, slot:slot + 512])
                    ctx.act("copy", kei8[:], ps_2[:, slot:slot + 512])
                    ctx.mark("act", ("kev", key, blk, pr))
                    ctx.wait(V, "act", ("kev", key, blk, pr))
                    r0, r1, r2, r3 = rscrb
                    ctx.dve("tensor_tensor", r0[:], kei[:], tblc[:], ALU.mult)
                    ctx.dve("tensor_tensor", r1[:], kei8[:], tbls[:], ALU.mult)
                    ctx.dve("tensor_tensor", d0, r0[:], r1[:], ALU.subtract)
                    ctx.dve("tensor_tensor", r2[:], kei8[:], tblc[:], ALU.mult)
                    ctx.dve("tensor_tensor", r3[:], kei[:], tbls[:], ALU.mult)
                    ctx.dve("tensor_tensor", d1, r2[:], r3[:], ALU.add)
                    ctx.mark("dve", ("rope", key, blk, pr))
                    if key == "k" and blk == 7 and pr == 0 and "tap_lnf" in taps:
                        ctx.wait(SY, "dve", "tbl")
                        ctx.dma(SY, "st", taps["tap_tblc"], tblc[:])
                        ctx.dma(SY, "st", taps["tap_tbls"], tbls[:])
                if not store_q:
                    ctx.mark("dve", ("kblk", blk))
                    ctx.gwait(gp, "dve", ("kblk", blk))
                    if blk == 7 and "tap_ob7" in taps:
                        gp.dma_start(out=taps["tap_ob7"],
                                     in_=obuf[:]).then_inc(ctx.sems["st"], 16)
                        ctx.cnt["st"] += 16
                    src = obuf[:].rearrange("p (dc t) -> p dc t", t=512)
                    with gp.If_eq(rg, 0):
                        gp.dma_start(
                            out=ksh3[:, :, 512 * blk:512 * (blk + 1)],
                            in_=src).then_inc(ctx.sems["qg"], 16)
                    with gp.Else():
                        gp.dma_start(
                            out=ksh3[:, :, KT + 512 * blk:KT + 512 * (blk + 1)],
                            in_=src).then_inc(ctx.sems["qg"], 16)
                    ctx.cnt["qg"] += 16
                    ctx.mark("qg", "kspill_prev")

        if PHASE_LIMIT >= 2:
            kq_pass("k", "wkT", ap["x_kv"], 8, 0, "iotak", False)

        if "tap_ksh" in taps and PHASE_LIMIT < 6:
            ctx.gwait(gp, "qg", "kspill_prev")
            gp.dma_start(out=taps["tap_ksh"], in_=ap["ksh"]).then_inc(
                ctx.sems["st"], 16)
            ctx.cnt["st"] += 16
            if PHASE_LIMIT >= 3:
                pass

        if PHASE_LIMIT >= 3:
          load_weight("wvT", "v")
          slots = [(ps_1, 0), (ps_1, 512), (ps_2, 0), (ps_2, 512)]
          for blk in range(8):
              ctx.wait(A, "qg", "kspill_prev")
              ctx.wait(A, "act", "vspill_guard")
              ln_block("v", blk, 0, ap["x_kv"])
              ctx.wait(T, "dve", ("lnf", "v", blk))
              it = 0
              for ssub in range(4):
                  for eb in range(4):
                      ps, off = slots[it % 4]
                      if it >= 4:
                          ctx.wait(T, "act", ("vev", blk, it - 4))
                      elif blk >= 1:
                          ctx.wait(T, "act", ("vev", blk - 1, 12 + it))
                      else:
                          ctx.wait(T, "dve", ("rope", "k", 7, 7))
                      last = None
                      for fc in range(16):
                          last = T.matmul(
                              ps[:, off:off + 512],
                              ln1f[:, 512 * fc + 128 * ssub:
                                   512 * fc + 128 * (ssub + 1)],
                              wslot[:, D * fc + 512 * eb:D * fc + 512 * (eb + 1)],
                              start=(fc == 0), stop=(fc == 15))
                      ctx.inc("pe", last)
                      ctx.mark("pe", ("vmm", blk, it))
                      ctx.wait(A, "pe", ("vmm", blk, it))
                      ctx.act("copy",
                              obuf[:, 2048 * ssub + 512 * eb:
                                   2048 * ssub + 512 * (eb + 1)],
                              ps[:, off:off + 512])
                      ctx.mark("act", ("vev", blk, it))
                      it += 1
              ctx.mark("act", ("vblk", blk))
              ctx.gwait(gp, "act", ("vblk", blk))
              trow0, trow1 = 512 * blk, KT + 512 * blk
              with gp.If_eq(rg, 0):
                  for cc in range(4):
                      src3 = obuf[:, 2048 * cc:2048 * (cc + 1)].rearrange(
                          "p (h e) -> p h e", e=HD)
                      r0 = trow0 + 128 * cc
                      gp.dma_start(
                          out=ap["vsh"][:, r0:r0 + 128, :].rearrange(
                              "h p e -> p h e"),
                          in_=src3).then_inc(ctx.sems["qg"], 16)
              with gp.Else():
                  for cc in range(4):
                      src3 = obuf[:, 2048 * cc:2048 * (cc + 1)].rearrange(
                          "p (h e) -> p h e", e=HD)
                      r1 = trow1 + 128 * cc
                      gp.dma_start(
                          out=ap["vsh"][:, r1:r1 + 128, :].rearrange(
                              "h p e -> p h e"),
                          in_=src3).then_inc(ctx.sems["qg"], 16)
              ctx.cnt["qg"] += 64
              ctx.mark("qg", ("vspill", blk))
              ctx.mark("qg", "kspill_prev")
          ctx.mark("qg", "vspill_done")

        if PHASE_LIMIT >= 4:
            kq_pass("q", "wqT", ap["x_own"], 2, NKT, "iotaq", True)
            ctx.mark("dve", "qdone")
            if "tap_q" in taps:
                ctx.wait(SY, "dve", "qdone")
                ctx.dma(SY, "st", taps["tap_q"], q_dm[:])

        sA1.close()
        if PHASE_LIMIT >= 5:
          v129 = alloc(sA, "v129", [128, 32 * 129 + 64], BF16)
          kfb = alloc(sA, "kfb", [128, 512], BF16)
          kfs = [alloc(sA, f"kfs{i}", [128, 512], F32) for i in range(2)]
          mstg = alloc(sA, "mstg", [128, 129], F32)
          v129_3 = v129[:, 0:4128].rearrange("p (c e) -> p c e", e=129) \
              if False else None

          krh = obuf
          ctx.mark("qg", "mspill_prev")
          for h in range(NH):
              ctx.gwait(gp, "dve", "qdone")
              ctx.gwait(gp, "qg", "vspill_done")
              if h >= 1:
                  ctx.gwait(gp, "pe", ("a2mm", h - 1))
              with gp.If_eq(rg, 0):
                  gp.dma_start(out=krh[:, 0:KT],
                               in_=ap["ksh"][128 * h:128 * (h + 1), 0:KT]
                               ).then_inc(ctx.sems["qg"], 16)
                  gp.dma_start(
                      out=v129[:, 0:4128].rearrange("p (c e) -> p c e", e=129)[:, 0:32, 0:128],
                      in_=ap["vsh"][h, 0:KT, :].rearrange("(c p) e -> p c e",
                                                          p=128)
                      ).then_inc(ctx.sems["qg"], 16)
              with gp.Else():
                  gp.dma_start(out=krh[:, 0:KT],
                               in_=ap["ksh"][128 * h:128 * (h + 1), KT:S]
                               ).then_inc(ctx.sems["qg"], 16)
                  gp.dma_start(
                      out=v129[:, 0:4128].rearrange("p (c e) -> p c e", e=129)[:, 0:32, 0:128],
                      in_=ap["vsh"][h, KT:S, :].rearrange("(c p) e -> p c e",
                                                          p=128)
                      ).then_inc(ctx.sems["qg"], 16)
              ctx.cnt["qg"] += 32
              ctx.mark("qg", ("a2kv", h))
              if h == 0:
                  ctx.wait(V, "qg", ("a2kv", h))
                  ctx.dve("memset",
                          v129[:, 0:4128].rearrange(
                              "p (c e) -> p c e", e=129)[:, 0:32, 128:129], 1.0)
                  ctx.mark("dve", "v129ones")
              for grp in range(8):
                  ctx.wait(T, "qg", ("a2kv", h))
                  ctx.wait(T, "dve", "leprev")
                  last = None
                  for g in range(4):
                      j = 4 * grp + g
                      last = T.transpose(ps_t[:, 128 * g:128 * (g + 1)],
                                         krh[:, 128 * j:128 * (j + 1)],
                                         sb["identb"][:])
                  ctx.inc("pe", last)
                  ctx.mark("pe", ("a2t", h, grp))
                  ctx.wait(V, "pe", ("a2t", h, grp))
                  ctx.dve("tensor_scalar", kfs[0][:], ps_t[:, 0:512], 0.0, None,
                          ALU.min)
                  ctx.mark("dve", ("a2m", h, grp))
                  ctx.wait(A, "dve", ("a2m", h, grp))
                  ctx.act("activation", kfs[1][:], kfs[0][:], AF.Exp)
                  ctx.mark("act", ("a2e", h, grp))
                  ctx.wait(V, "act", ("a2e", h, grp))
                  ctx.dve("scalar_tensor_tensor", kfb[:], ps_t[:, 0:512], 0.0,
                          kfs[1][:], ALU.max, ALU.add)
                  ctx.mark("dve", "leprev")
                  ctx.wait(T, "dve", "leprev")
                  ctx.wait(T, "dve", "v129ones")
                  last = None
                  for g in range(4):
                      j = 4 * grp + g
                      last = T.matmul(ps_3[:, 0:129],
                                      kfb[:, 128 * g:128 * (g + 1)],
                                      v129[:, 129 * j:129 * j + 129],
                                      start=(grp == 0 and g == 0),
                                      stop=(grp == 7 and g == 3))
                  ctx.inc("pe", last)
              ctx.mark("pe", ("a2mm", h))
              ctx.wait(A, "pe", ("a2mm", h))
              ctx.wait(A, "qg", "mspill_prev")
              ctx.act("copy", mstg[:], ps_3[:, 0:129])
              ctx.mark("act", ("a2ev", h))
              ctx.gwait(gp, "act", ("a2ev", h))
              with gp.If_eq(rg, 0):
                  gp.dma_start(out=ap["msh"][128 * h:128 * (h + 1), :],
                               in_=mstg[:]).then_inc(ctx.sems["qg"], 16)
              with gp.Else():
                  gp.dma_start(
                      out=ap["msh"][NH * 128 + 128 * h:NH * 128 + 128 * (h + 1), :],
                      in_=mstg[:]).then_inc(ctx.sems["qg"], 16)
              ctx.cnt["qg"] += 16
              ctx.mark("qg", "mspill_prev")

        if PHASE_LIMIT >= 6:
          with (gp.register("r_epoch") as r_epoch,
                gp.register("r_f") as r_f,
                gp.register("r_cont") as r_cont,
                gp.register("r_tgt") as r_tgt):
              gp.reg_load(r_epoch, sb["epoch_sb"][0:1, 0:1])
              with gp.If_eq(rg, 0):
                  gp.dma_start(out=ap["fsh"][0:1, 0:1],
                               in_=sb["epoch_sb"][0:1, 0:1]
                               ).then_inc(ctx.sems["qg"], 16)
              with gp.Else():
                  gp.dma_start(out=ap["fsh"][1:2, 0:1],
                               in_=sb["epoch_sb"][0:1, 0:1]
                               ).then_inc(ctx.sems["qg"], 16)
              ctx.cnt["qg"] += 16
              gp.wait_ge(ctx.sems["qg"], ctx.cnt["qg"])
              gp.reg_mov(r_cont, 1)
              gp.reg_mov(r_tgt, 16)
              with gp.While(r_cont):
                  with gp.If_eq(rg, 0):
                      gp.dma_start(out=sb["flag_sb"][0:1, 0:1],
                                   in_=ap["fsh"][1:2, 0:1]
                                   ).then_inc(ctx.sems["pl"], 16)
                  with gp.Else():
                      gp.dma_start(out=sb["flag_sb"][0:1, 0:1],
                                   in_=ap["fsh"][0:1, 0:1]
                                   ).then_inc(ctx.sems["pl"], 16)
                  gp.wait_ge(ctx.sems["pl"], r_tgt)
                  gp.reg_add(r_tgt, r_tgt, 16)
                  gp.reg_load(r_f, sb["flag_sb"][0:1, 0:1])
                  gp.reg_alu(r_cont, r_f, r_epoch, ALU.not_equal)

        if "tap_ksh" in taps:
            gp.dma_start(out=taps["tap_ksh"], in_=ap["ksh"]).then_inc(
                ctx.sems["st"], 16)
            gp.dma_start(out=taps["tap_vsh"],
                         in_=ap["vsh"].rearrange("h s e -> (h s) e")
                         ).then_inc(ctx.sems["st"], 16)
            ctx.cnt["st"] += 32
        if "tap_mem" in taps:
            gp.dma_start(out=taps["tap_mem"], in_=ap["msh"]).then_inc(
                ctx.sems["st"], 16)
            ctx.cnt["st"] += 16

    sBC = ExitStack()
    if PHASE_LIMIT >= 7:
      comb = alloc(sBC, "comb", [128, NH * NT], BF16, side="right")
      with ExitStack() as sB:
          ps_s0 = sB.enter_context(nc.psum_tensor([128, 1024], F32))
          ps_s1 = sB.enter_context(nc.psum_tensor([128, 1024], F32))
          ps_pv = sB.enter_context(nc.psum_tensor([128, 1024], F32))
          ps_den = sB.enter_context(nc.psum_tensor([1, 1024], F32))
          ps_s = [ps_s0, ps_s1]

          kh = [alloc(sB, f"kh{i}", [128, S], BF16) for i in range(2)]
          vt = [alloc(sB, f"vt{i}", [128, S], BF16) for i in range(2)]
          pT = [alloc(sB, f"pT{i}", [128, 1024], BF16) for i in range(2)]
          qfb = alloc(sB, "qfb", [128, 1024], BF16)
          qfs = [alloc(sB, f"qfs{i}", [128, 1024], F32) for i in range(2)]
          mem_sb = alloc(sB, "mem_sb", [128, 1024], F32)
          mp = alloc(sB, "mp", [128, 258], F32)
          mn = alloc(sB, "mn", [128, 129], BF16)
          Bsb = alloc(sB, "Bsb", [128, 1024], BF16)
          Blb = alloc(sB, "Blb", [128, 1024], BF16)
          u1 = alloc(sB, "u1", [128, 1024], F32)
          u2 = alloc(sB, "u2", [128, 1024], F32)
          rlr = alloc(sB, "rlr", [1, 1024], F32)
          rsr = alloc(sB, "rsr", [1, 1024], F32)
          rsb = alloc(sB, "rsb", [1, 1024], BF16)
          rlb = alloc(sB, "rlb", [1, 1024], BF16)

          for h in range(NH):
              if h >= 2:
                  ctx.gwait(gp, "pe", ("S", h - 2, 63))
                  ctx.gwait(gp, "pe", ("PV", h - 2, 63))
              if h >= 1:
                  ctx.gwait(gp, "dve", ("mn", h - 1))
              gp.dma_start(out=kh[h % 2][:],
                           in_=ap["ksh"][128 * h:128 * (h + 1), :]
                           ).then_inc(ctx.sems["qg"], 16)
              gp.dma_start(out=vt[h % 2][:].rearrange("p (c e) -> p c e", e=HD),
                           in_=ap["vsh"][h, :, :].rearrange("(c p) e -> p c e",
                                                            p=128)
                           ).then_inc(ctx.sems["qg"], 16)
              gp.dma_start(out=mp[:, 0:129],
                           in_=ap["msh"][128 * h:128 * (h + 1), :]
                           ).then_inc(ctx.sems["qg"], 16)
              gp.dma_start(out=mp[:, 129:258],
                           in_=ap["msh"][NH * 128 + 128 * h:
                                         NH * 128 + 128 * (h + 1), :]
                           ).then_inc(ctx.sems["qg"], 16)
              ctx.cnt["qg"] += 64
              ctx.mark("qg", ("bkv", h))

              ctx.wait(V, "qg", ("bkv", h))
              ctx.dve("tensor_tensor", mn[:], mp[:, 0:129], mp[:, 129:258],
                      ALU.add)
              ctx.mark("dve", ("mn", h))
              if h >= 1:
                  ctx.wait(V, "pe", ("bdl", h - 1))
              ctx.dve("tensor_scalar", qfs[0][:], q_dm[:, NT * h:NT * (h + 1)],
                      0.0, None, ALU.min)
              ctx.mark("dve", ("bqf0", h))
              ctx.wait(A, "dve", ("bqf0", h))
              ctx.act("activation", qfs[1][:], qfs[0][:], AF.Exp)
              ctx.mark("act", ("bqf1", h))
              ctx.wait(V, "act", ("bqf1", h))
              ctx.dve("scalar_tensor_tensor", qfb[:],
                      q_dm[:, NT * h:NT * (h + 1)], 0.0, qfs[1][:],
                      ALU.max, ALU.add)
              ctx.mark("dve", ("bqf", h))

              if h >= 1:
                  ctx.wait(T, "dve", ("comb", h - 1))
                  ctx.wait(T, "dve", ("rs", h - 1))
              ctx.wait(T, "dve", ("bqf", h))
              T.matmul(ps_pv[:, 0:512], mn[:, 0:128], qfb[:, 0:512],
                       start=True, stop=True)
              last = T.matmul(ps_pv[:, 512:1024], mn[:, 0:128], qfb[:, 512:1024],
                              start=True, stop=True)
              ctx.inc("pe", last)
              ctx.mark("pe", ("bmo", h))
              T.matmul(ps_den[0:1, 0:512], mn[:, 128:129], qfb[:, 0:512],
                       start=True, stop=True)
              last = T.matmul(ps_den[0:1, 512:1024], mn[:, 128:129],
                              qfb[:, 512:1024], start=True, stop=True)
              ctx.inc("pe", last)
              ctx.mark("pe", ("bdl", h))
              ctx.wait(A, "pe", ("bmo", h))
              if h >= 1:
                  ctx.wait(A, "dve", ("comb", h - 1))
              ctx.act("copy", mem_sb[:], ps_pv[:])
              ctx.mark("act", ("bmev", h))
              ctx.wait(V, "pe", ("bdl", h))
              ctx.dve("reciprocal", rlr[:], ps_den[0:1, :])
              ctx.mark("dve", ("brl", h))

              def emit_pv(j):
                  ctx.wait(T, "act", ("exp", h, j))
                  if j == 0:
                      ctx.wait(T, "act", ("bmev", h))
                      ctx.wait(T, "dve", ("brl", h))
                  T.matmul(ps_pv[:, 0:512], vt[h % 2][:, 128 * j:128 * (j + 1)],
                           pT[j % 2][:, 0:512], start=(j == 0), stop=(j == 63))
                  T.matmul(ps_pv[:, 512:1024],
                           vt[h % 2][:, 128 * j:128 * (j + 1)],
                           pT[j % 2][:, 512:1024], start=(j == 0), stop=(j == 63))
                  T.matmul(ps_den[0:1, 0:512], sb["ones_col"][:],
                           pT[j % 2][:, 0:512], start=(j == 0), stop=(j == 63))
                  last = T.matmul(ps_den[0:1, 512:1024], sb["ones_col"][:],
                                  pT[j % 2][:, 512:1024], start=(j == 0),
                                  stop=(j == 63))
                  ctx.inc("pe", last)
                  ctx.mark("pe", ("PV", h, j))

              for j in range(64):
                  if j >= 2:
                      ctx.wait(T, "act", ("exp", h, j - 2))
                  elif h >= 1:
                      ctx.wait(T, "act", ("bce", h - 1))
                  T.matmul(ps_s[j % 2][:, 0:512],
                           kh[h % 2][:, 128 * j:128 * (j + 1)],
                           q_dm[:, NT * h:NT * h + 512], start=True, stop=True)
                  last = T.matmul(ps_s[j % 2][:, 512:1024],
                                  kh[h % 2][:, 128 * j:128 * (j + 1)],
                                  q_dm[:, NT * h + 512:NT * h + 1024],
                                  start=True, stop=True)
                  ctx.inc("pe", last)
                  ctx.mark("pe", ("S", h, j))
                  ctx.wait(A, "pe", ("S", h, j))
                  ctx.act("activation", pT[j % 2][:], ps_s[j % 2][:], AF.Exp,
                          scale=SCALE, bias=EXP_BIAS)
                  ctx.mark("act", ("exp", h, j))
                  if j >= 1:
                      emit_pv(j - 1)
              emit_pv(63)

              ctx.wait(V, "pe", ("PV", h, 63))
              ctx.dve("reciprocal", rsr[:], ps_den[0:1, :])
              ctx.mark("dve", ("rs", h))
              ctx.dve("tensor_scalar", rsb[:], rsr[:],
                      sb["gate_1mg"][0:1, h:h + 1], None, ALU.mult)
              ctx.dve("tensor_scalar", rlb[:], rlr[:],
                      sb["gate_g"][0:1, h:h + 1], None, ALU.mult)
              ctx.mark("dve", ("brows", h))
              ctx.wait(T, "act", ("exp", h, 63))
              ctx.wait(T, "dve", ("brows", h))
              T.matmul(ps_s0[:, 0:512], sb["ones_row"][:], rsb[0:1, 0:512],
                       start=True, stop=True)
              T.matmul(ps_s0[:, 512:1024], sb["ones_row"][:], rsb[0:1, 512:1024],
                       start=True, stop=True)
              T.matmul(ps_s1[:, 0:512], sb["ones_row"][:], rlb[0:1, 0:512],
                       start=True, stop=True)
              last = T.matmul(ps_s1[:, 512:1024], sb["ones_row"][:],
                              rlb[0:1, 512:1024], start=True, stop=True)
              ctx.inc("pe", last)
              ctx.mark("pe", ("bc", h))
              ctx.wait(A, "pe", ("bc", h))
              ctx.act("copy", Bsb[:], ps_s0[:])
              ctx.act("copy", Blb[:], ps_s1[:])
              ctx.mark("act", ("bce", h))
              ctx.wait(V, "act", ("bce", h))
              ctx.dve("tensor_tensor", u1[:], ps_pv[:], Bsb[:], ALU.mult)
              ctx.dve("tensor_tensor", u2[:], mem_sb[:], Blb[:], ALU.mult)
              ctx.dve("tensor_tensor", comb[:, NT * h:NT * (h + 1)], u1[:],
                      u2[:], ALU.add)
              ctx.mark("dve", ("comb", h))

          ctx.mark("dve", "combdone")
          if "tap_comb" in taps:
              ctx.wait(SY, "dve", "combdone")
              ctx.dma(SY, "st", taps["tap_comb"], comb[:])

    sAB.close()

    ln2f = alloc(root, "ln2f", [128, 16 * NT], BF16)
    if PHASE_LIMIT >= 8:
      with ExitStack() as sC:
          cpa = sC.enter_context(nc.psum_tensor([128, 1024], F32))
          cpb = sC.enter_context(nc.psum_tensor([128, 1024], F32))
          ps_t = sC.enter_context(nc.psum_tensor([128, 1024], BF16))

          wo_bf = alloc(sC, "wo_bf", [128, 16 * D], BF16)
          xot = [alloc(sC, f"xot{i}", [128, D], F32) for i in range(2)]
          tC = alloc(sC, "tC", [128, D], F32)
          x1t = alloc(sC, "x1t", [128, D], F32)
          l2t = alloc(sC, "l2t", [128, D], BF16)
          st4 = alloc(sC, "st4", [128, 16], F32)
          sq4 = alloc(sC, "sq4", [128, 16], F32)

          ctx.wait(SY, "dve", "combdone")
          for i in range(16):
              if i >= 2:
                  ctx.wait(SY, "dve", ("cwo", i - 2))
              ctx.dma(SY, "qs", xot[i % 2][:], ap["woT"][128 * i:128 * (i + 1), :])
              ctx.mark("qs", ("cwl", i))
              ctx.wait(V, "qs", ("cwl", i))
              ctx.dve("tensor_copy", wo_bf[:, D * i:D * (i + 1)], xot[i % 2][:])
              ctx.mark("dve", ("cwo", i))
          ctx.mark("dve", "cwodone")

          psl = [cpa[:, 0:512], cpa[:, 512:1024], cpb[:, 0:512], cpb[:, 512:1024]]
          ln2f_3d = ln2f[:].rearrange("p (fc t) -> p fc t", t=NT)
          pst_3d = ps_t[:].rearrange("p (k e) -> p k e", e=128)
          ctx.mark("dve", "l2ev_prev")

          for j in range(8):
              ctx.wait(SY, "dve", "cwodone")
              if j >= 2:
                  ctx.wait(SY, "dve", ("cx1", j - 2))
              ctx.dma(SY, "qs", xot[j % 2][:],
                      ap["x_own"][128 * j:128 * (j + 1), :])
              ctx.mark("qs", ("cx", j))
              ctx.wait(T, "dve", "cwodone")
              if j >= 1:
                  ctx.wait(T, "act", ("cwoev", j - 1))
              last = None
              for ic in range(16):
                  for db in range(4):
                      last = T.matmul(
                          psl[db],
                          comb[:, NT * ic + 128 * j:NT * ic + 128 * (j + 1)],
                          wo_bf[:, D * ic + 512 * db:D * ic + 512 * (db + 1)],
                          start=(ic == 0), stop=(ic == 15))
              ctx.inc("pe", last)
              ctx.mark("pe", ("cwo", j))
              ctx.wait(V, "pe", ("cwo", j))
              ctx.wait(A, "pe", ("cwo", j))
              if CDBG == 1:
                  ctx.dve("memset", st4[:, 0:4], 0.0)
                  ctx.dve("memset", sq4[:, 0:4], float(D))
              elif CDBG == 2:
                  for db in range(4):
                      ctx.dve("reduce_sum", st4[:, db:db + 1], psl[db], AX.X)
                  ctx.dve("memset", sq4[:, 0:4], float(D))
              elif CDBG == 3:
                  ctx.dve("memset", st4[:, 0:4], 0.0)
                  for db in range(4):
                      ctx.act("activation", sb["junk"][:, 0:512], psl[db],
                              AF.Square, accum_out=sq4[:, db:db + 1])
              else:
                  for db in range(4):
                      ctx.dve("reduce_sum", st4[:, db:db + 1], psl[db], AX.X)
                  ctx.mark("dve", ("cred", j))
                  ctx.wait(A, "dve", ("cred", j))
                  for db in range(4):
                      ctx.act("activation", sb["junk"][:, 0:512], psl[db],
                              AF.Square, accum_out=sq4[:, db:db + 1])
              ctx.mark("act", ("csq", j))
              ctx.dve("reduce_sum", st4[:, 8:9], st4[:, 0:4], AX.X)
              ctx.wait(V, "act", ("csq", j))
              ctx.dve("reduce_sum", sq4[:, 8:9], sq4[:, 0:4], AX.X)
              ctx.dve("tensor_scalar", st4[:, 9:10], st4[:, 8:9], 1.0 / D, None,
                      ALU.mult)
              ctx.dve("tensor_scalar", sq4[:, 9:10], sq4[:, 8:9], 1.0 / D, None,
                      ALU.mult)
              ctx.dve("tensor_tensor", st4[:, 10:11], st4[:, 9:10], st4[:, 9:10],
                      ALU.mult)
              ctx.dve("tensor_tensor", sq4[:, 10:11], sq4[:, 9:10],
                      st4[:, 10:11], ALU.subtract)
              ctx.dve("tensor_scalar", sq4[:, 11:12], sq4[:, 10:11], LN_EPS,
                      None, ALU.add)
              ctx.dve("reciprocal", sq4[:, 12:13], sq4[:, 11:12])
              ctx.mark("dve", ("crv", j))
              ctx.wait(A, "dve", ("crv", j))
              ctx.act("activation", st4[:, 11:12], sq4[:, 12:13], AF.Sqrt)
              ctx.mark("act", ("crstd", j))
              ctx.wait(V, "act", ("crstd", j))
              ctx.dve("scalar_tensor_tensor", st4[:, 12:13], st4[:, 9:10], -1.0,
                      st4[:, 11:12], ALU.mult, ALU.mult)
              ctx.mark("dve", ("cbias", j))
              ctx.wait(A, "dve", ("cbias", j))
              for db in range(4):
                  ctx.act("activation", tC[:, 512 * db:512 * (db + 1)], psl[db],
                          AF.Identity, scale=st4[:, 11:12], bias=st4[:, 12:13])
              ctx.mark("act", ("cwoev", j))
              ctx.wait(V, "act", ("cwoev", j))
              ctx.wait(V, "qs", ("cx", j))
              if j >= 1:
                  ctx.wait(V, "qs", ("cx1s", j - 1))
              ctx.dve("tensor_tensor", x1t[:], tC[:], xot[j % 2][:], ALU.add)
              ctx.mark("dve", ("cx1", j))
              ctx.wait(SY, "dve", ("cx1", j))
              ctx.dma(SY, "qs", ap["x1d"][128 * j:128 * (j + 1), :], x1t[:])
              if "tap_x1" in taps:
                  ctx.dma(SY, "st", taps["tap_x1"][128 * j:128 * (j + 1), :],
                          x1t[:])
              ctx.mark("qs", ("cx1s", j))
              ctx.dve("reduce_sum", st4[:, 13:14], x1t[:], AX.X)
              ctx.wait(A, "dve", ("cx1", j))
              ctx.act("activation", sb["junk"][:], x1t[:], AF.Square,
                      accum_out=sq4[:, 13:14])
              ctx.mark("act", ("l2sq", j))
              ctx.dve("tensor_scalar", st4[:, 9:10], st4[:, 13:14], 1.0 / D,
                      None, ALU.mult)
              ctx.wait(V, "act", ("l2sq", j))
              ctx.dve("tensor_scalar", sq4[:, 9:10], sq4[:, 13:14], 1.0 / D,
                      None, ALU.mult)
              ctx.dve("tensor_tensor", st4[:, 10:11], st4[:, 9:10], st4[:, 9:10],
                      ALU.mult)
              ctx.dve("tensor_tensor", sq4[:, 10:11], sq4[:, 9:10],
                      st4[:, 10:11], ALU.subtract)
              ctx.dve("tensor_scalar", sq4[:, 11:12], sq4[:, 10:11], LN_EPS,
                      None, ALU.add)
              ctx.dve("reciprocal", sq4[:, 12:13], sq4[:, 11:12])
              ctx.mark("dve", ("l2rv", j))
              ctx.wait(A, "dve", ("l2rv", j))
              ctx.act("activation", st4[:, 11:12], sq4[:, 12:13], AF.Sqrt)
              ctx.mark("act", ("l2rstd", j))
              ctx.wait(V, "act", ("l2rstd", j))
              ctx.dve("scalar_tensor_tensor", st4[:, 12:13], st4[:, 9:10], -1.0,
                      st4[:, 11:12], ALU.mult, ALU.mult)
              ctx.mark("dve", ("l2bias", j))
              ctx.wait(A, "dve", ("l2bias", j))
              ctx.act("activation", l2t[:], x1t[:], AF.Identity,
                      scale=st4[:, 11:12], bias=st4[:, 12:13])
              ctx.mark("act", ("l2t", j))
              for rnd in range(2):
                  ctx.wait(T, "act", ("l2t", j))
                  ctx.wait(T, "dve", "l2ev_prev")
                  last = None
                  for k in range(8):
                      fc = 8 * rnd + k
                      last = T.transpose(ps_t[:, 128 * k:128 * (k + 1)],
                                         l2t[:, 128 * fc:128 * (fc + 1)],
                                         sb["identb"][:])
                  ctx.inc("pe", last)
                  ctx.mark("pe", ("l2T", j, rnd))
                  ctx.wait(V, "pe", ("l2T", j, rnd))
                  ctx.dve("tensor_copy",
                          ln2f_3d[:, 8 * rnd:8 * rnd + 8, 128 * j:128 * (j + 1)],
                          pst_3d[:, :, :])
                  ctx.mark("dve", "l2ev_prev")
          ctx.mark("dve", "ln2done")
          if "tap_ln2" in taps:
              ctx.wait(SY, "dve", "ln2done")
              ctx.dma(SY, "st", taps["tap_ln2"], ln2f[:])

    sBC.close()

    if PHASE_LIMIT >= 9:
      with ExitStack() as sD:
          dga = sD.enter_context(nc.psum_tensor([128, 1024], F32))
          dgb = sD.enter_context(nc.psum_tensor([128, 1024], F32))
          ps_t = [sD.enter_context(nc.psum_tensor(f"ps_t{i}", [128, 1024],
                                                  BF16))
                  for i in range(2)]
          dse = sD.enter_context(nc.psum_tensor([128, 512], F32))

          act_q = alloc(sD, "act_q", [128, 16 * NT], BF16)
          out_acc = [alloc(sD, f"out_acc{i}", [128, 16 * NT], BF16)
                     for i in range(2)]
          wgst = [alloc(sD, f"wgst{i}", [128, D], F32) for i in range(2)]
          wqh = [alloc(sD, f"wqh{i}", [128, D], BF16) for i in range(2)]
          qwb = [alloc(sD, f"qwb{i}", [128, D], BF16) for i in range(2)]
          qtb = alloc(sD, "qtb", [128, D // 2], BF16)
          qpb = alloc(sD, "qpb", [128, D // 2], BF16)
          qsc = alloc(sD, "qsc", [128, 32], F32)
          qscb = alloc(sD, "qscb", [128, 32], BF16)
          qscn = alloc(sD, "qscn", [128, 32], BF16)
          qrec = alloc(sD, "qrec", [128, 32], F32)
          qrf = alloc(sD, "qrf", [128, 32], F32)
          wfm = [alloc(sD, f"wfm{i}", [128, 16 * 128], BF16) for i in range(2)]
          wpTb = alloc(sD, "wpTb", [128, 16 * 128], BF16)
          sgt = alloc(sD, "sgt", [128, 1024], F32)
          sqb = alloc(sD, "sqb", [128, 1024], BF16)
          invb = alloc(sD, "invb", [128, 16], F32)
          ftm = alloc(sD, "ftm", [128, D], BF16, side="right")
          ftf = alloc(sD, "ftf", [128, D], F32, side="right")

          ctx.mark("dve", "dq_prev")
          ctx.mark("act", "t16_prev")
          ctx.mark("pe", "t16pe_prev")

          def quant(key, ci):
              nG = 32
              qw = qwb[ci % 2]
              w3i = qw[:, :].rearrange("p (g k) -> p g k", k=64)
              ctx.wait(V, "act", ("qcast", ci))
              ctx.dve("tensor_reduce", qrf[:, 0:nG], w3i, AX.X, ALU.add,
                      apply_absolute_value=True)
              ctx.dve("tensor_scalar", qsc[:, 0:nG], qrf[:, 0:nG], 1.0 / 64,
                      1e-8, ALU.mult, ALU.max)
              ctx.dve("tensor_copy", qscb[:, 0:nG], qsc[:, 0:nG])
              ctx.dve("tensor_scalar", qscn[:, 0:nG], qscb[:, 0:nG], -1.0, None,
                      ALU.mult)
              ctx.dve("tensor_copy", qsc[:, 0:nG], qscb[:, 0:nG])
              ctx.dve("reciprocal", qrec[:, 0:nG], qsc[:, 0:nG])
              for hv in range(2):
                  gsl = slice(16 * hv, 16 * hv + 16)
                  csl = slice(1024 * hv, 1024 * hv + 1024)
                  ctx.dve("tensor_tensor",
                          qtb[:, :].rearrange("p (g k) -> p g k", k=64),
                          w3i[:, gsl, :],
                          qrec[:, gsl].unsqueeze(2).broadcast_to([128, 16, 64]),
                          ALU.mult)
                  ctx.dve("tensor_scalar", qpb[:], qtb[:], 0.5, None, ALU.is_gt)
                  ctx.dve("scalar_tensor_tensor", qw[:, csl], qtb[:], -0.5,
                          qpb[:], ALU.is_lt, ALU.subtract)
              if ci >= 2:
                  ctx.wait(V, "pe", ("t16L", ci - 2))
              ctx.dve("tensor_tensor",
                      wqh[ci % 2][:, :].rearrange("p (g k) -> p g k", k=64),
                      qw[:, :].rearrange("p (g k) -> p g k", k=64),
                      qscn[:, 0:nG].unsqueeze(2).broadcast_to([128, nG, 64]),
                      ALU.mult)
              ctx.mark("dve", ("quant", key))
              ctx.mark("dve", "dq_prev")

          def transpose16(key, ci, dst, prev_reader_mark):
              for rnd in range(2):
                  pst = ps_t[rnd]
                  ctx.wait(T, "dve", ("quant", key))
                  if ci >= 1:
                      ctx.wait(T, "act", ("t16ev", ci - 1, rnd))
                  else:
                      ctx.wait(T, "act", "t16_prev")
                  last = None
                  for k in range(8):
                      fc = 8 * rnd + k
                      last = T.transpose(pst[:, 128 * k:128 * (k + 1)],
                                         wqh[ci % 2][:, 128 * fc:128 * (fc + 1)],
                                         sb["identb"][:])
                  ctx.inc("pe", last)
                  ctx.mark("pe", ("t16", key, rnd))
                  ctx.mark("pe", "t16pe_prev")
                  if rnd == 1:
                      ctx.mark("pe", ("t16L", ci))
                  ctx.wait(A, "pe", ("t16", key, rnd))
                  if prev_reader_mark is not None:
                      ctx.wait(A, "pe", val=prev_reader_mark)
                  ctx.act("activation", dst[:, 1024 * rnd:1024 * (rnd + 1)],
                          pst[:], AF.Identity)
                  ctx.mark("act", ("t16ev", ci, rnd))
                  ctx.mark("act", "t16_prev")
              ctx.mark("act", ("t16d", key))

          psl_d = {0: dga, 1: dgb}
          chunk_srcs = []
          for q in range(NQ):
              for hp in range(16):
                  for wi, row0 in ((0, HQ * q + 128 * hp),
                                   (1, FH + HQ * q + 128 * hp)):
                      chunk_srcs.append((("gu", q, hp, wi),
                                         ap["wgu"][row0:row0 + 128, :]))
              for dc in range(16):
                  chunk_srcs.append(
                      (("pj", q, dc),
                       ap["wp"][128 * dc:128 * (dc + 1),
                                HQ * q:HQ * (q + 1)]))
          NCH = len(chunk_srcs)
          pumped = [0]

          def pump(n):
              while pumped[0] < min(n, NCH):
                  ci = pumped[0]
                  key, wsrc = chunk_srcs[ci]
                  if ci >= 2:
                      ctx.wait(SY, "act", ("qcast", ci - 2))
                  else:
                      ctx.wait(SY, "dve", "dq_prev")
                      ctx.wait(SY, "pe", "t16pe_prev")
                      ctx.wait(SY, "act", "t16_prev")
                  ctx.dma(SY, "qs", wgst[ci % 2][:], wsrc)
                  ctx.mark("qs", ("dwl", key))
                  ctx.wait(A, "qs", ("dwl", key))
                  if ci >= 2:
                      ctx.wait(A, "dve", ("quant", chunk_srcs[ci - 2][0]))
                  ctx.act("activation", qwb[ci % 2][:], wgst[ci % 2][:],
                          AF.Identity)
                  ctx.mark("act", ("qcast", ci))
                  pumped[0] += 1

          for q in range(NQ):
              for hp in range(16):
                  for wi, row0 in ((0, HQ * q + 128 * hp),
                                   (1, FH + HQ * q + 128 * hp)):
                      key = ("gu", q, hp, wi)
                      ci = q * 48 + hp * 2 + wi
                      pump(ci + 2)
                      quant(key, ci)
                      pump(ci + 3)
                      prev = None
                      if (q, hp) != (0, 0):
                          prevkey = ("dgu", q, hp - 1, wi) if hp >= 1 else \
                              ("dgu", q - 1, 15, wi)
                          prev = ctx.mval("pe", prevkey)
                      transpose16(key, ci, wfm[wi], prev)
                      ps = psl_d[wi]
                      ctx.wait(T, "act", ("t16d", key))
                      ctx.wait(T, "dve", "ln2done")
                      if (q, hp) != (0, 0):
                          ctx.wait(T, "dve", ("dact", q, hp - 1) if hp >= 1
                                   else ("dact", q - 1, 15))
                      if hp == 0 and q >= 1:
                          ctx.wait(T, "dve", ("dpe", q - 1, 15))
                      last = None
                      for fc in range(16):
                          last = T.matmul(ps[:, 0:512],
                                          wfm[wi][:, 128 * fc:128 * (fc + 1)],
                                          ln2f[:, NT * fc:NT * fc + 512],
                                          start=(fc == 0), stop=(fc == 15))
                          last = T.matmul(ps[:, 512:1024],
                                          wfm[wi][:, 128 * fc:128 * (fc + 1)],
                                          ln2f[:, NT * fc + 512:NT * fc + 1024],
                                          start=(fc == 0), stop=(fc == 15))
                      ctx.inc("pe", last)
                      ctx.mark("pe", ("dgu", q, hp, wi))
                  ctx.wait(A, "pe", ("dgu", q, hp, 0))
                  ctx.act("activation", sgt[:], dga[:], AF.Silu)
                  ctx.mark("act", ("dsilu", q, hp))
                  ctx.wait(V, "act", ("dsilu", q, hp))
                  ctx.wait(V, "pe", ("dgu", q, hp, 1))
                  ctx.dve("tensor_tensor", act_q[:, NT * hp:NT * (hp + 1)],
                          dgb[:], sgt[:], ALU.mult)
                  ctx.mark("dve", ("dact", q, hp))
                  ctx.wait(A, "dve", ("dact", q, hp))
                  ctx.act("activation", sqb[:], act_q[:, NT * hp:NT * (hp + 1)],
                          AF.Square)
                  ctx.mark("act", ("dsq", q, hp))
                  ctx.wait(T, "act", ("dsq", q, hp))
                  last = None
                  for jj in range(8):
                      last = T.matmul(dse[:, jj:jj + 1],
                                      sqb[:, 128 * jj:128 * (jj + 1)],
                                      sb["ones_col"][:],
                                      start=(q == 0 and hp == 0 and jj == 0),
                                      stop=(q == NQ - 1 and hp == 15))
                  ctx.inc("pe", last)
                  ctx.mark("pe", ("dssq", q, hp))

              for dc in range(16):
                  key = ("pj", q, dc)
                  ci = q * 48 + 32 + dc
                  pump(ci + 2)
                  quant(key, ci)
                  pump(ci + 3)
                  prev = ctx.mval("pe", ("dpj", q, dc - 1)) if dc >= 1 else \
                      (ctx.mval("pe", ("dpj", q - 1, 15)) if q >= 1 else None)
                  transpose16(key, ci, wpTb, prev)
                  ps = psl_d[dc % 2]
                  ctx.wait(T, "act", ("t16d", key))
                  if dc < 2:
                      ctx.wait(T, "dve", ("dact", q, 15))
                  if dc >= 2:
                      ctx.wait(T, "dve", ("dpe", q, dc - 2))
                  elif q >= 1:
                      ctx.wait(T, "dve", ("dpe", q - 1, 14 + dc))
                  last = None
                  for hc in range(16):
                      last = T.matmul(ps[:, 0:512],
                                      wpTb[:, 128 * hc:128 * (hc + 1)],
                                      act_q[:, NT * hc:NT * hc + 512],
                                      start=(hc == 0), stop=(hc == 15))
                      last = T.matmul(ps[:, 512:1024],
                                      wpTb[:, 128 * hc:128 * (hc + 1)],
                                      act_q[:, NT * hc + 512:NT * hc + 1024],
                                      start=(hc == 0), stop=(hc == 15))
                  ctx.inc("pe", last)
                  ctx.mark("pe", ("dpj", q, dc))
                  ctx.wait(V, "pe", ("dpj", q, dc))
                  if q == 0:
                      ctx.dve("tensor_copy",
                              out_acc[0][:, NT * dc:NT * (dc + 1)], ps[:])
                  else:
                      ctx.dve("tensor_tensor",
                              out_acc[q % 2][:, NT * dc:NT * (dc + 1)], ps[:],
                              out_acc[(q - 1) % 2][:, NT * dc:NT * (dc + 1)],
                              ALU.add)
                  ctx.mark("dve", ("dpe", q, dc))

          ctx.wait(V, "pe", ("dssq", NQ - 1, 15))
          ctx.dve("tensor_scalar", invb[:, 0:8], dse[:, 0:8], 1.0 / FH, RMS_EPS,
                  ALU.mult, ALU.add)
          ctx.dve("reciprocal", invb[:, 8:16], invb[:, 0:8])
          ctx.mark("dve", "drv")
          ctx.wait(A, "dve", "drv")
          ctx.act("activation", invb[:, 0:8], invb[:, 8:16], AF.Sqrt)
          ctx.mark("act", "dinv")

          if "tap_ffn" in taps:
              ctx.wait(SY, "dve", ("dpe", NQ - 1, 15))
              ctx.dma(SY, "st", taps["tap_ffn"], out_acc[(NQ - 1) % 2][:])

          ftm_3d = ftm[:].rearrange("p (k e) -> p k e", e=128)
          ctx.mark("dve", "et16_prev")
          fln = wgst[0]
          x1t2 = wgst[1]
          last_ci = (NQ - 1) * 48 + 32 + 15
          for j in range(8):
              if j >= 1:
                  ctx.wait(SY, "dve", ("eout", j - 1))
              else:
                  ctx.wait(SY, "act", ("qcast", last_ci))
              ctx.dma(SY, "qs", x1t2[:], ap["x1d"][128 * j:128 * (j + 1), :])
              ctx.mark("qs", ("ex1", j))
              for rnd in range(2):
                  ctx.wait(T, "dve", ("dpe", NQ - 1, 15))
                  ctx.wait(T, "act", "t16_prev")
                  ctx.wait(T, "dve", "et16_prev")
                  last = None
                  for k in range(8):
                      dc = 8 * rnd + k
                      last = T.transpose(
                          ps_t[rnd][:, 128 * k:128 * (k + 1)],
                          out_acc[(NQ - 1) % 2][
                              :, NT * dc + 128 * j:NT * dc + 128 * (j + 1)],
                          sb["identb"][:])
                  ctx.inc("pe", last)
                  ctx.mark("pe", ("eT", j, rnd))
                  ctx.wait(V, "pe", ("eT", j, rnd))
                  if j >= 1 and rnd == 0:
                      ctx.wait(V, "act", ("eln", j - 1))
                  ctx.dve("tensor_copy", ftm_3d[:, 8 * rnd:8 * rnd + 8, :],
                          ps_t[rnd][:].rearrange("p (k e) -> p k e", e=128))
                  ctx.mark("dve", "et16_prev")
              ctx.wait(V, "act", "dinv")
              if j >= 1:
                  ctx.wait(V, "st", ("y", j - 1))
              ctx.dve("tensor_scalar", ftf[:], ftm[:], invb[:, j:j + 1], None,
                      ALU.mult)
              ctx.mark("dve", ("eff", j))
              ctx.dve("reduce_sum", sb["sc_a"][:, 0:1], ftf[:], AX.X)
              ctx.wait(A, "dve", ("eff", j))
              ctx.act("activation", sb["junk"][:], ftf[:], AF.Square,
                      accum_out=sb["sc_a"][:, 1:2])
              ctx.mark("act", ("esq", j))
              ctx.dve("tensor_scalar", sb["sc_a"][:, 2:3], sb["sc_a"][:, 0:1],
                      1.0 / D, None, ALU.mult)
              ctx.wait(V, "act", ("esq", j))
              ctx.dve("tensor_scalar", sb["sc_a"][:, 3:4], sb["sc_a"][:, 1:2],
                      1.0 / D, None, ALU.mult)
              ctx.dve("tensor_tensor", sb["sc_a"][:, 4:5], sb["sc_a"][:, 2:3],
                      sb["sc_a"][:, 2:3], ALU.mult)
              ctx.dve("tensor_tensor", sb["sc_a"][:, 5:6], sb["sc_a"][:, 3:4],
                      sb["sc_a"][:, 4:5], ALU.subtract)
              ctx.dve("tensor_scalar", sb["sc_a"][:, 6:7], sb["sc_a"][:, 5:6],
                      LN_EPS, None, ALU.add)
              ctx.dve("reciprocal", sb["sc_a"][:, 7:8], sb["sc_a"][:, 6:7])
              ctx.mark("dve", ("erv", j))
              ctx.wait(A, "dve", ("erv", j))
              ctx.act("activation", sb["sc_a"][:, 8:9], sb["sc_a"][:, 7:8],
                      AF.Sqrt)
              ctx.mark("act", ("erstd", j))
              ctx.wait(V, "act", ("erstd", j))
              ctx.dve("scalar_tensor_tensor", sb["sc_a"][:, 9:10],
                      sb["sc_a"][:, 2:3], -1.0, sb["sc_a"][:, 8:9],
                      ALU.mult, ALU.mult)
              ctx.mark("dve", ("ebias", j))
              ctx.wait(A, "dve", ("ebias", j))
              ctx.act("activation", fln[:], ftf[:], AF.Identity,
                      scale=sb["sc_a"][:, 8:9], bias=sb["sc_a"][:, 9:10])
              ctx.mark("act", ("eln", j))
              ctx.wait(V, "act", ("eln", j))
              ctx.wait(V, "qs", ("ex1", j))
              ctx.dve("tensor_tensor", ftf[:], fln[:], x1t2[:], ALU.add)
              ctx.mark("dve", ("eout", j))
              ctx.wait(SY, "dve", ("eout", j))
              ctx.dma(SY, "st", ap["y"][128 * j:128 * (j + 1), :], ftf[:])
              ctx.mark("st", ("y", j))

    if ctx.cnt["st"] > 0:
        ctx.wait(SY, "st", val=ctx.cnt["st"])
        gp.wait_ge(ctx.sems["st"], ctx.cnt["st"])
    else:
        gp.nop()


def kernel(L0_attn_hidden_states, L0_attn_Wq, L0_attn_Wk, L0_attn_Wv,
           L0_attn_Wo, L0_attn_gate, L0_ffn_w_gate_up, L0_ffn_w_proj):
    X = np.ascontiguousarray(np.asarray(L0_attn_hidden_states,
                                        np.float32).reshape(S, D))
    wqT = np.ascontiguousarray(np.asarray(L0_attn_Wq, np.float32).T)
    wkT = np.ascontiguousarray(np.asarray(L0_attn_Wk, np.float32).T)
    wvT = np.ascontiguousarray(np.asarray(L0_attn_Wv, np.float32).T)
    woT = np.ascontiguousarray(np.asarray(L0_attn_Wo, np.float32).T)
    wgu = np.ascontiguousarray(np.asarray(L0_ffn_w_gate_up, np.float32))
    wp = np.ascontiguousarray(np.asarray(L0_ffn_w_proj, np.float32))
    gate = np.ascontiguousarray(
        np.asarray(L0_attn_gate, np.float32).reshape(1, NH))

    inv_freq = (1.0 / 10000.0 ** (np.arange(0, D, 2, dtype=np.float64) / D)
                ).astype(np.float32)
    invf = np.ascontiguousarray(inv_freq.reshape(8, 128).T)
    invf2 = np.ascontiguousarray(
        (invf.astype(np.float64) / (2 * np.pi)).astype(np.float32))
    ident = np.eye(128, dtype=np.float32)
    epoch = np.array([[np.random.randint(1, 2 ** 31 - 1)]], np.int32)
    ar512 = np.arange(512, dtype=np.float32)

    nc, taps = build_program()
    in_maps = []
    for c in range(8):
        role = c & 1
        in_maps.append({
            "x_own": np.ascontiguousarray(X[NT * c:NT * (c + 1)]),
            "x_kv": np.ascontiguousarray(X[KT * role:KT * (role + 1)]),
            "wqT": wqT, "wkT": wkT, "wvT": wvT, "woT": woT,
            "wgu": wgu, "wp": wp, "gate": gate,
            "iotak": (KT * role + ar512)[None, :],
            "iotaq": (NT * c + ar512)[None, :],
            "invf": invf, "invf2": invf2, "identm": ident,
            "role_t": np.array([[role]], np.int32),
            "epoch_t": epoch,
        })
    res = run_bass_kernel_spmd(nc, in_maps, core_ids=list(range(8)))
    out = np.concatenate([res.results[c]["y"] for c in range(8)], axis=0)
    return out.reshape(1, S, D)


# version 09
# speedup vs baseline: 1.4898x; 1.1618x over previous
import math
import numpy as np
import concourse.bass as bass
import concourse.mybir as mybir
from concourse.bass_utils import run_bass_kernel_spmd
from contextlib import ExitStack

F32 = mybir.dt.float32
BF16 = mybir.dt.bfloat16
I32 = mybir.dt.int32
AF = mybir.ActivationFunctionType
ALU = mybir.AluOpType
AX = mybir.AxisListType

S = 8192
D = 2048
NH = 16
HD = 128
NT = 1024
KT = 4096
FH = 8192
NQ = 4
HQ = FH // NQ

SCALE = 1.0 / math.sqrt(HD)
EXP_BIAS = -12.0
LN_EPS = 1e-5
RMS_EPS = float(np.finfo(np.float32).eps)
MAGIC = 12582912.0
C1 = 6.28125
C2 = 0.0019353071795864769
HALF_PI = math.pi / 2.0

DMA_SEMS = ("qs", "qg", "pl", "st")
DEBUG_TAPS = ()
PHASE_LIMIT = 99
PARANOID = False
CDBG = 0


class Ctx:
    def __init__(self, nc):
        self.nc = nc
        self.sems = {}
        self.cnt = {}
        self.marks = {}
        self.last_wait = {}

    def add_sem(self, stack, name):
        self.sems[name] = stack.enter_context(self.nc.semaphore(name))
        self.cnt[name] = 0

    def inc(self, name, instr, n=1):
        step = 16 if name in DMA_SEMS else 1
        self.cnt[name] += n * step
        instr.then_inc(self.sems[name], n * step)
        return self.cnt[name]

    def mark(self, name, key):
        self.marks[(name, key)] = self.cnt[name]

    def mval(self, name, key):
        return self.marks[(name, key)]

    def wait(self, eng, name, key=None, val=None):
        if val is None:
            val = self.marks[(name, key)] if key is not None else self.cnt[name]
        if val <= 0:
            return
        k = (id(eng), name)
        if self.last_wait.get(k, -1) >= val:
            return
        self.last_wait[k] = val
        eng.wait_ge(self.sems[name], val)

    def gwait(self, gp, name, key=None, val=None):
        if val is None:
            val = self.marks[(name, key)] if key is not None else self.cnt[name]
        if val <= 0:
            return
        k = ("gp", name)
        if self.last_wait.get(k, -1) >= val:
            return
        self.last_wait[k] = val
        gp.wait_ge(self.sems[name], val)

    def dve(self, opname, *a, **kw):
        eng = self.nc.vector
        self.wait(eng, "dve", val=self.cnt["dve"])
        return self.inc("dve", getattr(eng, opname)(*a, **kw))

    def act(self, opname, *a, **kw):
        eng = self.nc.scalar
        self.wait(eng, "act", val=self.cnt["act"])
        return self.inc("act", getattr(eng, opname)(*a, **kw))

    def dma(self, eng, sem, out, in_, **kw):
        return self.inc(sem, eng.dma_start(out=out, in_=in_, **kw))


class StopEmission(Exception):
    pass


def build_program():
    nc = bass.Bass("TRN2", target_bir_lowering=False)
    ap = {}

    def din(name, shape, dt=F32):
        ap[name] = nc.dram_tensor(name, shape, dt, kind="ExternalInput").ap()

    din("x_own", [NT, D])
    din("x_kv", [KT, D])
    din("wqT", [D, D])
    din("wkT", [D, D])
    din("wvT", [D, D])
    din("woT", [D, D])
    din("wgu", [2 * FH, D])
    din("wp", [D, FH])
    din("gate", [1, NH])
    din("iotak", [1, 512])
    din("iotaq", [1, 512])
    din("invf", [128, 8])
    din("invf2", [128, 8])
    din("identm", [128, 128])
    din("role_t", [1, 1], I32)
    din("epoch_t", [1, 1], I32)
    ap["y"] = nc.dram_tensor("y", [NT, D], F32, kind="ExternalOutput").ap()

    ap["ksh"] = nc.dram_tensor("ksh", [D, S], BF16, kind="Internal",
                               addr_space="Shared").ap()
    ap["vsh"] = nc.dram_tensor("vsh", [NH, S, HD], BF16, kind="Internal",
                               addr_space="Shared").ap()
    ap["msh"] = nc.dram_tensor("msh", [2 * NH * 128, 129], F32, kind="Internal",
                               addr_space="Shared").ap()
    ap["fsh"] = nc.dram_tensor("fsh", [2, 8], I32, kind="Internal",
                               addr_space="Shared").ap()
    ap["x1d"] = nc.dram_tensor("x1d", [NT, D], F32, kind="Internal").ap()

    taps = {}

    def tout(name, shape, dt):
        taps[name] = nc.dram_tensor(name, shape, dt, kind="ExternalOutput").ap()

    if "ksh" in DEBUG_TAPS:
        tout("tap_ksh", [D, S], BF16)
        tout("tap_vsh", [NH * S, HD], BF16)
    if "q" in DEBUG_TAPS:
        tout("tap_q", [128, NH * NT], BF16)
    if "mem" in DEBUG_TAPS:
        tout("tap_mem", [2 * NH * 128, 129], F32)
    if "comb" in DEBUG_TAPS:
        tout("tap_comb", [128, NH * NT], BF16)
    if "x1" in DEBUG_TAPS:
        tout("tap_x1", [NT, D], F32)
    if "ln2" in DEBUG_TAPS:
        tout("tap_ln2", [128, 16 * NT], BF16)
    if "ffn" in DEBUG_TAPS:
        tout("tap_ffn", [128, 16 * NT], BF16)
    if "ob7" in DEBUG_TAPS:
        tout("tap_ob7", [128, 16 * 512], BF16)
    if "lnf" in DEBUG_TAPS:
        tout("tap_lnf", [128, 16 * 512], BF16)
        tout("tap_tblc", [128, 512], F32)
        tout("tap_tbls", [128, 512], F32)

    with ExitStack() as root:
        ctx = Ctx(nc)
        for s_ in ("qs", "qg", "pl", "st", "pe", "act", "dve"):
            ctx.add_sem(root, s_)
        blk = root.enter_context(nc.Block())

        @blk.gpsimd
        def _(gp):
            emit(nc, ctx, ap, taps, gp, root)

    return nc, taps


def emit(nc, ctx, ap, taps, gp, root):
    V, A, T, SY = nc.vector, nc.scalar, nc.tensor, nc.sync
    sb = {}

    def alloc(stack, name, shape, dt, side="left"):
        sb[name] = stack.enter_context(
            nc.sbuf_tensor("s_" + name, shape, dt, side=side))
        return sb[name]

    for nm, shp, dt in (
        ("invf", [128, 8], F32), ("invf2", [128, 8], F32),
        ("ones_col", [128, 1], BF16), ("ones_row", [1, 128], BF16),
        ("ones1f", [1, 128], F32),
        ("gate_raw", [1, NH], F32), ("gate_g", [1, NH], F32),
        ("gate_1mg", [1, NH], F32),
        ("role_sb", [1, 1], I32), ("epoch_sb", [1, 8], I32),
        ("flag_sb", [1, 8], I32),
        ("iotak", [1, 512], F32), ("iotaq", [1, 512], F32),
        ("st_sum", [128, 48], F32), ("st_ssq", [128, 48], F32),
        ("st_rstd", [128, 48], F32), ("st_bias", [128, 48], F32),
        ("sc_a", [128, 48], F32), ("sc_b", [128, 48], F32),
        ("identb", [128, 128], BF16), ("identf", [128, 128], F32),
        ("junk", [128, D], BF16), ("cexpb", [128, 1], F32),
    ):
        alloc(root, nm, shp, dt)

    V.memset(sb["ones_col"][:], 1.0)
    V.memset(sb["ones_row"][:], 1.0)
    V.memset(sb["ones1f"][:], 1.0)
    V.memset(sb["cexpb"][:], EXP_BIAS)
    nc.const_aps.register(F32, EXP_BIAS, sb["cexpb"][:])

    for nm in ("invf", "invf2", "iotak", "iotaq", "identm"):
        dst = "identf" if nm == "identm" else nm
        ctx.dma(SY, "qs", sb[dst][:], ap[nm])
    ctx.dma(SY, "qs", sb["gate_raw"][:], ap["gate"])
    ctx.dma(SY, "qs", sb["role_sb"][:], ap["role_t"])
    ctx.dma(SY, "qs", sb["epoch_sb"][:, 0:1], ap["epoch_t"])
    ctx.mark("qs", "consts")
    ctx.wait(V, "qs", "consts")
    ctx.dve("tensor_copy", sb["identb"][:], sb["identf"][:])
    ctx.mark("dve", "identb")

    rg = gp.register("r_role").__enter__()
    ctx.gwait(gp, "qs", "consts")
    gp.reg_load(rg, sb["role_sb"][0:1, 0:1])

    NKT, NOT_ = KT // 128, NT // 128
    NTILES = NKT + NOT_
    with ExitStack() as sA0:
        xb0 = [alloc(sA0, f"a0x{i}", [128, D], F32) for i in range(2)]
        ctx.wait(A, "qs", "consts")
        ctx.act("activation", sb["gate_g"][:], sb["gate_raw"][:], AF.Sigmoid)
        ctx.mark("act", "gate")
        ctx.wait(V, "act", "gate")
        ctx.dve("tensor_scalar", sb["gate_1mg"][:], sb["gate_g"][:],
                -1.0, 1.0, ALU.mult, ALU.add)
        for t in range(NTILES):
            src = (ap["x_kv"][128 * t:128 * (t + 1), :] if t < NKT else
                   ap["x_own"][128 * (t - NKT):128 * (t - NKT + 1), :])
            if t >= 2:
                ctx.wait(SY, "act", ("a0", t - 2))
                ctx.wait(SY, "dve", ("a0v", t - 2))
            ctx.dma(SY, "qs", xb0[t % 2][:], src)
            ctx.mark("qs", ("a0x", t))
            ctx.wait(V, "qs", ("a0x", t))
            ctx.dve("reduce_sum", sb["st_sum"][:, t:t + 1], xb0[t % 2][:], AX.X)
            ctx.mark("dve", ("a0v", t))
            ctx.wait(A, "qs", ("a0x", t))
            ctx.act("activation", sb["junk"][:], xb0[t % 2][:], AF.Square,
                    accum_out=sb["st_ssq"][:, t:t + 1])
            ctx.mark("act", ("a0", t))
        n = NTILES
        ctx.dve("tensor_scalar", sb["sc_a"][:, 0:n], sb["st_sum"][:, 0:n],
                1.0 / D, None, ALU.mult)
        ctx.wait(V, "act", ("a0", NTILES - 1))
        ctx.dve("tensor_scalar", sb["sc_b"][:, 0:n], sb["st_ssq"][:, 0:n],
                1.0 / D, None, ALU.mult)
        ctx.dve("tensor_tensor", sb["st_rstd"][:, 0:n], sb["sc_a"][:, 0:n],
                sb["sc_a"][:, 0:n], ALU.mult)
        ctx.dve("tensor_tensor", sb["st_bias"][:, 0:n], sb["sc_b"][:, 0:n],
                sb["st_rstd"][:, 0:n], ALU.subtract)
        ctx.dve("tensor_scalar", sb["sc_b"][:, 0:n], sb["st_bias"][:, 0:n],
                LN_EPS, None, ALU.add)
        ctx.dve("reciprocal", sb["st_rstd"][:, 0:n], sb["sc_b"][:, 0:n])
        ctx.mark("dve", "a0rv")
        ctx.wait(A, "dve", "a0rv")
        ctx.act("activation", sb["st_bias"][:, 0:n], sb["st_rstd"][:, 0:n],
                AF.Sqrt)
        ctx.mark("act", "a0rstd")
        ctx.wait(V, "act", "a0rstd")
        ctx.dve("tensor_copy", sb["st_rstd"][:, 0:n], sb["st_bias"][:, 0:n])
        ctx.dve("scalar_tensor_tensor", sb["st_bias"][:, 0:n],
                sb["sc_a"][:, 0:n], -1.0, sb["st_rstd"][:, 0:n],
                ALU.mult, ALU.mult)
        ctx.mark("dve", "stats")

    sAB = ExitStack()
    q_dm = alloc(sAB, "q_dm", [128, NH * NT], BF16)

    with ExitStack() as sA:
        ps_1 = sA.enter_context(nc.psum_tensor([128, 1024], F32))
        ps_2 = sA.enter_context(nc.psum_tensor([128, 1024], F32))
        ps_t = sA.enter_context(nc.psum_tensor([128, 1024], BF16))
        ps_3 = sA.enter_context(nc.psum_tensor([128, 512], F32))

        ln1f = alloc(sA, "ln1f", [128, 16 * 512], BF16)
        obuf = alloc(sA, "obuf", [128, 16 * 512], BF16)
        ln1f_3d = ln1f[:].rearrange("p (fc t) -> p fc t", t=512)
        pst_3d = ps_t[:].rearrange("p (k e) -> p k e", e=128)
        ksh3 = ap["ksh"].rearrange("(dc p) t -> p dc t", p=128)
        prev_pass = {"k": None, "v": "k", "q": "v"}

        sA1 = ExitStack()
        wslot = alloc(sA1, "wslot", [128, 16 * D], BF16)
        xb = [alloc(sA1, f"xab{i}", [128, D], F32) for i in range(2)]
        lt = [alloc(sA1, f"lt{i}", [128, D], BF16) for i in range(2)]
        tsb = alloc(sA1, "tsb", [128, 512], F32)
        tscr = [alloc(sA1, f"tscr{i}", [128, 512], F32) for i in range(4)]
        sarg = alloc(sA1, "sarg", [128, 512], F32)
        carg = alloc(sA1, "carg", [128, 512], F32)
        tblc = alloc(sA1, "tblc", [128, 512], BF16)
        tbls = alloc(sA1, "tbls", [128, 512], BF16)
        kei = alloc(sA1, "kei", [128, 512], BF16)
        kei8 = alloc(sA1, "kei8", [128, 512], BF16)
        rscrb = [alloc(sA1, f"rscrb{i}", [128, 512], BF16) for i in range(4)]

        ctx.mark("dve", "leprev")
        ctx.mark("qg", "kspill_prev")
        ctx.mark("act", "vspill_guard")

        def load_weight(wname, key):
            ctx.wait(V, "pe", val=ctx.cnt["pe"])
            ctx.wait(SY, "act", val=ctx.cnt["act"])
            ctx.wait(SY, "dve", val=ctx.cnt["dve"])
            for i in range(16):
                if i >= 2:
                    ctx.wait(SY, "dve", ("wc", key, i - 2))
                ctx.dma(SY, "qs", xb[i % 2][:],
                        ap[wname][128 * i:128 * (i + 1), :])
                ctx.mark("qs", ("wl", key, i))
                ctx.wait(V, "qs", ("wl", key, i))
                ctx.dve("tensor_copy", wslot[:, D * i:D * (i + 1)],
                        xb[i % 2][:])
                ctx.mark("dve", ("wc", key, i))
            ctx.mark("dve", ("wdone", key))

        def ln_block(key, blk, tile0, xsrc, prev_mm_mark=None,
                     prev_lt_mark=None):
            ctx.wait(SY, "dve", ("wdone", key))
            for tt in range(4):
                t = tile0 + 4 * blk + tt
                gi = 4 * blk + tt
                r0 = 512 * blk + 128 * tt
                if gi >= 2:
                    ctx.wait(SY, "act", ("lnc", key, gi - 2))
                ctx.dma(SY, "qs", xb[gi % 2][:], xsrc[r0:r0 + 128, :])
                ctx.mark("qs", ("xa", key, gi))
                ctx.wait(A, "qs", ("xa", key, gi))
                ctx.wait(A, "dve", "stats")
                if gi >= 2:
                    ctx.wait(A, "pe", ("lt", key, gi - 2, 1))
                elif prev_lt_mark is not None:
                    ctx.wait(A, "pe", val=prev_lt_mark)
                ctx.act("activation", lt[gi % 2][:], xb[gi % 2][:], AF.Identity,
                        bias=sb["st_bias"][:, t:t + 1],
                        scale=sb["st_rstd"][:, t:t + 1])
                ctx.mark("act", ("lnc", key, gi))
                for rnd in range(2):
                    ctx.wait(T, "act", ("lnc", key, gi))
                    ctx.wait(T, "dve", "leprev")
                    last = None
                    for k in range(8):
                        fc = 8 * rnd + k
                        last = T.transpose(ps_t[:, 128 * k:128 * (k + 1)],
                                           lt[gi % 2][:, 128 * fc:128 * (fc + 1)],
                                           sb["identb"][:])
                    ctx.inc("pe", last)
                    ctx.mark("pe", ("lt", key, gi, rnd))
                    ctx.wait(V, "pe", ("lt", key, gi, rnd))
                    if tt == 0 and rnd == 0 and prev_mm_mark is not None:
                        ctx.wait(V, "pe", val=prev_mm_mark)
                    ctx.dve("tensor_copy",
                            ln1f_3d[:, 8 * rnd:8 * rnd + 8,
                                    128 * tt:128 * (tt + 1)],
                            pst_3d[:, :, :])
                    ctx.mark("dve", "leprev")
            ctx.mark("dve", ("lnf", key, blk))

        def gen_block_bcast(iota_name):
            if ("act", "tsb") in ctx.marks:
                ctx.wait(T, "act", "tsb")
            mm = T.matmul(ps_3[:, 0:512], sb["ones1f"][:],
                          sb[iota_name][0:1, :], start=True, stop=True)
            ctx.inc("pe", mm)
            ctx.mark("pe", "tbc")
            ctx.wait(A, "pe", "tbc")
            ctx.act("copy", tsb[:], ps_3[:, 0:512])
            ctx.mark("act", "tsb")

        def gen_tables(pr, tlo):
            ctx.wait(V, "act", "tsb")
            th, u, kk, y1 = tscr
            ctx.dve("tensor_scalar", th[:], tsb[:], float(tlo),
                    sb["invf"][:, pr:pr + 1], ALU.add, ALU.mult)
            ctx.dve("tensor_scalar", u[:], tsb[:], float(tlo),
                    sb["invf2"][:, pr:pr + 1], ALU.add, ALU.mult)
            ctx.dve("tensor_scalar", kk[:], u[:], MAGIC, MAGIC,
                    ALU.add, ALU.subtract)
            ctx.dve("scalar_tensor_tensor", y1[:], kk[:], -C1, th[:],
                    ALU.mult, ALU.add)
            ctx.dve("scalar_tensor_tensor", sarg[:], kk[:], -C2, y1[:],
                    ALU.mult, ALU.add)
            ctx.dve("tensor_scalar", kk[:], u[:], 0.25, None, ALU.add)
            ctx.dve("tensor_scalar", u[:], kk[:], MAGIC, MAGIC,
                    ALU.add, ALU.subtract)
            ctx.dve("scalar_tensor_tensor", y1[:], u[:], -C1, th[:],
                    ALU.mult, ALU.add)
            ctx.dve("scalar_tensor_tensor", kk[:], u[:], -C2, y1[:],
                    ALU.mult, ALU.add)
            ctx.dve("tensor_scalar", carg[:], kk[:], HALF_PI, None, ALU.add)
            ctx.mark("dve", "tblarg")
            ctx.wait(A, "dve", "tblarg")
            ctx.act("activation", tbls[:], sarg[:], AF.Sin)
            ctx.act("activation", tblc[:], carg[:], AF.Sin)
            ctx.mark("act", "tbl")

        def kq_pass(key, wname, xsrc, nblk, tile0, iota_name, store_q):
            load_weight(wname, key)
            for blk in range(nblk):
                if not store_q:
                    ctx.wait(V, "qg", "kspill_prev")
                pm = ctx.mval("pe", ("mm", key, blk - 1, 7)) if blk >= 1 else \
                    None
                pl_ = None
                if blk == 0 and ("pe", ("lt", prev_pass[key], 31, 1)) in \
                        ctx.marks:
                    pl_ = ctx.mval("pe", ("lt", prev_pass[key], 31, 1))
                ln_block(key, blk, tile0, xsrc, pm, pl_)
                ctx.wait(T, "dve", ("lnf", key, blk))
                if key == "k" and blk == 7 and "tap_lnf" in taps:
                    ctx.wait(SY, "dve", ("lnf", key, blk))
                    ctx.dma(SY, "st", taps["tap_lnf"], ln1f[:])
                gen_block_bcast(iota_name)
                for pr in range(8):
                    if key == "q" and blk == 0 and pr < 2:
                        ctx.wait(T, "act", ("vev", 7, 15))
                    gen_tables(pr, 512 * blk)
                    slot = (pr % 2) * 512
                    if pr >= 2:
                        ctx.wait(T, "act", ("kev", key, blk, pr - 2))
                    elif blk >= 1:
                        ctx.wait(T, "act", ("kev", key, blk - 1, pr + 6))
                    for half, ps in ((0, ps_1), (1, ps_2)):
                        dc = pr + 8 * half
                        last = None
                        for fc in range(16):
                            last = T.matmul(
                                ps[:, slot:slot + 512],
                                wslot[:, D * fc + 128 * dc:
                                      D * fc + 128 * (dc + 1)],
                                ln1f[:, 512 * fc:512 * (fc + 1)],
                                start=(fc == 0), stop=(fc == 15))
                        ctx.inc("pe", last)
                    ctx.mark("pe", ("mm", key, blk, pr))
                    ctx.wait(A, "pe", ("mm", key, blk, pr))
                    i, i8 = pr, pr + 8
                    if store_q:
                        d0 = q_dm[:, NT * i + 512 * blk:
                                  NT * i + 512 * blk + 512]
                        d1 = q_dm[:, NT * i8 + 512 * blk:
                                  NT * i8 + 512 * blk + 512]
                    else:
                        d0 = obuf[:, 512 * i:512 * (i + 1)]
                        d1 = obuf[:, 512 * i8:512 * (i8 + 1)]
                    ctx.act("copy", kei[:], ps_1[:, slot:slot + 512])
                    ctx.act("copy", kei8[:], ps_2[:, slot:slot + 512])
                    ctx.mark("act", ("kev", key, blk, pr))
                    ctx.wait(V, "act", ("kev", key, blk, pr))
                    r0, r1, r2, r3 = rscrb
                    ctx.dve("tensor_tensor", r0[:], kei[:], tblc[:], ALU.mult)
                    ctx.dve("tensor_tensor", r1[:], kei8[:], tbls[:], ALU.mult)
                    ctx.dve("tensor_tensor", d0, r0[:], r1[:], ALU.subtract)
                    ctx.dve("tensor_tensor", r2[:], kei8[:], tblc[:], ALU.mult)
                    ctx.dve("tensor_tensor", r3[:], kei[:], tbls[:], ALU.mult)
                    ctx.dve("tensor_tensor", d1, r2[:], r3[:], ALU.add)
                    ctx.mark("dve", ("rope", key, blk, pr))
                    if key == "k" and blk == 7 and pr == 0 and "tap_lnf" in taps:
                        ctx.wait(SY, "dve", "tbl")
                        ctx.dma(SY, "st", taps["tap_tblc"], tblc[:])
                        ctx.dma(SY, "st", taps["tap_tbls"], tbls[:])
                if not store_q:
                    ctx.mark("dve", ("kblk", blk))
                    ctx.gwait(gp, "dve", ("kblk", blk))
                    if blk == 7 and "tap_ob7" in taps:
                        gp.dma_start(out=taps["tap_ob7"],
                                     in_=obuf[:]).then_inc(ctx.sems["st"], 16)
                        ctx.cnt["st"] += 16
                    src = obuf[:].rearrange("p (dc t) -> p dc t", t=512)
                    with gp.If_eq(rg, 0):
                        gp.dma_start(
                            out=ksh3[:, :, 512 * blk:512 * (blk + 1)],
                            in_=src).then_inc(ctx.sems["qg"], 16)
                    with gp.Else():
                        gp.dma_start(
                            out=ksh3[:, :, KT + 512 * blk:KT + 512 * (blk + 1)],
                            in_=src).then_inc(ctx.sems["qg"], 16)
                    ctx.cnt["qg"] += 16
                    ctx.mark("qg", "kspill_prev")

        if PHASE_LIMIT >= 2:
            kq_pass("k", "wkT", ap["x_kv"], 8, 0, "iotak", False)

        if "tap_ksh" in taps and PHASE_LIMIT < 6:
            ctx.gwait(gp, "qg", "kspill_prev")
            gp.dma_start(out=taps["tap_ksh"], in_=ap["ksh"]).then_inc(
                ctx.sems["st"], 16)
            ctx.cnt["st"] += 16
            if PHASE_LIMIT >= 3:
                pass

        if PHASE_LIMIT >= 3:
          load_weight("wvT", "v")
          slots = [(ps_1, 0), (ps_1, 512), (ps_2, 0), (ps_2, 512)]
          for blk in range(8):
              ctx.wait(A, "qg", "kspill_prev")
              ctx.wait(A, "act", "vspill_guard")
              ln_block("v", blk, 0, ap["x_kv"])
              ctx.wait(T, "dve", ("lnf", "v", blk))
              it = 0
              for ssub in range(4):
                  for eb in range(4):
                      ps, off = slots[it % 4]
                      if it >= 4:
                          ctx.wait(T, "act", ("vev", blk, it - 4))
                      elif blk >= 1:
                          ctx.wait(T, "act", ("vev", blk - 1, 12 + it))
                      else:
                          ctx.wait(T, "dve", ("rope", "k", 7, 7))
                      last = None
                      for fc in range(16):
                          last = T.matmul(
                              ps[:, off:off + 512],
                              ln1f[:, 512 * fc + 128 * ssub:
                                   512 * fc + 128 * (ssub + 1)],
                              wslot[:, D * fc + 512 * eb:D * fc + 512 * (eb + 1)],
                              start=(fc == 0), stop=(fc == 15))
                      ctx.inc("pe", last)
                      ctx.mark("pe", ("vmm", blk, it))
                      ctx.wait(A, "pe", ("vmm", blk, it))
                      ctx.act("copy",
                              obuf[:, 2048 * ssub + 512 * eb:
                                   2048 * ssub + 512 * (eb + 1)],
                              ps[:, off:off + 512])
                      ctx.mark("act", ("vev", blk, it))
                      it += 1
              ctx.mark("act", ("vblk", blk))
              ctx.gwait(gp, "act", ("vblk", blk))
              trow0, trow1 = 512 * blk, KT + 512 * blk
              with gp.If_eq(rg, 0):
                  for cc in range(4):
                      src3 = obuf[:, 2048 * cc:2048 * (cc + 1)].rearrange(
                          "p (h e) -> p h e", e=HD)
                      r0 = trow0 + 128 * cc
                      gp.dma_start(
                          out=ap["vsh"][:, r0:r0 + 128, :].rearrange(
                              "h p e -> p h e"),
                          in_=src3).then_inc(ctx.sems["qg"], 16)
              with gp.Else():
                  for cc in range(4):
                      src3 = obuf[:, 2048 * cc:2048 * (cc + 1)].rearrange(
                          "p (h e) -> p h e", e=HD)
                      r1 = trow1 + 128 * cc
                      gp.dma_start(
                          out=ap["vsh"][:, r1:r1 + 128, :].rearrange(
                              "h p e -> p h e"),
                          in_=src3).then_inc(ctx.sems["qg"], 16)
              ctx.cnt["qg"] += 64
              ctx.mark("qg", ("vspill", blk))
              ctx.mark("qg", "kspill_prev")
          ctx.mark("qg", "vspill_done")

        if PHASE_LIMIT >= 4:
            kq_pass("q", "wqT", ap["x_own"], 2, NKT, "iotaq", True)
            ctx.mark("dve", "qdone")
            if "tap_q" in taps:
                ctx.wait(SY, "dve", "qdone")
                ctx.dma(SY, "st", taps["tap_q"], q_dm[:])

        sA1.close()
        if PHASE_LIMIT >= 5:
          v129 = alloc(sA, "v129", [128, 32 * 129 + 64], BF16)
          kfb = alloc(sA, "kfb", [128, 512], BF16)
          kfs = [alloc(sA, f"kfs{i}", [128, 512], F32) for i in range(2)]
          mstg = alloc(sA, "mstg", [128, 129], F32)
          v129_3 = v129[:, 0:4128].rearrange("p (c e) -> p c e", e=129) \
              if False else None

          krh = obuf
          ctx.mark("qg", "mspill_prev")
          for h in range(NH):
              ctx.gwait(gp, "dve", "qdone")
              ctx.gwait(gp, "qg", "vspill_done")
              if h >= 1:
                  ctx.gwait(gp, "pe", ("a2mm", h - 1))
              with gp.If_eq(rg, 0):
                  gp.dma_start(out=krh[:, 0:KT],
                               in_=ap["ksh"][128 * h:128 * (h + 1), 0:KT]
                               ).then_inc(ctx.sems["qg"], 16)
                  gp.dma_start(
                      out=v129[:, 0:4128].rearrange("p (c e) -> p c e", e=129)[:, 0:32, 0:128],
                      in_=ap["vsh"][h, 0:KT, :].rearrange("(c p) e -> p c e",
                                                          p=128)
                      ).then_inc(ctx.sems["qg"], 16)
              with gp.Else():
                  gp.dma_start(out=krh[:, 0:KT],
                               in_=ap["ksh"][128 * h:128 * (h + 1), KT:S]
                               ).then_inc(ctx.sems["qg"], 16)
                  gp.dma_start(
                      out=v129[:, 0:4128].rearrange("p (c e) -> p c e", e=129)[:, 0:32, 0:128],
                      in_=ap["vsh"][h, KT:S, :].rearrange("(c p) e -> p c e",
                                                          p=128)
                      ).then_inc(ctx.sems["qg"], 16)
              ctx.cnt["qg"] += 32
              ctx.mark("qg", ("a2kv", h))
              if h == 0:
                  ctx.wait(V, "qg", ("a2kv", h))
                  ctx.dve("memset",
                          v129[:, 0:4128].rearrange(
                              "p (c e) -> p c e", e=129)[:, 0:32, 128:129], 1.0)
                  ctx.mark("dve", "v129ones")
              for grp in range(8):
                  ctx.wait(T, "qg", ("a2kv", h))
                  ctx.wait(T, "dve", "leprev")
                  last = None
                  for g in range(4):
                      j = 4 * grp + g
                      last = T.transpose(ps_t[:, 128 * g:128 * (g + 1)],
                                         krh[:, 128 * j:128 * (j + 1)],
                                         sb["identb"][:])
                  ctx.inc("pe", last)
                  ctx.mark("pe", ("a2t", h, grp))
                  ctx.wait(V, "pe", ("a2t", h, grp))
                  ctx.dve("tensor_scalar", kfs[0][:], ps_t[:, 0:512], 0.0, None,
                          ALU.min)
                  ctx.mark("dve", ("a2m", h, grp))
                  ctx.wait(A, "dve", ("a2m", h, grp))
                  ctx.act("activation", kfs[1][:], kfs[0][:], AF.Exp)
                  ctx.mark("act", ("a2e", h, grp))
                  ctx.wait(V, "act", ("a2e", h, grp))
                  ctx.dve("scalar_tensor_tensor", kfb[:], ps_t[:, 0:512], 0.0,
                          kfs[1][:], ALU.max, ALU.add)
                  ctx.mark("dve", "leprev")
                  ctx.wait(T, "dve", "leprev")
                  ctx.wait(T, "dve", "v129ones")
                  last = None
                  for g in range(4):
                      j = 4 * grp + g
                      last = T.matmul(ps_3[:, 0:129],
                                      kfb[:, 128 * g:128 * (g + 1)],
                                      v129[:, 129 * j:129 * j + 129],
                                      start=(grp == 0 and g == 0),
                                      stop=(grp == 7 and g == 3))
                  ctx.inc("pe", last)
              ctx.mark("pe", ("a2mm", h))
              ctx.wait(A, "pe", ("a2mm", h))
              ctx.wait(A, "qg", "mspill_prev")
              ctx.act("copy", mstg[:], ps_3[:, 0:129])
              ctx.mark("act", ("a2ev", h))
              ctx.gwait(gp, "act", ("a2ev", h))
              with gp.If_eq(rg, 0):
                  gp.dma_start(out=ap["msh"][128 * h:128 * (h + 1), :],
                               in_=mstg[:]).then_inc(ctx.sems["qg"], 16)
              with gp.Else():
                  gp.dma_start(
                      out=ap["msh"][NH * 128 + 128 * h:NH * 128 + 128 * (h + 1), :],
                      in_=mstg[:]).then_inc(ctx.sems["qg"], 16)
              ctx.cnt["qg"] += 16
              ctx.mark("qg", "mspill_prev")

        if PHASE_LIMIT >= 6:
          with (gp.register("r_epoch") as r_epoch,
                gp.register("r_f") as r_f,
                gp.register("r_cont") as r_cont,
                gp.register("r_tgt") as r_tgt):
              gp.reg_load(r_epoch, sb["epoch_sb"][0:1, 0:1])
              with gp.If_eq(rg, 0):
                  gp.dma_start(out=ap["fsh"][0:1, 0:1],
                               in_=sb["epoch_sb"][0:1, 0:1]
                               ).then_inc(ctx.sems["qg"], 16)
              with gp.Else():
                  gp.dma_start(out=ap["fsh"][1:2, 0:1],
                               in_=sb["epoch_sb"][0:1, 0:1]
                               ).then_inc(ctx.sems["qg"], 16)
              ctx.cnt["qg"] += 16
              gp.wait_ge(ctx.sems["qg"], ctx.cnt["qg"])
              gp.reg_mov(r_cont, 1)
              gp.reg_mov(r_tgt, 16)
              with gp.While(r_cont):
                  with gp.If_eq(rg, 0):
                      gp.dma_start(out=sb["flag_sb"][0:1, 0:1],
                                   in_=ap["fsh"][1:2, 0:1]
                                   ).then_inc(ctx.sems["pl"], 16)
                  with gp.Else():
                      gp.dma_start(out=sb["flag_sb"][0:1, 0:1],
                                   in_=ap["fsh"][0:1, 0:1]
                                   ).then_inc(ctx.sems["pl"], 16)
                  gp.wait_ge(ctx.sems["pl"], r_tgt)
                  gp.reg_add(r_tgt, r_tgt, 16)
                  gp.reg_load(r_f, sb["flag_sb"][0:1, 0:1])
                  gp.reg_alu(r_cont, r_f, r_epoch, ALU.not_equal)

        if "tap_ksh" in taps:
            gp.dma_start(out=taps["tap_ksh"], in_=ap["ksh"]).then_inc(
                ctx.sems["st"], 16)
            gp.dma_start(out=taps["tap_vsh"],
                         in_=ap["vsh"].rearrange("h s e -> (h s) e")
                         ).then_inc(ctx.sems["st"], 16)
            ctx.cnt["st"] += 32
        if "tap_mem" in taps:
            gp.dma_start(out=taps["tap_mem"], in_=ap["msh"]).then_inc(
                ctx.sems["st"], 16)
            ctx.cnt["st"] += 16

    sBC = ExitStack()
    if PHASE_LIMIT >= 7:
      comb = alloc(sBC, "comb", [128, NH * NT], BF16, side="right")
      with ExitStack() as sB:
          ps_s0 = sB.enter_context(nc.psum_tensor([128, 1024], F32))
          ps_s1 = sB.enter_context(nc.psum_tensor([128, 1024], F32))
          ps_pv = sB.enter_context(nc.psum_tensor([128, 1024], F32))
          ps_den = sB.enter_context(nc.psum_tensor([1, 1024], F32))
          ps_s = [ps_s0, ps_s1]

          kh = [alloc(sB, f"kh{i}", [128, S], BF16) for i in range(2)]
          vt = [alloc(sB, f"vt{i}", [128, S], BF16) for i in range(2)]
          pT = [alloc(sB, f"pT{i}", [128, 1024], BF16) for i in range(2)]
          qfb = alloc(sB, "qfb", [128, 1024], BF16)
          qfs = [alloc(sB, f"qfs{i}", [128, 1024], F32) for i in range(2)]
          mem_sb = alloc(sB, "mem_sb", [128, 1024], F32)
          mp = alloc(sB, "mp", [128, 258], F32)
          mn = alloc(sB, "mn", [128, 129], BF16)
          Bsb = alloc(sB, "Bsb", [128, 1024], BF16)
          Blb = alloc(sB, "Blb", [128, 1024], BF16)
          u1 = alloc(sB, "u1", [128, 1024], F32)
          u2 = alloc(sB, "u2", [128, 1024], F32)
          rlr = alloc(sB, "rlr", [1, 1024], F32)
          rsr = alloc(sB, "rsr", [1, 1024], F32)
          rsb = alloc(sB, "rsb", [1, 1024], BF16)
          rlb = alloc(sB, "rlb", [1, 1024], BF16)

          for h in range(NH):
              if h >= 2:
                  ctx.gwait(gp, "pe", ("S", h - 2, 63))
                  ctx.gwait(gp, "pe", ("PV", h - 2, 63))
              if h >= 1:
                  ctx.gwait(gp, "dve", ("mn", h - 1))
              gp.dma_start(out=kh[h % 2][:],
                           in_=ap["ksh"][128 * h:128 * (h + 1), :]
                           ).then_inc(ctx.sems["qg"], 16)
              gp.dma_start(out=vt[h % 2][:].rearrange("p (c e) -> p c e", e=HD),
                           in_=ap["vsh"][h, :, :].rearrange("(c p) e -> p c e",
                                                            p=128)
                           ).then_inc(ctx.sems["qg"], 16)
              gp.dma_start(out=mp[:, 0:129],
                           in_=ap["msh"][128 * h:128 * (h + 1), :]
                           ).then_inc(ctx.sems["qg"], 16)
              gp.dma_start(out=mp[:, 129:258],
                           in_=ap["msh"][NH * 128 + 128 * h:
                                         NH * 128 + 128 * (h + 1), :]
                           ).then_inc(ctx.sems["qg"], 16)
              ctx.cnt["qg"] += 64
              ctx.mark("qg", ("bkv", h))

              ctx.wait(V, "qg", ("bkv", h))
              ctx.dve("tensor_tensor", mn[:], mp[:, 0:129], mp[:, 129:258],
                      ALU.add)
              ctx.mark("dve", ("mn", h))
              if h >= 1:
                  ctx.wait(V, "pe", ("bdl", h - 1))
              ctx.dve("tensor_scalar", qfs[0][:], q_dm[:, NT * h:NT * (h + 1)],
                      0.0, None, ALU.min)
              ctx.mark("dve", ("bqf0", h))
              ctx.wait(A, "dve", ("bqf0", h))
              ctx.act("activation", qfs[1][:], qfs[0][:], AF.Exp)
              ctx.mark("act", ("bqf1", h))
              ctx.wait(V, "act", ("bqf1", h))
              ctx.dve("scalar_tensor_tensor", qfb[:],
                      q_dm[:, NT * h:NT * (h + 1)], 0.0, qfs[1][:],
                      ALU.max, ALU.add)
              ctx.mark("dve", ("bqf", h))

              if h >= 1:
                  ctx.wait(T, "dve", ("comb", h - 1))
                  ctx.wait(T, "dve", ("rs", h - 1))
              ctx.wait(T, "dve", ("bqf", h))
              T.matmul(ps_pv[:, 0:512], mn[:, 0:128], qfb[:, 0:512],
                       start=True, stop=True)
              last = T.matmul(ps_pv[:, 512:1024], mn[:, 0:128], qfb[:, 512:1024],
                              start=True, stop=True)
              ctx.inc("pe", last)
              ctx.mark("pe", ("bmo", h))
              T.matmul(ps_den[0:1, 0:512], mn[:, 128:129], qfb[:, 0:512],
                       start=True, stop=True)
              last = T.matmul(ps_den[0:1, 512:1024], mn[:, 128:129],
                              qfb[:, 512:1024], start=True, stop=True)
              ctx.inc("pe", last)
              ctx.mark("pe", ("bdl", h))
              ctx.wait(A, "pe", ("bmo", h))
              if h >= 1:
                  ctx.wait(A, "dve", ("comb", h - 1))
              ctx.act("copy", mem_sb[:], ps_pv[:])
              ctx.mark("act", ("bmev", h))
              ctx.wait(V, "pe", ("bdl", h))
              ctx.dve("reciprocal", rlr[:], ps_den[0:1, :])
              ctx.mark("dve", ("brl", h))

              def emit_pv(j):
                  ctx.wait(T, "act", ("exp", h, j))
                  if j == 0:
                      ctx.wait(T, "act", ("bmev", h))
                      ctx.wait(T, "dve", ("brl", h))
                  T.matmul(ps_pv[:, 0:512], vt[h % 2][:, 128 * j:128 * (j + 1)],
                           pT[j % 2][:, 0:512], start=(j == 0), stop=(j == 63))
                  T.matmul(ps_pv[:, 512:1024],
                           vt[h % 2][:, 128 * j:128 * (j + 1)],
                           pT[j % 2][:, 512:1024], start=(j == 0), stop=(j == 63))
                  T.matmul(ps_den[0:1, 0:512], sb["ones_col"][:],
                           pT[j % 2][:, 0:512], start=(j == 0), stop=(j == 63))
                  last = T.matmul(ps_den[0:1, 512:1024], sb["ones_col"][:],
                                  pT[j % 2][:, 512:1024], start=(j == 0),
                                  stop=(j == 63))
                  ctx.inc("pe", last)
                  ctx.mark("pe", ("PV", h, j))

              for j in range(64):
                  if j >= 2:
                      ctx.wait(T, "act", ("exp", h, j - 2))
                  elif h >= 1:
                      ctx.wait(T, "act", ("bce", h - 1))
                  T.matmul(ps_s[j % 2][:, 0:512],
                           kh[h % 2][:, 128 * j:128 * (j + 1)],
                           q_dm[:, NT * h:NT * h + 512], start=True, stop=True)
                  last = T.matmul(ps_s[j % 2][:, 512:1024],
                                  kh[h % 2][:, 128 * j:128 * (j + 1)],
                                  q_dm[:, NT * h + 512:NT * h + 1024],
                                  start=True, stop=True)
                  ctx.inc("pe", last)
                  ctx.mark("pe", ("S", h, j))
                  ctx.wait(A, "pe", ("S", h, j))
                  ctx.act("activation", pT[j % 2][:], ps_s[j % 2][:], AF.Exp,
                          scale=SCALE, bias=EXP_BIAS)
                  ctx.mark("act", ("exp", h, j))
                  if j >= 1:
                      emit_pv(j - 1)
              emit_pv(63)

              ctx.wait(V, "pe", ("PV", h, 63))
              ctx.dve("reciprocal", rsr[:], ps_den[0:1, :])
              ctx.mark("dve", ("rs", h))
              ctx.dve("tensor_scalar", rsb[:], rsr[:],
                      sb["gate_1mg"][0:1, h:h + 1], None, ALU.mult)
              ctx.dve("tensor_scalar", rlb[:], rlr[:],
                      sb["gate_g"][0:1, h:h + 1], None, ALU.mult)
              ctx.mark("dve", ("brows", h))
              ctx.wait(T, "act", ("exp", h, 63))
              ctx.wait(T, "dve", ("brows", h))
              T.matmul(ps_s0[:, 0:512], sb["ones_row"][:], rsb[0:1, 0:512],
                       start=True, stop=True)
              T.matmul(ps_s0[:, 512:1024], sb["ones_row"][:], rsb[0:1, 512:1024],
                       start=True, stop=True)
              T.matmul(ps_s1[:, 0:512], sb["ones_row"][:], rlb[0:1, 0:512],
                       start=True, stop=True)
              last = T.matmul(ps_s1[:, 512:1024], sb["ones_row"][:],
                              rlb[0:1, 512:1024], start=True, stop=True)
              ctx.inc("pe", last)
              ctx.mark("pe", ("bc", h))
              ctx.wait(A, "pe", ("bc", h))
              ctx.act("copy", Bsb[:], ps_s0[:])
              ctx.act("copy", Blb[:], ps_s1[:])
              ctx.mark("act", ("bce", h))
              ctx.wait(V, "act", ("bce", h))
              ctx.dve("tensor_tensor", u1[:], ps_pv[:], Bsb[:], ALU.mult)
              ctx.dve("tensor_tensor", u2[:], mem_sb[:], Blb[:], ALU.mult)
              ctx.dve("tensor_tensor", comb[:, NT * h:NT * (h + 1)], u1[:],
                      u2[:], ALU.add)
              ctx.mark("dve", ("comb", h))

          ctx.mark("dve", "combdone")
          if "tap_comb" in taps:
              ctx.wait(SY, "dve", "combdone")
              ctx.dma(SY, "st", taps["tap_comb"], comb[:])

    sAB.close()

    ln2f = alloc(root, "ln2f", [128, 16 * NT], BF16)
    if PHASE_LIMIT >= 8:
      with ExitStack() as sC:
          cpa = sC.enter_context(nc.psum_tensor([128, 1024], F32))
          cpb = sC.enter_context(nc.psum_tensor([128, 1024], F32))
          ps_t = sC.enter_context(nc.psum_tensor([128, 1024], BF16))

          wo_bf = alloc(sC, "wo_bf", [128, 16 * D], BF16)
          xot = [alloc(sC, f"xot{i}", [128, D], F32) for i in range(2)]
          tC = alloc(sC, "tC", [128, D], F32)
          x1t = alloc(sC, "x1t", [128, D], F32)
          l2t = alloc(sC, "l2t", [128, D], BF16)
          st4 = alloc(sC, "st4", [128, 16], F32)
          sq4 = alloc(sC, "sq4", [128, 16], F32)

          ctx.wait(SY, "dve", "combdone")
          for i in range(16):
              if i >= 2:
                  ctx.wait(SY, "dve", ("cwo", i - 2))
              ctx.dma(SY, "qs", xot[i % 2][:], ap["woT"][128 * i:128 * (i + 1), :])
              ctx.mark("qs", ("cwl", i))
              ctx.wait(V, "qs", ("cwl", i))
              ctx.dve("tensor_copy", wo_bf[:, D * i:D * (i + 1)], xot[i % 2][:])
              ctx.mark("dve", ("cwo", i))
          ctx.mark("dve", "cwodone")

          psl = [cpa[:, 0:512], cpa[:, 512:1024], cpb[:, 0:512], cpb[:, 512:1024]]
          ln2f_3d = ln2f[:].rearrange("p (fc t) -> p fc t", t=NT)
          pst_3d = ps_t[:].rearrange("p (k e) -> p k e", e=128)
          ctx.mark("dve", "l2ev_prev")

          for j in range(8):
              ctx.wait(SY, "dve", "cwodone")
              if j >= 2:
                  ctx.wait(SY, "dve", ("cx1", j - 2))
              ctx.dma(SY, "qs", xot[j % 2][:],
                      ap["x_own"][128 * j:128 * (j + 1), :])
              ctx.mark("qs", ("cx", j))
              ctx.wait(T, "dve", "cwodone")
              if j >= 1:
                  ctx.wait(T, "act", ("cwoev", j - 1))
              last = None
              for ic in range(16):
                  for db in range(4):
                      last = T.matmul(
                          psl[db],
                          comb[:, NT * ic + 128 * j:NT * ic + 128 * (j + 1)],
                          wo_bf[:, D * ic + 512 * db:D * ic + 512 * (db + 1)],
                          start=(ic == 0), stop=(ic == 15))
              ctx.inc("pe", last)
              ctx.mark("pe", ("cwo", j))
              ctx.wait(V, "pe", ("cwo", j))
              ctx.wait(A, "pe", ("cwo", j))
              if CDBG == 1:
                  ctx.dve("memset", st4[:, 0:4], 0.0)
                  ctx.dve("memset", sq4[:, 0:4], float(D))
              elif CDBG == 2:
                  for db in range(4):
                      ctx.dve("reduce_sum", st4[:, db:db + 1], psl[db], AX.X)
                  ctx.dve("memset", sq4[:, 0:4], float(D))
              elif CDBG == 3:
                  ctx.dve("memset", st4[:, 0:4], 0.0)
                  for db in range(4):
                      ctx.act("activation", sb["junk"][:, 0:512], psl[db],
                              AF.Square, accum_out=sq4[:, db:db + 1])
              else:
                  for db in range(4):
                      ctx.dve("reduce_sum", st4[:, db:db + 1], psl[db], AX.X)
                  ctx.mark("dve", ("cred", j))
                  ctx.wait(A, "dve", ("cred", j))
                  for db in range(4):
                      ctx.act("activation", sb["junk"][:, 0:512], psl[db],
                              AF.Square, accum_out=sq4[:, db:db + 1])
              ctx.mark("act", ("csq", j))
              ctx.dve("reduce_sum", st4[:, 8:9], st4[:, 0:4], AX.X)
              ctx.wait(V, "act", ("csq", j))
              ctx.dve("reduce_sum", sq4[:, 8:9], sq4[:, 0:4], AX.X)
              ctx.dve("tensor_scalar", st4[:, 9:10], st4[:, 8:9], 1.0 / D, None,
                      ALU.mult)
              ctx.dve("tensor_scalar", sq4[:, 9:10], sq4[:, 8:9], 1.0 / D, None,
                      ALU.mult)
              ctx.dve("tensor_tensor", st4[:, 10:11], st4[:, 9:10], st4[:, 9:10],
                      ALU.mult)
              ctx.dve("tensor_tensor", sq4[:, 10:11], sq4[:, 9:10],
                      st4[:, 10:11], ALU.subtract)
              ctx.dve("tensor_scalar", sq4[:, 11:12], sq4[:, 10:11], LN_EPS,
                      None, ALU.add)
              ctx.dve("reciprocal", sq4[:, 12:13], sq4[:, 11:12])
              ctx.mark("dve", ("crv", j))
              ctx.wait(A, "dve", ("crv", j))
              ctx.act("activation", st4[:, 11:12], sq4[:, 12:13], AF.Sqrt)
              ctx.mark("act", ("crstd", j))
              ctx.wait(V, "act", ("crstd", j))
              ctx.dve("scalar_tensor_tensor", st4[:, 12:13], st4[:, 9:10], -1.0,
                      st4[:, 11:12], ALU.mult, ALU.mult)
              ctx.mark("dve", ("cbias", j))
              ctx.wait(A, "dve", ("cbias", j))
              for db in range(4):
                  ctx.act("activation", tC[:, 512 * db:512 * (db + 1)], psl[db],
                          AF.Identity, scale=st4[:, 11:12], bias=st4[:, 12:13])
              ctx.mark("act", ("cwoev", j))
              ctx.wait(V, "act", ("cwoev", j))
              ctx.wait(V, "qs", ("cx", j))
              if j >= 1:
                  ctx.wait(V, "qs", ("cx1s", j - 1))
              ctx.dve("tensor_tensor", x1t[:], tC[:], xot[j % 2][:], ALU.add)
              ctx.mark("dve", ("cx1", j))
              ctx.wait(SY, "dve", ("cx1", j))
              ctx.dma(SY, "qs", ap["x1d"][128 * j:128 * (j + 1), :], x1t[:])
              if "tap_x1" in taps:
                  ctx.dma(SY, "st", taps["tap_x1"][128 * j:128 * (j + 1), :],
                          x1t[:])
              ctx.mark("qs", ("cx1s", j))
              ctx.dve("reduce_sum", st4[:, 13:14], x1t[:], AX.X)
              ctx.wait(A, "dve", ("cx1", j))
              ctx.act("activation", sb["junk"][:], x1t[:], AF.Square,
                      accum_out=sq4[:, 13:14])
              ctx.mark("act", ("l2sq", j))
              ctx.dve("tensor_scalar", st4[:, 9:10], st4[:, 13:14], 1.0 / D,
                      None, ALU.mult)
              ctx.wait(V, "act", ("l2sq", j))
              ctx.dve("tensor_scalar", sq4[:, 9:10], sq4[:, 13:14], 1.0 / D,
                      None, ALU.mult)
              ctx.dve("tensor_tensor", st4[:, 10:11], st4[:, 9:10], st4[:, 9:10],
                      ALU.mult)
              ctx.dve("tensor_tensor", sq4[:, 10:11], sq4[:, 9:10],
                      st4[:, 10:11], ALU.subtract)
              ctx.dve("tensor_scalar", sq4[:, 11:12], sq4[:, 10:11], LN_EPS,
                      None, ALU.add)
              ctx.dve("reciprocal", sq4[:, 12:13], sq4[:, 11:12])
              ctx.mark("dve", ("l2rv", j))
              ctx.wait(A, "dve", ("l2rv", j))
              ctx.act("activation", st4[:, 11:12], sq4[:, 12:13], AF.Sqrt)
              ctx.mark("act", ("l2rstd", j))
              ctx.wait(V, "act", ("l2rstd", j))
              ctx.dve("scalar_tensor_tensor", st4[:, 12:13], st4[:, 9:10], -1.0,
                      st4[:, 11:12], ALU.mult, ALU.mult)
              ctx.mark("dve", ("l2bias", j))
              ctx.wait(A, "dve", ("l2bias", j))
              ctx.act("activation", l2t[:], x1t[:], AF.Identity,
                      scale=st4[:, 11:12], bias=st4[:, 12:13])
              ctx.mark("act", ("l2t", j))
              for rnd in range(2):
                  ctx.wait(T, "act", ("l2t", j))
                  ctx.wait(T, "dve", "l2ev_prev")
                  last = None
                  for k in range(8):
                      fc = 8 * rnd + k
                      last = T.transpose(ps_t[:, 128 * k:128 * (k + 1)],
                                         l2t[:, 128 * fc:128 * (fc + 1)],
                                         sb["identb"][:])
                  ctx.inc("pe", last)
                  ctx.mark("pe", ("l2T", j, rnd))
                  ctx.wait(V, "pe", ("l2T", j, rnd))
                  ctx.dve("tensor_copy",
                          ln2f_3d[:, 8 * rnd:8 * rnd + 8, 128 * j:128 * (j + 1)],
                          pst_3d[:, :, :])
                  ctx.mark("dve", "l2ev_prev")
          ctx.mark("dve", "ln2done")
          if "tap_ln2" in taps:
              ctx.wait(SY, "dve", "ln2done")
              ctx.dma(SY, "st", taps["tap_ln2"], ln2f[:])

    sBC.close()

    if PHASE_LIMIT >= 9:
      with ExitStack() as sD:
          dga = sD.enter_context(nc.psum_tensor([128, 1024], F32))
          dgb = sD.enter_context(nc.psum_tensor([128, 1024], F32))
          ps_t = [sD.enter_context(nc.psum_tensor(f"ps_t{i}", [128, 1024],
                                                  BF16))
                  for i in range(2)]
          dse = sD.enter_context(nc.psum_tensor([128, 512], F32))

          act_q = alloc(sD, "act_q", [128, 16 * NT], BF16)
          out_acc = alloc(sD, "out_acc", [128, 16 * NT], BF16)
          dtmp = alloc(sD, "dtmp", [128, 1024], BF16)
          wgst = [alloc(sD, f"wgst{i}", [128, D], F32) for i in range(2)]
          wqh = [alloc(sD, f"wqh{i}", [128, D], BF16) for i in range(2)]
          qwb = [alloc(sD, f"qwb{i}", [128, D], BF16) for i in range(2)]
          qtb = alloc(sD, "qtb", [128, D // 2], BF16)
          qpb = alloc(sD, "qpb", [128, D // 2], BF16)
          qsc = alloc(sD, "qsc", [128, 32], F32)
          qscb = alloc(sD, "qscb", [128, 32], BF16)
          qscn = alloc(sD, "qscn", [128, 32], BF16)
          qrec = alloc(sD, "qrec", [128, 32], F32)
          qrf = alloc(sD, "qrf", [128, 32], F32)
          wfm = [alloc(sD, f"wfm{i}", [128, 16 * 128], BF16) for i in range(2)]
          wpTb = alloc(sD, "wpTb", [128, 16 * 128], BF16)
          sgt = alloc(sD, "sgt", [128, 1024], F32)
          sqb = alloc(sD, "sqb", [128, 1024], BF16)
          invb = alloc(sD, "invb", [128, 16], F32)
          ftm = alloc(sD, "ftm", [128, D], BF16, side="right")
          ftf = alloc(sD, "ftf", [128, D], F32, side="right")

          ctx.mark("dve", "dq_prev")
          ctx.mark("act", "t16_prev")
          ctx.mark("pe", "t16pe_prev")

          def quant(key, ci):
              nG = 32
              qw = qwb[ci % 2]
              w3i = qw[:, :].rearrange("p (g k) -> p g k", k=64)
              ctx.wait(V, "act", ("qcast", ci))
              ctx.dve("tensor_reduce", qrf[:, 0:nG], w3i, AX.X, ALU.add,
                      apply_absolute_value=True)
              ctx.dve("tensor_scalar", qsc[:, 0:nG], qrf[:, 0:nG], 1.0 / 64,
                      1e-8, ALU.mult, ALU.max)
              ctx.dve("tensor_copy", qscb[:, 0:nG], qsc[:, 0:nG])
              ctx.dve("tensor_scalar", qscn[:, 0:nG], qscb[:, 0:nG], -1.0, None,
                      ALU.mult)
              ctx.dve("tensor_copy", qsc[:, 0:nG], qscb[:, 0:nG])
              ctx.dve("reciprocal", qrec[:, 0:nG], qsc[:, 0:nG])
              for hv in range(2):
                  gsl = slice(16 * hv, 16 * hv + 16)
                  csl = slice(1024 * hv, 1024 * hv + 1024)
                  ctx.dve("tensor_tensor",
                          qtb[:, :].rearrange("p (g k) -> p g k", k=64),
                          w3i[:, gsl, :],
                          qrec[:, gsl].unsqueeze(2).broadcast_to([128, 16, 64]),
                          ALU.mult)
                  ctx.dve("tensor_scalar", qpb[:], qtb[:], 0.5, None, ALU.is_gt)
                  ctx.dve("scalar_tensor_tensor", qw[:, csl], qtb[:], -0.5,
                          qpb[:], ALU.is_lt, ALU.subtract)
              if ci >= 2:
                  ctx.wait(V, "pe", ("t16L", ci - 2))
              ctx.dve("tensor_tensor",
                      wqh[ci % 2][:, :].rearrange("p (g k) -> p g k", k=64),
                      qw[:, :].rearrange("p (g k) -> p g k", k=64),
                      qscn[:, 0:nG].unsqueeze(2).broadcast_to([128, nG, 64]),
                      ALU.mult)
              ctx.mark("dve", ("quant", key))
              ctx.mark("dve", "dq_prev")

          def transpose16(key, ci, dst, prev_reader_mark):
              for rnd in range(2):
                  pst = ps_t[rnd]
                  ctx.wait(T, "dve", ("quant", key))
                  if ci >= 1:
                      ctx.wait(T, "act", ("t16ev", ci - 1, rnd))
                  else:
                      ctx.wait(T, "act", "t16_prev")
                  last = None
                  for k in range(8):
                      fc = 8 * rnd + k
                      last = T.transpose(pst[:, 128 * k:128 * (k + 1)],
                                         wqh[ci % 2][:, 128 * fc:128 * (fc + 1)],
                                         sb["identb"][:])
                  ctx.inc("pe", last)
                  ctx.mark("pe", ("t16", key, rnd))
                  ctx.mark("pe", "t16pe_prev")
                  if rnd == 1:
                      ctx.mark("pe", ("t16L", ci))
                  ctx.wait(A, "pe", ("t16", key, rnd))
                  if prev_reader_mark is not None:
                      ctx.wait(A, "pe", val=prev_reader_mark)
                  ctx.act("activation", dst[:, 1024 * rnd:1024 * (rnd + 1)],
                          pst[:], AF.Identity)
                  ctx.mark("act", ("t16ev", ci, rnd))
                  ctx.mark("act", "t16_prev")
              ctx.mark("act", ("t16d", key))

          psl_d = {0: dga, 1: dgb}
          chunk_srcs = []
          for q in range(NQ):
              for hp in range(16):
                  for wi, row0 in ((0, HQ * q + 128 * hp),
                                   (1, FH + HQ * q + 128 * hp)):
                      chunk_srcs.append((("gu", q, hp, wi),
                                         ap["wgu"][row0:row0 + 128, :]))
              for dc in range(16):
                  chunk_srcs.append(
                      (("pj", q, dc),
                       ap["wp"][128 * dc:128 * (dc + 1),
                                HQ * q:HQ * (q + 1)]))
          NCH = len(chunk_srcs)
          pumped = [0]

          def pump(n):
              while pumped[0] < min(n, NCH):
                  ci = pumped[0]
                  key, wsrc = chunk_srcs[ci]
                  if ci >= 2:
                      ctx.wait(SY, "act", ("qcast", ci - 2))
                  else:
                      ctx.wait(SY, "dve", "dq_prev")
                      ctx.wait(SY, "pe", "t16pe_prev")
                      ctx.wait(SY, "act", "t16_prev")
                  ctx.dma(SY, "qs", wgst[ci % 2][:], wsrc)
                  ctx.mark("qs", ("dwl", key))
                  ctx.wait(A, "qs", ("dwl", key))
                  if ci >= 2:
                      ctx.wait(A, "dve", ("quant", chunk_srcs[ci - 2][0]))
                  ctx.act("activation", qwb[ci % 2][:], wgst[ci % 2][:],
                          AF.Identity)
                  ctx.mark("act", ("qcast", ci))
                  pumped[0] += 1

          def d1_mm(key, ci, wi, q, hp):
              prev = None
              if (q, hp) != (0, 0):
                  prevkey = ("dgu", q, hp - 1, wi) if hp >= 1 else \
                      ("dgu", q - 1, 15, wi)
                  prev = ctx.mval("pe", prevkey)
              transpose16(key, ci, wfm[wi], prev)
              ps = psl_d[wi]
              ctx.wait(T, "act", ("t16d", key))
              ctx.wait(T, "dve", "ln2done")
              if (q, hp) != (0, 0):
                  ctx.wait(T, "dve", ("dact", q, hp - 1) if hp >= 1
                           else ("dact", q - 1, 15))
              if hp == 0 and q >= 1:
                  ctx.wait(T, "dve", ("dpe", q - 1, 15))
              last = None
              for fc in range(16):
                  last = T.matmul(ps[:, 0:512],
                                  wfm[wi][:, 128 * fc:128 * (fc + 1)],
                                  ln2f[:, NT * fc:NT * fc + 512],
                                  start=(fc == 0), stop=(fc == 15))
                  last = T.matmul(ps[:, 512:1024],
                                  wfm[wi][:, 128 * fc:128 * (fc + 1)],
                                  ln2f[:, NT * fc + 512:NT * fc + 1024],
                                  start=(fc == 0), stop=(fc == 15))
              ctx.inc("pe", last)
              ctx.mark("pe", ("dgu", q, hp, wi))

          def hp_epi(q, hp):
              ctx.wait(A, "pe", ("dgu", q, hp, 0))
              ctx.act("activation", sgt[:], dga[:], AF.Silu)
              ctx.mark("act", ("dsilu", q, hp))
              ctx.wait(V, "act", ("dsilu", q, hp))
              ctx.wait(V, "pe", ("dgu", q, hp, 1))
              ctx.dve("tensor_tensor", act_q[:, NT * hp:NT * (hp + 1)],
                      dgb[:], sgt[:], ALU.mult)
              ctx.mark("dve", ("dact", q, hp))
              ctx.wait(A, "dve", ("dact", q, hp))
              ctx.act("activation", sqb[:], act_q[:, NT * hp:NT * (hp + 1)],
                      AF.Square)
              ctx.mark("act", ("dsq", q, hp))
              ctx.wait(T, "act", ("dsq", q, hp))
              last = None
              for jj in range(8):
                  last = T.matmul(dse[:, jj:jj + 1],
                                  sqb[:, 128 * jj:128 * (jj + 1)],
                                  sb["ones_col"][:],
                                  start=(q == 0 and hp == 0 and jj == 0),
                                  stop=(q == NQ - 1 and hp == 15))
              ctx.inc("pe", last)
              ctx.mark("pe", ("dssq", q, hp))

          def dpe_acc(q, dc):
              ps = psl_d[dc % 2]
              ctx.wait(V, "pe", ("dpj", q, dc))
              if q == 0:
                  ctx.dve("tensor_copy", out_acc[:, NT * dc:NT * (dc + 1)],
                          ps[:])
              else:
                  ctx.dve("tensor_tensor", dtmp[:], ps[:],
                          out_acc[:, NT * dc:NT * (dc + 1)], ALU.add)
                  ctx.dve("tensor_copy", out_acc[:, NT * dc:NT * (dc + 1)],
                          dtmp[:])
              ctx.mark("dve", ("dpe", q, dc))

          for q in range(NQ):
              for hp in range(16):
                  for wi, row0 in ((0, HQ * q + 128 * hp),
                                   (1, FH + HQ * q + 128 * hp)):
                      key = ("gu", q, hp, wi)
                      ci = q * 48 + hp * 2 + wi
                      pump(ci + 2)
                      quant(key, ci)
                      pump(ci + 3)
                      if wi == 0 and hp >= 1:
                          hp_epi(q, hp - 1)
                      d1_mm(key, ci, wi, q, hp)

              for dc in range(16):
                  key = ("pj", q, dc)
                  ci = q * 48 + 32 + dc
                  pump(ci + 2)
                  quant(key, ci)
                  pump(ci + 3)
                  if dc == 0:
                      hp_epi(q, 15)
                  else:
                      dpe_acc(q, dc - 1)
                  prev = ctx.mval("pe", ("dpj", q, dc - 1)) if dc >= 1 else \
                      (ctx.mval("pe", ("dpj", q - 1, 15)) if q >= 1 else None)
                  transpose16(key, ci, wpTb, prev)
                  ps = psl_d[dc % 2]
                  ctx.wait(T, "act", ("t16d", key))
                  if dc < 2:
                      ctx.wait(T, "dve", ("dact", q, 15))
                  if dc >= 2:
                      ctx.wait(T, "dve", ("dpe", q, dc - 2))
                  elif q >= 1:
                      ctx.wait(T, "dve", ("dpe", q - 1, 14 + dc))
                  last = None
                  for hc in range(16):
                      last = T.matmul(ps[:, 0:512],
                                      wpTb[:, 128 * hc:128 * (hc + 1)],
                                      act_q[:, NT * hc:NT * hc + 512],
                                      start=(hc == 0), stop=(hc == 15))
                      last = T.matmul(ps[:, 512:1024],
                                      wpTb[:, 128 * hc:128 * (hc + 1)],
                                      act_q[:, NT * hc + 512:NT * hc + 1024],
                                      start=(hc == 0), stop=(hc == 15))
                  ctx.inc("pe", last)
                  ctx.mark("pe", ("dpj", q, dc))
              dpe_acc(q, 15)

          ctx.wait(V, "pe", ("dssq", NQ - 1, 15))
          ctx.dve("tensor_scalar", invb[:, 0:8], dse[:, 0:8], 1.0 / FH, RMS_EPS,
                  ALU.mult, ALU.add)
          ctx.dve("reciprocal", invb[:, 8:16], invb[:, 0:8])
          ctx.mark("dve", "drv")
          ctx.wait(A, "dve", "drv")
          ctx.act("activation", invb[:, 0:8], invb[:, 8:16], AF.Sqrt)
          ctx.mark("act", "dinv")

          if "tap_ffn" in taps:
              ctx.wait(SY, "dve", ("dpe", NQ - 1, 15))
              ctx.dma(SY, "st", taps["tap_ffn"], out_acc[:])

          ftm_3d = ftm[:].rearrange("p (k e) -> p k e", e=128)
          ctx.mark("dve", "et16_prev")
          fln = wgst[0]
          x1t2 = wgst[1]
          last_ci = (NQ - 1) * 48 + 32 + 15
          for j in range(8):
              if j >= 1:
                  ctx.wait(SY, "dve", ("eout", j - 1))
              else:
                  ctx.wait(SY, "act", ("qcast", last_ci))
              ctx.dma(SY, "qs", x1t2[:], ap["x1d"][128 * j:128 * (j + 1), :])
              ctx.mark("qs", ("ex1", j))
              for rnd in range(2):
                  ctx.wait(T, "dve", ("dpe", NQ - 1, 15))
                  ctx.wait(T, "act", "t16_prev")
                  ctx.wait(T, "dve", "et16_prev")
                  last = None
                  for k in range(8):
                      dc = 8 * rnd + k
                      last = T.transpose(
                          ps_t[rnd][:, 128 * k:128 * (k + 1)],
                          out_acc[:, NT * dc + 128 * j:NT * dc + 128 * (j + 1)],
                          sb["identb"][:])
                  ctx.inc("pe", last)
                  ctx.mark("pe", ("eT", j, rnd))
                  ctx.wait(V, "pe", ("eT", j, rnd))
                  if j >= 1 and rnd == 0:
                      ctx.wait(V, "act", ("eln", j - 1))
                  ctx.dve("tensor_copy", ftm_3d[:, 8 * rnd:8 * rnd + 8, :],
                          ps_t[rnd][:].rearrange("p (k e) -> p k e", e=128))
                  ctx.mark("dve", "et16_prev")
              ctx.wait(V, "act", "dinv")
              if j >= 1:
                  ctx.wait(V, "st", ("y", j - 1))
              ctx.dve("tensor_scalar", ftf[:], ftm[:], invb[:, j:j + 1], None,
                      ALU.mult)
              ctx.mark("dve", ("eff", j))
              ctx.dve("reduce_sum", sb["sc_a"][:, 0:1], ftf[:], AX.X)
              ctx.wait(A, "dve", ("eff", j))
              ctx.act("activation", sb["junk"][:], ftf[:], AF.Square,
                      accum_out=sb["sc_a"][:, 1:2])
              ctx.mark("act", ("esq", j))
              ctx.dve("tensor_scalar", sb["sc_a"][:, 2:3], sb["sc_a"][:, 0:1],
                      1.0 / D, None, ALU.mult)
              ctx.wait(V, "act", ("esq", j))
              ctx.dve("tensor_scalar", sb["sc_a"][:, 3:4], sb["sc_a"][:, 1:2],
                      1.0 / D, None, ALU.mult)
              ctx.dve("tensor_tensor", sb["sc_a"][:, 4:5], sb["sc_a"][:, 2:3],
                      sb["sc_a"][:, 2:3], ALU.mult)
              ctx.dve("tensor_tensor", sb["sc_a"][:, 5:6], sb["sc_a"][:, 3:4],
                      sb["sc_a"][:, 4:5], ALU.subtract)
              ctx.dve("tensor_scalar", sb["sc_a"][:, 6:7], sb["sc_a"][:, 5:6],
                      LN_EPS, None, ALU.add)
              ctx.dve("reciprocal", sb["sc_a"][:, 7:8], sb["sc_a"][:, 6:7])
              ctx.mark("dve", ("erv", j))
              ctx.wait(A, "dve", ("erv", j))
              ctx.act("activation", sb["sc_a"][:, 8:9], sb["sc_a"][:, 7:8],
                      AF.Sqrt)
              ctx.mark("act", ("erstd", j))
              ctx.wait(V, "act", ("erstd", j))
              ctx.dve("scalar_tensor_tensor", sb["sc_a"][:, 9:10],
                      sb["sc_a"][:, 2:3], -1.0, sb["sc_a"][:, 8:9],
                      ALU.mult, ALU.mult)
              ctx.mark("dve", ("ebias", j))
              ctx.wait(A, "dve", ("ebias", j))
              ctx.act("activation", fln[:], ftf[:], AF.Identity,
                      scale=sb["sc_a"][:, 8:9], bias=sb["sc_a"][:, 9:10])
              ctx.mark("act", ("eln", j))
              ctx.wait(V, "act", ("eln", j))
              ctx.wait(V, "qs", ("ex1", j))
              ctx.dve("tensor_tensor", ftf[:], fln[:], x1t2[:], ALU.add)
              ctx.mark("dve", ("eout", j))
              ctx.wait(SY, "dve", ("eout", j))
              ctx.dma(SY, "st", ap["y"][128 * j:128 * (j + 1), :], ftf[:])
              ctx.mark("st", ("y", j))

    if ctx.cnt["st"] > 0:
        ctx.wait(SY, "st", val=ctx.cnt["st"])
        gp.wait_ge(ctx.sems["st"], ctx.cnt["st"])
    else:
        gp.nop()


def kernel(L0_attn_hidden_states, L0_attn_Wq, L0_attn_Wk, L0_attn_Wv,
           L0_attn_Wo, L0_attn_gate, L0_ffn_w_gate_up, L0_ffn_w_proj):
    X = np.ascontiguousarray(np.asarray(L0_attn_hidden_states,
                                        np.float32).reshape(S, D))
    wqT = np.ascontiguousarray(np.asarray(L0_attn_Wq, np.float32).T)
    wkT = np.ascontiguousarray(np.asarray(L0_attn_Wk, np.float32).T)
    wvT = np.ascontiguousarray(np.asarray(L0_attn_Wv, np.float32).T)
    woT = np.ascontiguousarray(np.asarray(L0_attn_Wo, np.float32).T)
    wgu = np.ascontiguousarray(np.asarray(L0_ffn_w_gate_up, np.float32))
    wp = np.ascontiguousarray(np.asarray(L0_ffn_w_proj, np.float32))
    gate = np.ascontiguousarray(
        np.asarray(L0_attn_gate, np.float32).reshape(1, NH))

    inv_freq = (1.0 / 10000.0 ** (np.arange(0, D, 2, dtype=np.float64) / D)
                ).astype(np.float32)
    invf = np.ascontiguousarray(inv_freq.reshape(8, 128).T)
    invf2 = np.ascontiguousarray(
        (invf.astype(np.float64) / (2 * np.pi)).astype(np.float32))
    ident = np.eye(128, dtype=np.float32)
    epoch = np.array([[np.random.randint(1, 2 ** 31 - 1)]], np.int32)
    ar512 = np.arange(512, dtype=np.float32)

    nc, taps = build_program()
    in_maps = []
    for c in range(8):
        role = c & 1
        in_maps.append({
            "x_own": np.ascontiguousarray(X[NT * c:NT * (c + 1)]),
            "x_kv": np.ascontiguousarray(X[KT * role:KT * (role + 1)]),
            "wqT": wqT, "wkT": wkT, "wvT": wvT, "woT": woT,
            "wgu": wgu, "wp": wp, "gate": gate,
            "iotak": (KT * role + ar512)[None, :],
            "iotaq": (NT * c + ar512)[None, :],
            "invf": invf, "invf2": invf2, "identm": ident,
            "role_t": np.array([[role]], np.int32),
            "epoch_t": epoch,
        })
    res = run_bass_kernel_spmd(nc, in_maps, core_ids=list(range(8)))
    out = np.concatenate([res.results[c]["y"] for c in range(8)], axis=0)
    return out.reshape(1, S, D)
